# version 04
# baseline (speedup 1.0000x reference)
import numpy as np
from contextlib import ExitStack

import concourse.bass as bass
import concourse.mybir as mybir

F32 = mybir.dt.float32
BF16 = mybir.dt.bfloat16
AF = mybir.ActivationFunctionType
ALU = mybir.AluOpType
AX = mybir.AxisListType.X

D = 1024
DC = 8
NPAIR = 8
E = 8
I_FF = 512
SI = 2048
LN_EPS = 1e-5


class Sem:
    def __init__(self, stack, nc, name):
        self.h = stack.enter_context(nc.semaphore(name))
        self.n = 0

    def inc(self, inst, k=1):
        inst.then_inc(self.h, k)
        self.n += k
        return self.n


def build(T=2048, dumps=(), detect_races=False):
    TC = T // 128
    NQC = T // 512
    KC = T // 128
    assert T % 512 == 0

    nc = bass.Bass("TRN2", target_bir_lowering=False,
                   detect_race_conditions=detect_races)

    x_h = nc.dram_tensor("x", [T, D], F32, kind="ExternalInput").ap()
    wqkv_h = nc.dram_tensor("w_qkv", [3 * D, D], F32, kind="ExternalInput").ap()
    wo_h = nc.dram_tensor("w_o", [D, D], F32, kind="ExternalInput").ap()
    gatew_h = nc.dram_tensor("gate_w", [E, D], F32, kind="ExternalInput").ap()
    gateb_h = nc.dram_tensor("gate_b", [E], F32, kind="ExternalInput").ap()
    upw_h = nc.dram_tensor("up_w", [E, I_FF, D], F32, kind="ExternalInput").ap()
    downw_h = nc.dram_tensor("down_w", [E, D, I_FF], F32, kind="ExternalInput").ap()
    shup_h = nc.dram_tensor("shup_w", [SI, D], F32, kind="ExternalInput").ap()
    shdown_h = nc.dram_tensor("shdown_w", [D, SI], F32, kind="ExternalInput").ap()
    c_h = nc.dram_tensor("c", [D], F32, kind="ExternalInput").ap()
    wmod_h = nc.dram_tensor("wmod", [D, 6 * D], F32, kind="ExternalInput").ap()
    out_h = nc.dram_tensor("out", [T, D], F32, kind="ExternalOutput").ap()

    dmp = {}
    def mkdump(name, shape, dtype=F32):
        if name in dumps:
            dmp[name] = nc.dram_tensor("dmp_" + name, list(shape), dtype,
                                       kind="ExternalOutput").ap()
        return dmp.get(name)

    st = ExitStack()
    sbt = lambda name, shape, dt: st.enter_context(nc.sbuf_tensor(name, shape, dt))

    x_res = sbt("x_res", [128, TC * 1024], BF16)
    hT = sbt("hT", [128, DC * T], BF16)
    aun = sbt("aun", [128, NPAIR * T], BF16)
    ARN = 51200 if NPAIR * T >= 16384 else 69632
    arena = sbt("arena", [128, ARN], BF16)
    arf = arena.ap().bitcast(F32)

    modT = sbt("modT", [128, 48], F32)
    coef = sbt("coef", [128, 48], F32)
    lnst = sbt("lnst", [128, 4 * TC], F32)
    stat6 = sbt("stat6", [128, 16], F32)
    sil_c = sbt("sil_c", [128, 8], BF16)
    sil_f = sbt("sil_f", [128, 16], F32)
    id_f = sbt("id_f", [128, 128], F32)
    id_b = sbt("id_b", [128, 128], BF16)
    ones64 = sbt("ones64", [128, 1], BF16)
    gwt = sbt("gwt", [128, 64], BF16)
    gwf = sbt("gwf", [128, 64], F32)
    gbias = sbt("gbias", [128, 8], F32)
    gb1 = sbt("gb1", [1, 8], F32)
    rtr = sbt("rtr", [128, 384], F32)
    cwsq = sbt("cwsq", [128, 8 * TC], F32)
    negm = sbt("negm", [128, 4], F32)
    mtmp = sbt("mtmp", [128, 8], F32)
    mcol = sbt("mcol", [1, 256], F32)
    mrow = sbt("mrow", [1, 8], F32)
    mrowb = sbt("mrowb", [1, 1], BF16)
    gb1b = sbt("gb1b", [1, 8], BF16)
    onesrow = sbt("onesrow", [1, 128], BF16)
    ffr = sbt("ffr", [128, 6 * 512], BF16)

    O_WQKVT = 0
    O_WOT = 0
    O_QK65 = 24576
    O_V65 = O_QK65 + 4 * T
    O_PT = O_V65 + 2 * TC * 130
    O_SUM = O_PT + 2048
    O_RCP = O_SUM
    O_SSTG = O_SUM + T
    O_H1S = O_SSTG + 1024
    O_IND = O_H1S + 1024
    O_MQR = O_IND + 1024
    O_HSTG = O_MQR + 256
    O_STGB = O_HSTG + 4096
    assert O_STGB + 2048 <= ARN, O_STGB

    P_WMF = 24576
    P_WMB = P_WMF + 12288
    P_WQS = P_WMB + 6144
    assert P_WQS + 4096 <= ARN

    F_SHUP = 0
    F_SHDN = 16384
    F_UPA = 32768
    F_UPB = 0
    F_DNB = 16384
    F_OUTS = 32768
    if NPAIR * T < 16384:
        F_DNA_SMALL = F_UPA + 16384

    F_STG = 49152 if NPAIR * T >= 16384 else 65536
    stgb = arf[:, O_STGB // 2: O_STGB // 2 + 1024]
    stga = stgb
    wmf = arf[:, P_WMF // 2: P_WMF // 2 + 6144]
    wmb = arena[:, P_WMB:P_WMB + 6144]
    wqs = [arf[:, P_WQS // 2 + i * 1024: P_WQS // 2 + (i + 1) * 1024] for i in (0, 1)]
    hpre = [stgb[:, 0:1024]]
    fstg = arf[:, F_STG // 2: F_STG // 2 + 1024]

    psum = st.enter_context(nc.psum_tensor("psum", [128, 4096], F32))
    PB = lambda b: psum[:, b * 512:(b + 1) * 512]
    SC = [psum[:, 0:1024], psum[:, 1024:2048]]
    AVP = [psum[:, 2048:2560], psum[:, 2560:3072]]

    t_s = Sem(st, nc, "t_s")
    v_s = Sem(st, nc, "v_s")
    s_s = Sem(st, nc, "s_s")
    g_s = Sem(st, nc, "g_s")
    d_sy = Sem(st, nc, "d_sy")
    d_sc = Sem(st, nc, "d_sc")
    d_gp = Sem(st, nc, "d_gp")
    d_out = Sem(st, nc, "d_out")

    blk = st.enter_context(nc.Block())

    @blk.scalar
    def _(sc):
        d_sc.inc(sc.dma_start(out=sil_f[:, 0:8],
                              in_=c_h.rearrange("(j p) -> p j", p=128),
                              allow_slow_non_contiguous=True), 16)
        d_sc.inc(sc.dma_start(out=gb1[:], in_=gateb_h[None, :]), 16)
    ld_c = d_sc.n

    @blk.gpsimd
    def _(g):
        d_gp.inc(g.dma_start(out=x_res[:],
                             in_=x_h.rearrange("(t p) d -> p t d", p=128)), 16)
    ld_x = d_gp.n

    @blk.gpsimd
    def _(g):
        g_s.inc(g.memset(id_f[:], 0.0))
        g_s.inc(g.memset(id_b[:], 0.0))
        g_s.inc(g.memset(ones64[:], 1.0))
        g.wait_ge(g_s.h, g_s.n)
        g_s.inc(g.affine_select(out=id_f[:], in_=id_f[:], compare_op=ALU.not_equal,
                                fill=1.0, base=0, pattern=[[-1, 128]],
                                channel_multiplier=1))
        g_s.inc(g.affine_select(out=id_b[:], in_=id_b[:], compare_op=ALU.not_equal,
                                fill=1.0, base=0, pattern=[[-1, 128]],
                                channel_multiplier=1))
        g_s.inc(g.memset(onesrow[:], 1.0))
    g_const = g_s.n

    @blk.vector
    def _(v):
        v.wait_ge(d_sc.h, ld_c)
        v_s.inc(v.tensor_copy(gb1b[:], gb1[:]))
    gb_cast = v_s.n

    @blk.tensor
    def _(t):
        t.wait_ge(v_s.h, gb_cast)
        t.wait_ge(g_s.h, g_const)
        t_s.inc(t.matmul(PB(7)[:, 500:508], onesrow[:], gb1b[:],
                         start=True, stop=True))
    gb_mm = t_s.n

    @blk.vector
    def _(v):
        v.wait_ge(t_s.h, gb_mm)
        v_s.inc(v.tensor_copy(gbias[:], PB(7)[:, 500:508]))
    gb_done = v_s.n

    @blk.scalar
    def _(sc):
        sc.wait_ge(d_sc.h, ld_c)
        s_s.inc(sc.activation(sil_f[:, 8:16], sil_f[:, 0:8], AF.Silu))
    sil_done = s_s.n

    @blk.vector
    def _(v):
        v.wait_ge(s_s.h, sil_done)
        v_s.inc(v.tensor_copy(sil_c[:], sil_f[:, 8:16]))
    silc_done = v_s.n

    wm_dma, wm_cast, wm_mm = [], [], []
    for dc in range(DC):
        @blk.sync
        def _(sy, dc=dc):
            if dc >= 1:
                sy.wait_ge(v_s.h, wm_cast[dc - 1])
            d_sy.inc(sy.dma_start(out=wmf[:], in_=wmod_h[dc * 128:(dc + 1) * 128, :]), 16)
            wm_dma.append(d_sy.n)

        @blk.vector
        def _(v, dc=dc):
            v.wait_ge(d_sy.h, wm_dma[dc])
            if dc >= 1:
                v.wait_ge(t_s.h, wm_mm[dc - 1])
            v_s.inc(v.tensor_copy(wmb[:], wmf[:]))
            wm_cast.append(v_s.n)

        @blk.tensor
        def _(t, dc=dc):
            t.wait_ge(v_s.h, wm_cast[dc])
            if dc == 0:
                t.wait_ge(v_s.h, silc_done)
            last = None
            for ncx in range(48):
                last = t.matmul(psum[:, ncx * 8 + dc:ncx * 8 + dc + 1],
                                wmb[:, ncx * 128:(ncx + 1) * 128],
                                sil_c[:, dc:dc + 1],
                                start=True, stop=True)
            t_s.inc(last)
            wm_mm.append(t_s.n)
    mod_mm = wm_mm[-1]

    @blk.vector
    def _(v):
        v.wait_ge(t_s.h, mod_mm)
        v_s.inc(v.tensor_reduce(modT[:],
                                psum.ap()[:, 0:384].rearrange("p (n d) -> p n d", d=8),
                                axis=AX, op=ALU.add))
    modT_done = v_s.n

    @blk.vector
    def _(v):
        v.wait_ge(v_s.h, modT_done)
        v_s.inc(v.tensor_copy(coef[:, 0:8], modT[:, 0:8]))
        v_s.inc(v.tensor_scalar(coef[:, 8:16], modT[:, 8:16], 1.0, None, ALU.add))
        v_s.inc(v.tensor_copy(coef[:, 24:32], modT[:, 24:32]))
        v_s.inc(v.tensor_scalar(coef[:, 32:40], modT[:, 32:40], 1.0, None, ALU.add))
    pre_coef = v_s.n

    @blk.scalar
    def _(sc):
        sc.wait_ge(v_s.h, modT_done)
        s_s.inc(sc.activation(coef[:, 16:24], modT[:, 16:24], AF.Abs))
        s_s.inc(sc.activation(coef[:, 40:48], modT[:, 40:48], AF.Abs))
    gabs_done = s_s.n

    @blk.vector
    def _(v):
        v.wait_ge(s_s.h, gabs_done)
        for src in (16, 40):
            v_s.inc(v.tensor_scalar(stat6[:, 0:8], modT[:, src:src + 8], 0.0,
                                    None, ALU.is_ge))
            v.wait_ge(v_s.h, v_s.n)
            v_s.inc(v.tensor_scalar(stat6[:, 8:16], stat6[:, 0:8], 2.0, -1.0,
                                    ALU.mult, ALU.add))
            v_s.inc(v.tensor_scalar(coef[:, src:src + 8], coef[:, src:src + 8],
                                    0.25, None, ALU.max))
            v.wait_ge(v_s.h, v_s.n)
            v_s.inc(v.tensor_mul(coef[:, src:src + 8], coef[:, src:src + 8],
                                 stat6[:, 8:16]))
    coef_done = v_s.n

    def ln_stats(lnbase, xwait=None):
        for tc in range(TC):
            @blk.vector
            def _(v, tc=tc):
                if tc == 0 and xwait is not None:
                    v.wait_ge(*xwait)
                xs = x_res[:, tc * 1024:(tc + 1) * 1024]
                v_s.inc(v.bn_stats(stat6[:, 0:6], xs[:, 0:512]))
                v_s.inc(v.bn_stats(stat6[:, 6:12], xs[:, 512:1024]))
                v.wait_ge(v_s.h, v_s.n)
                v_s.inc(v.bn_aggr(stat6[:, 12:14], stat6[:, 0:12]))
                v.wait_ge(v_s.h, v_s.n)
                v_s.inc(v.tensor_scalar(stat6[:, 14:15], stat6[:, 13:14], LN_EPS,
                                        None, ALU.add))
                v.wait_ge(v_s.h, v_s.n)
                v_s.inc(v.reciprocal(stat6[:, 15:16], stat6[:, 14:15]))

            @blk.scalar
            def _(sc, tc=tc):
                sc.wait_ge(v_s.h, v_s.n)
                s_s.inc(sc.activation(lnst[:, lnbase + 2 * tc:lnbase + 2 * tc + 1],
                                      stat6[:, 15:16], AF.Sqrt))

            @blk.vector
            def _(v, tc=tc):
                v.wait_ge(s_s.h, s_s.n)
                v_s.inc(v.tensor_mul(stat6[:, 14:15], stat6[:, 12:13],
                                     lnst[:, lnbase + 2 * tc:lnbase + 2 * tc + 1]))
                v.wait_ge(v_s.h, v_s.n)
                v_s.inc(v.tensor_scalar(
                    lnst[:, lnbase + 2 * tc + 1:lnbase + 2 * tc + 2],
                    stat6[:, 14:15], -1.0, None, ALU.mult))

    def ln_apply(lnbase, cA, cB, first_wait=()):
        tr_done, ev_done = [], []
        for tc in range(TC):
            slot = 0
            @blk.vector
            def _(v, tc=tc, slot=slot):
                if tc == 0:
                    v.wait_ge(v_s.h, coef_done)
                    for w in first_wait:
                        v.wait_ge(*w)
                if tc >= 1:
                    v.wait_ge(t_s.h, tr_done[tc - 1])
                rs = lnst[:, lnbase + 2 * tc:lnbase + 2 * tc + 1]
                nm = lnst[:, lnbase + 2 * tc + 1:lnbase + 2 * tc + 2]
                v_s.inc(v.tensor_scalar(hpre[slot],
                                        x_res[:, tc * 1024:(tc + 1) * 1024],
                                        rs, nm, ALU.mult, ALU.add))
            ap_done = v_s.n

            @blk.tensor
            def _(t, tc=tc, slot=slot, ap_done=ap_done):
                t.wait_ge(v_s.h, ap_done)
                if tc >= 2:
                    t.wait_ge(v_s.h, ev_done[tc - 2])
                last = None
                for dc in range(DC):
                    bank = 2 + 2 * (tc % 2) + (dc // 4)
                    last = t.transpose(
                        PB(bank)[:, (dc % 4) * 128:(dc % 4) * 128 + 128],
                        hpre[slot][:, dc * 128:(dc + 1) * 128], id_f[:])
                t_s.inc(last)
                tr_done.append(t_s.n)

            @blk.vector
            def _(v, tc=tc):
                v.wait_ge(t_s.h, tr_done[tc])
                last = None
                for dc in range(DC):
                    bank = 2 + 2 * (tc % 2) + (dc // 4)
                    last = v.tensor_scalar(
                        hT[:, dc * T + tc * 128: dc * T + (tc + 1) * 128],
                        PB(bank)[:, (dc % 4) * 128:(dc % 4) * 128 + 128],
                        coef[:, cA + dc:cA + dc + 1],
                        coef[:, cB + dc:cB + dc + 1],
                        ALU.mult, ALU.add)
                v_s.inc(last)
                ev_done.append(v_s.n)
        return ev_done[-1]

    ln_stats(0, xwait=(d_gp.h, ld_x))

    wq_dma, wq_tr, wq_ev = [], [], []
    for ch in range(24):
        slot = ch % 2
        @blk.sync
        def _(sy, ch=ch, slot=slot):
            if ch >= 2:
                sy.wait_ge(t_s.h, wq_tr[ch - 2])
            d_sy.inc(sy.dma_start(out=wqs[slot],
                                  in_=wqkv_h[ch * 128:(ch + 1) * 128, :]), 16)
            wq_dma.append(d_sy.n)

        @blk.tensor
        def _(t, ch=ch, slot=slot):
            t.wait_ge(d_sy.h, wq_dma[ch])
            if ch >= 1:
                t.wait_ge(v_s.h, wq_ev[ch - 1])
            last = None
            for dc in range(DC):
                bank = 6 + (dc // 4)
                last = t.transpose(PB(bank)[:, (dc % 4) * 128:(dc % 4) * 128 + 128],
                                   wqs[slot][:, dc * 128:(dc + 1) * 128], id_f[:])
            t_s.inc(last)
            wq_tr.append(t_s.n)

        @blk.vector
        def _(v, ch=ch):
            v.wait_ge(t_s.h, wq_tr[ch])
            last = None
            for dc in range(DC):
                bank = 6 + (dc // 4)
                last = v.tensor_copy(
                    arena[:, O_WQKVT + dc * 3072 + ch * 128:
                          O_WQKVT + dc * 3072 + (ch + 1) * 128],
                    PB(bank)[:, (dc % 4) * 128:(dc % 4) * 128 + 128])
            v_s.inc(last)
            wq_ev.append(v_s.n)
    wqkvT_done = v_s.n

    ln1_done = ln_apply(0, 8, 0)

    if "hT" in dumps:
        h = mkdump("hT", [128, DC * T], BF16)
        @blk.sync
        def _(sy):
            sy.wait_ge(v_s.h, ln1_done)
            d_sy.inc(sy.dma_start(out=h, in_=hT[:]), 16)

    @blk.gpsimd
    def _(g):
        g.wait_ge(t_s.h, wq_tr[-1])
        g.wait_ge(t_s.h, mod_mm)
        ap = arena.ap()[:, O_V65:O_V65 + 2 * TC * 130]
        g_s.inc(g.memset(ap.rearrange("p (b t h j) -> p b t h j", b=2, t=TC, h=2)
                         [:, :, :, :, 64:65], 1.0))
        g_s.inc(g.memset(arena[64:65, 24576 + 2 * T:24576 + 4 * T], 1.0))
        g_s.inc(g.memset(arena[0:16, O_IND:O_IND + 1024], 0.0))
        g.wait_ge(g_s.h, g_s.n)
        g_s.inc(g.affine_select(
            out=arena.ap()[0:16, O_IND:O_IND + 1024].rearrange(
                "p (a h j) -> p a h j", h=2, j=64),
            in_=arena.ap()[0:16, O_IND:O_IND + 1024].rearrange(
                "p (a h j) -> p a h j", h=2, j=64),
            compare_op=ALU.not_equal, fill=1.0, base=0,
            pattern=[[-2, 8], [-1, 2], [0, 64]], channel_multiplier=1))
    v65_init = g_s.n

    QB = [O_QK65 + 0 * T, O_QK65 + 1 * T]
    KB = [O_QK65 + 2 * T, O_QK65 + 3 * T]
    QC2 = max(1, NQC // 2)
    NB = 2 * QC2
    def qcs_of(b):
        qc2 = b % QC2
        return [q for q in (2 * qc2, 2 * qc2 + 1) if q < NQC]

    sqbf = arena[:, O_STGB:O_STGB + T]
    mqb = arena[:, O_MQR:O_MQR + 16]
    mqrow = arena[0:16, O_MQR + 128:O_MQR + 256]
    hstg = arena[:, O_HSTG:O_HSTG + 4096]
    qnf = rtr[:, 320:352]
    PT = [arena[:, O_PT:O_PT + 1024], arena[:, O_PT + 1024:O_PT + 2048]]
    sums_bf = arena[0:16, O_SUM:O_SUM + T]
    rcp_bf = arena[0:16, O_RCP:O_RCP + T]
    sstg = arena[:, O_SSTG:O_SSTG + 1024]
    h1s = [arena[0:64, O_H1S:O_H1S + 512], arena[0:64, O_H1S + 512:O_H1S + 1024]]
    h1_dma_all = [0, 0]
    sums_dma_all = [0, 0]
    SUMSCALE = 2097152.0

    def plan_qkv(p, evac_pred, hend_ref, odd_dma):
        tcnt = {}
        seq = ([('v', tc) for tc in range(TC)]
               + [('q', n) for n in range(NQC)] + [('k', n) for n in range(NQC)])
        jobs_t, jobs_v = [], []
        v_at_plan = v_s.n
        for si, (kind, idx) in enumerate(seq):
            bank = 6 + (si % 2)
            def tjob(t, kind=kind, idx=idx, si=si, bank=bank):
                if si >= 2:
                    t.wait_ge(v_s.h, evac_pred[si - 2])
                else:
                    t.wait_ge(v_s.h, v_at_plan)
                last = None
                if kind in ('q', 'k'):
                    ebase = (0 if kind == 'q' else 1024) + p * 128
                    for dc in range(DC):
                        last = t.matmul(
                            PB(bank),
                            arena[:, dc * 3072 + ebase: dc * 3072 + ebase + 128],
                            hT[:, dc * T + idx * 512: dc * T + (idx + 1) * 512],
                            start=(dc == 0), stop=(dc == DC - 1))
                else:
                    ebase = 2048 + p * 128
                    for dc in range(DC):
                        last = t.matmul(
                            PB(bank)[:, 0:128],
                            hT[:, dc * T + idx * 128: dc * T + (idx + 1) * 128],
                            arena[:, dc * 3072 + ebase: dc * 3072 + ebase + 128],
                            start=(dc == 0), stop=(dc == DC - 1))
                t_s.inc(last)
                tcnt[si] = t_s.n
            def vjob(v, kind=kind, idx=idx, si=si, bank=bank):
                v.wait_ge(t_s.h, tcnt[si])
                if kind == 'v':
                    dst = arena.ap()[:, O_V65 + (p % 2) * TC * 130 + idx * 130:
                                     O_V65 + (p % 2) * TC * 130 + (idx + 1) * 130]
                    v_s.inc(v.tensor_copy(
                        dst.rearrange("p (h j) -> p h j", h=2, j=65)[:, :, 0:64],
                        PB(bank)[:, 0:128].rearrange("p (h j) -> p h j", h=2, j=64)))
                else:
                    base = QB[0] if kind == 'q' else KB[0]
                    hoff = 0 if kind == 'q' else 2048
                    v.wait_ge(t_s.h, hend_ref['h0'])
                    v_s.inc(v.tensor_copy(
                        arena[0:64, base + idx * 512: base + (idx + 1) * 512],
                        PB(bank)[0:64, :]))
                    v.wait_ge(d_sy.h, odd_dma)
                    v_s.inc(v.tensor_copy(
                        hstg[64:128, hoff + idx * 512: hoff + (idx + 1) * 512],
                        PB(bank)[64:128, :]))
            jobs_t.append(tjob)
            jobs_v.append(vjob)
        nops = [1 if kind == 'v' else 2 for (kind, idx) in seq]
        return jobs_t, jobs_v, nops

    def emit_odd_dmas(pair_end_t):
        @blk.sync
        def _(sy):
            sy.wait_ge(t_s.h, pair_end_t)
            sy.wait_ge(v_s.h, v_s.n)
            d_sy.inc(sy.dma_start(out=arena[0:64, QB[1]:QB[1] + T],
                                  in_=hstg[64:128, 0:T]), 16)
            d_sy.inc(sy.dma_start(out=arena[0:64, KB[1]:KB[1] + T],
                                  in_=hstg[64:128, 2048:2048 + T]), 16)
        return d_sy.n

    def emit_norms(p, h, data_wait):
        res = {}
        @blk.vector
        def _(v):
            v.wait_ge(*data_wait)
            v_s.inc(v.tensor_mul(sqbf[0:64, :], arena[0:64, QB[h]:QB[h] + T],
                                 arena[0:64, QB[h]:QB[h] + T]))
        res['qsq'] = v_s.n
        @blk.tensor
        def _(t):
            t.wait_ge(v_s.h, res['qsq'])
            last = None
            for tcx in range(TC):
                last = t.matmul(PB(7)[:, tcx:tcx + 1],
                                sqbf[0:64, tcx * 128:(tcx + 1) * 128],
                                ones64[0:64, :], start=True, stop=True)
            t_s.inc(last)
        res['nq'] = t_s.n
        @blk.vector
        def _(v):
            v.wait_ge(t_s.h, res['nq'])
            v_s.inc(v.tensor_mul(sqbf[0:64, :], arena[0:64, KB[h]:KB[h] + T],
                                 arena[0:64, KB[h]:KB[h] + T]))
        res['ksq'] = v_s.n
        @blk.tensor
        def _(t):
            t.wait_ge(v_s.h, res['ksq'])
            last = None
            for tcx in range(TC):
                last = t.matmul(PB(7)[:, TC + tcx:TC + tcx + 1],
                                sqbf[0:64, tcx * 128:(tcx + 1) * 128],
                                ones64[0:64, :], start=True, stop=True)
            t_s.inc(last)
        res['nk'] = t_s.n
        @blk.vector
        def _(v):
            v.wait_ge(t_s.h, res['nk'])
            v_s.inc(v.reduce_max(mtmp[:, 0:1], PB(7)[:, TC:2 * TC], axis=AX))
        res['kred'] = v_s.n
        @blk.sync
        def _(sy):
            sy.wait_ge(v_s.h, res['kred'])
            d_sy.inc(sy.dma_start(out=mcol[0:1, 0:128], in_=mtmp[:, 0:1],
                                  allow_slow_non_contiguous=True), 16)
        res['coll'] = d_sy.n
        @blk.vector
        def _(v):
            v.wait_ge(d_sy.h, res['coll'])
            v_s.inc(v.reduce_max(mrow[0:1, 0:1], mcol[0:1, 0:128], axis=AX))
            v.wait_ge(v_s.h, v_s.n)
            v_s.inc(v.tensor_copy(mrowb[:], mrow[0:1, 0:1]))
        res['mkb'] = v_s.n
        @blk.tensor
        def _(t):
            t.wait_ge(v_s.h, res['mkb'])
            t_s.inc(t.matmul(PB(7)[:, 40:41], onesrow[:], mrowb[:],
                             start=True, stop=True))
        res['bc'] = t_s.n
        @blk.vector
        def _(v):
            v.wait_ge(t_s.h, res['bc'])
            v_s.inc(v.tensor_copy(mtmp[:, 4:5], PB(7)[:, 40:41]))
            v_s.inc(v.tensor_copy(qnf[:, 0:TC], PB(7)[:, 0:TC]))
        res['prep'] = v_s.n
        @blk.scalar
        def _(sc):
            sc.wait_ge(v_s.h, res['prep'])
            s_s.inc(sc.activation(qnf[:, 16:16 + TC], qnf[:, 0:TC], AF.Sqrt,
                                  scale=mtmp[:, 4:5]))
        res['mq'] = s_s.n
        @blk.vector
        def _(v):
            v.wait_ge(s_s.h, res['mq'])
            v_s.inc(v.tensor_scalar(mqb[:, 0:TC], qnf[:, 16:16 + TC], -1.0,
                                    None, ALU.mult))
        res['neg'] = v_s.n
        @blk.tensor
        def _(t):
            t.wait_ge(v_s.h, res['neg'])
            t_s.inc(t.transpose(PB(7).bitcast(BF16)[0:TC, 256:384],
                                mqb[:, 0:TC], id_b[:]))
        res['tr'] = t_s.n
        @blk.vector
        def _(v):
            v.wait_ge(t_s.h, res['tr'])
            v_s.inc(v.tensor_copy(mqrow[0:TC, :], PB(7).bitcast(BF16)[0:TC, 256:384]))
        res['row'] = v_s.n
        @blk.sync
        def _(sy):
            sy.wait_ge(v_s.h, res['row'])
            d_sy.inc(sy.dma_start(out=arena[64:65, QB[h]:QB[h] + T],
                                  in_=mqrow[0:TC, :]), 16)
        res['dma'] = d_sy.n
        return res['dma']

    pre_base = v_s.n
    nseq = TC + 2 * NQC
    nops_all = [1 if i < TC else 2 for i in range(nseq)]
    odd_prev = [d_sy.n]
    jt0, jv0, nops0 = plan_qkv(0, [pre_base + sum(nops_all[:i + 1])
                                   for i in range(nseq)],
                               {'h0': 0}, d_sy.n)
    @blk.tensor
    def _(t):
        t.wait_ge(v_s.h, wqkvT_done)
        t.wait_ge(v_s.h, ln1_done)
        t.wait_ge(g_s.h, v65_init)
        for j in jt0:
            j(t)
    @blk.vector
    def _(v):
        v.wait_ge(g_s.h, v65_init)
        for j in jv0:
            j(v)
    odd0 = emit_odd_dmas(t_s.n)
    nd0 = emit_norms(0, 0, (v_s.h, v_s.n))
    nd1 = emit_norms(0, 1, (d_sy.h, odd0))
    pair_ready = {0: (t_s.n, v_s.n, s_s.n, nd1)}

    if "qkv0" in dumps:
        h = mkdump("qkv0", [128, 4 * T + TC * 130], BF16)
        @blk.sync
        def _(sy):
            sy.wait_ge(v_s.h, pair_ready[0][1])
            sy.wait_ge(d_sy.h, d_sy.n)
            d_sy.inc(sy.dma_start(out=h[:, 0:4 * T],
                                  in_=arena[:, O_QK65:O_QK65 + 4 * T]), 16)
            d_sy.inc(sy.dma_start(out=h[:, 4 * T:],
                                  in_=arena[:, O_V65:O_V65 + TC * 130]), 16)

    for p in range(NPAIR):
        t0, v0, s0, mdma = pair_ready[p]
        NG2 = NB * KC

        njobs = nseq if p + 1 < NPAIR else 0
        per_b = [[] for _ in range(NB)]
        if njobs:
            for i in range(TC):
                per_b[min(i * QC2 // TC, QC2 - 1)].append(i)
            for i in range(TC, nseq):
                per_b[QC2 + min((i - TC) * QC2 // (2 * NQC), QC2 - 1)].append(i)

        v_base = v_s.n
        nops = nops_all
        evac_done = []
        job_pred = {}
        acc = v_base
        for b in range(NB):
            acc += 2 * len(qcs_of(b))
            evac_done.append(acc)
            for i in per_b[b]:
                job_pred[i] = acc + sum(nops[i2] for i2 in per_b[b]
                                        if i2 <= i)
            acc += sum(nops[i2] for i2 in per_b[b])
        s_base = s_s.n
        hend = {'h0': 0}

        if p + 1 < NPAIR:
            jobs_t, jobs_v, _ = plan_qkv(
                p + 1, [job_pred[i] for i in range(njobs)],
                hend, odd_prev[0])
        else:
            jobs_t, jobs_v = [], []

        qk_cnt, av_cnt = {}, {}

        def emit_av(t, b, kc, h):
            g2 = b * KC + kc
            par = g2 % 2
            qcs = qcs_of(b)
            t.wait_ge(s_s.h, s_base + g2 + 1)
            if kc == 0 and b >= 1:
                t.wait_ge(v_s.h, evac_done[b - 1])
            vb = O_V65 + (p % 2) * TC * 130
            last = None
            for j, qc in enumerate(qcs):
                last = t.matmul(
                    AVP[j][0:65, :],
                    arena[:, vb + kc * 130 + h * 65: vb + kc * 130 + h * 65 + 65],
                    PT[par][:, j * 512:(j + 1) * 512],
                    start=(kc == 0), stop=(kc == KC - 1))
            av_cnt[g2] = t_s.inc(last)

        @blk.tensor
        def _(t, p=p):
            t.wait_ge(v_s.h, v0)
            t.wait_ge(d_sy.h, mdma)
            for b in range(NB):
                h = b // QC2
                qcs = qcs_of(b)
                for kc in range(KC):
                    g2 = b * KC + kc
                    par = g2 % 2
                    if g2 >= 2:
                        t.wait_ge(s_s.h, s_base + (g2 - 2) + 1)
                    elif p > 0:
                        t.wait_ge(s_s.h, s_base)
                    last = None
                    for j, qc in enumerate(qcs):
                        last = t.matmul(
                            SC[par][:, j * 512:(j + 1) * 512],
                            arena[0:65, KB[h] + kc * 128: KB[h] + (kc + 1) * 128],
                            arena[0:65, QB[h] + qc * 512: QB[h] + (qc + 1) * 512],
                            start=True, stop=True)
                    qk_cnt[g2] = t_s.inc(last)
                    if kc >= 1:
                        emit_av(t, b, kc - 1, h)
                emit_av(t, b, KC - 1, h)
                if b == QC2 - 1:
                    hend['h0'] = t_s.n
                if b == NB - 1:
                    hend['h1'] = t_s.n
                for i in per_b[b]:
                    jobs_t[i](t)

        @blk.scalar
        def _(sc, p=p):
            sc.wait_ge(t_s.h, t0)
            for b in range(NB):
                w = 512 * len(qcs_of(b))
                for kc in range(KC):
                    g2 = b * KC + kc
                    par = g2 % 2
                    sc.wait_ge(t_s.h, qk_cnt[g2])
                    if g2 >= 2:
                        sc.wait_ge(t_s.h, av_cnt[g2 - 2])
                    s_s.inc(sc.activation(PT[par][:, 0:w], SC[par][:, 0:w],
                                          AF.Exp, scale=0.125))
        assert s_s.n == s_base + NG2

        for b in range(NB):
            h = b // QC2
            @blk.vector
            def _(v, p=p, b=b, h=h):
                v.wait_ge(t_s.h, av_cnt[b * KC + KC - 1])
                for j, qc in enumerate(qcs_of(b)):
                    gi = 2 * p + h
                    if h == 0:
                        v_s.inc(v.tensor_copy(
                            aun[0:64, p * T + qc * 512: p * T + (qc + 1) * 512],
                            AVP[j][0:64, :]))
                    else:
                        v.wait_ge(d_sy.h, h1_dma_all[-2])
                        v_s.inc(v.tensor_copy(h1s[qc % 2], AVP[j][0:64, :]))
                    v.wait_ge(d_sy.h, sums_dma_all[-2])
                    v_s.inc(v.tensor_scalar(
                        sstg[64:65, (qc % 2) * 512:(qc % 2) * 512 + 512],
                        AVP[j][64:65, :], SUMSCALE, None, ALU.mult))
                for i in per_b[b]:
                    jobs_v[i](v)
            assert v_s.n == evac_done[b] + sum(nops[i2] for i2 in per_b[b])

            @blk.sync
            def _(sy, p=p, b=b, h=h):
                sy.wait_ge(v_s.h, v_s.n)
                for qc in qcs_of(b):
                    if h == 1:
                        d_sy.inc(sy.dma_start(
                            out=aun[64:128, p * T + qc * 512: p * T + (qc + 1) * 512],
                            in_=h1s[qc % 2]), 16)
                        h1_dma_all.append(d_sy.n)
                    d_sy.inc(sy.dma_start(
                        out=sums_bf[2 * p + h: 2 * p + h + 1, qc * 512:(qc + 1) * 512],
                        in_=sstg[64:65, (qc % 2) * 512:(qc % 2) * 512 + 512]), 16)
                    sums_dma_all.append(d_sy.n)

        if p + 1 < NPAIR:
            odd_n = emit_odd_dmas(hend['h1'])
            odd_prev[0] = odd_n
            ndh0 = emit_norms(p + 1, 0, (v_s.h, v_s.n))
            ndh1 = emit_norms(p + 1, 1, (d_sy.h, odd_n))
            pair_ready[p + 1] = (t_s.n, v_s.n, s_s.n, ndh1)

    attn_t, attn_v = t_s.n, v_s.n

    if "aun" in dumps:
        h = mkdump("aun", [128, NPAIR * T], BF16)
        @blk.sync
        def _(sy):
            sy.wait_ge(v_s.h, attn_v)
            d_sy.inc(sy.dma_start(out=h, in_=aun[:]), 16)
    if "sums" in dumps:
        h = mkdump("sums", [16, T], BF16)
        @blk.sync
        def _(sy):
            sy.wait_ge(d_sy.h, d_sy.n)
            d_sy.inc(sy.dma_start(out=h, in_=sums_bf[:]), 16)

    @blk.vector
    def _(v):
        v.wait_ge(d_sy.h, d_sy.n)
        with nc.allow_low_precision(reason="softmax denominators in bf16; error budget analyzed"):
            v_s.inc(v.reciprocal(rcp_bf[:], sums_bf[:]))
    rcp_done = v_s.n

    rs_pred = v_s.n
    rs_t = {}
    @blk.tensor
    def _(t):
        t.wait_ge(v_s.h, rcp_done)
        for p in range(NPAIR):
            for qc in range(NQC):
                i = p * NQC + qc
                bank = 4 + (i % 2)
                if i >= 2:
                    t.wait_ge(v_s.h, rs_pred + (i - 2) + 1)
                rs_t[i] = t_s.inc(t.matmul(
                    PB(bank), arena[0:16, O_IND + p * 128: O_IND + (p + 1) * 128],
                    rcp_bf[:, qc * 512:(qc + 1) * 512], start=True, stop=True))

    @blk.vector
    def _(v):
        for p in range(NPAIR):
            for qc in range(NQC):
                i = p * NQC + qc
                bank = 4 + (i % 2)
                v.wait_ge(t_s.h, rs_t[i])
                sl = aun[:, p * T + qc * 512: p * T + (qc + 1) * 512]
                v_s.inc(v.scalar_tensor_tensor(sl, PB(bank), SUMSCALE, sl,
                                               ALU.mult, ALU.mult))
    rescale_done = v_s.n
    assert rescale_done == rs_pred + NPAIR * NQC

    wo_dma, wo_sc, wo_tr, wo_ev = [], [], [], []
    for ch in range(8):
        @blk.sync
        def _(sy, ch=ch):
            if ch == 0:
                sy.wait_ge(t_s.h, attn_t)
            else:
                sy.wait_ge(t_s.h, wo_tr[ch - 1])
            d_sy.inc(sy.dma_start(out=stga, in_=wo_h[ch * 128:(ch + 1) * 128, :]), 16)
            wo_dma.append(d_sy.n)
        @blk.vector
        def _(v, ch=ch):
            v.wait_ge(d_sy.h, wo_dma[ch])
            v_s.inc(v.tensor_scalar(stga, stga, coef[:, 16 + ch:16 + ch + 1],
                                    None, ALU.mult))
            wo_sc.append(v_s.n)
        @blk.tensor
        def _(t, ch=ch):
            t.wait_ge(v_s.h, wo_sc[ch])
            if ch >= 1:
                t.wait_ge(v_s.h, wo_ev[ch - 1])
            last = None
            for dc in range(DC):
                bank = 6 + (dc // 4)
                last = t.transpose(PB(bank)[:, (dc % 4) * 128:(dc % 4) * 128 + 128],
                                   stga[:, dc * 128:(dc + 1) * 128], id_f[:])
            t_s.inc(last)
            wo_tr.append(t_s.n)
        @blk.vector
        def _(v, ch=ch):
            v.wait_ge(t_s.h, wo_tr[ch])
            last = None
            for dc in range(DC):
                bank = 6 + (dc // 4)
                last = v.tensor_copy(
                    arena[:, O_WOT + dc * 1024 + ch * 128:
                          O_WOT + dc * 1024 + (ch + 1) * 128],
                    PB(bank)[:, (dc % 4) * 128:(dc % 4) * 128 + 128])
            v_s.inc(last)
            wo_ev.append(v_s.n)
    wot_done = v_s.n

    op_pred = v_s.n
    op_t = {}
    @blk.tensor
    def _(t):
        t.wait_ge(v_s.h, rescale_done)
        t.wait_ge(v_s.h, wot_done)
        for tc in range(TC):
            for ec in range(2):
                i = tc * 2 + ec
                bank = i % 2
                if i >= 2:
                    t.wait_ge(v_s.h, op_pred + (i - 2) + 1)
                last = None
                for pc in range(NPAIR):
                    last = t.matmul(PB(bank),
                                    aun[:, pc * T + tc * 128: pc * T + (tc + 1) * 128],
                                    arena[:, O_WOT + pc * 1024 + ec * 512:
                                          O_WOT + pc * 1024 + (ec + 1) * 512],
                                    start=(pc == 0), stop=(pc == NPAIR - 1))
                op_t[i] = t_s.inc(last)

    @blk.vector
    def _(v):
        for tc in range(TC):
            for ec in range(2):
                i = tc * 2 + ec
                bank = i % 2
                v.wait_ge(t_s.h, op_t[i])
                sl = x_res[:, tc * 1024 + ec * 512: tc * 1024 + (ec + 1) * 512]
                v_s.inc(v.tensor_add(sl, sl, PB(bank)))
    x1_done = v_s.n

    if "x1" in dumps:
        h = mkdump("x1", [128, TC * 1024], BF16)
        @blk.sync
        def _(sy):
            sy.wait_ge(v_s.h, x1_done)
            d_sy.inc(sy.dma_start(out=h, in_=x_res[:]), 16)

    ln_stats(2 * TC, xwait=(v_s.h, x1_done))
    ln2_done = ln_apply(2 * TC, 32, 24, first_wait=[(v_s.h, x1_done)])

    if "h2T" in dumps:
        h = mkdump("h2T", [128, DC * T], BF16)
        @blk.sync
        def _(sy):
            sy.wait_ge(v_s.h, ln2_done)
            d_sy.inc(sy.dma_start(out=h, in_=hT[:]), 16)

    @blk.sync
    def _(sy):
        for dcx in range(DC):
            d_sy.inc(sy.dma_start(
                out=gwf[:, dcx * 8:(dcx + 1) * 8],
                in_=gatew_h[:, dcx * 128:(dcx + 1) * 128].rearrange("e p -> p e"),
                allow_slow_non_contiguous=True), 16)
    gw_dma = d_sy.n

    @blk.vector
    def _(v):
        v.wait_ge(d_sy.h, gw_dma)
        v_s.inc(v.tensor_copy(gwt[:], gwf[:]))
    gwt_done = v_s.n

    @blk.tensor
    def _(t):
        t.wait_ge(v_s.h, gwt_done)
        t.wait_ge(v_s.h, ln2_done)
        last = None
        for tc in range(TC):
            for dc in range(DC):
                last = t.matmul(PB(6)[:, tc * 8:(tc + 1) * 8],
                                hT[:, dc * T + tc * 128: dc * T + (tc + 1) * 128],
                                gwt[:, dc * 8:(dc + 1) * 8],
                                start=(dc == 0), stop=(dc == DC - 1))
        t_s.inc(last)
    lg_done = t_s.n

    @blk.scalar
    def _(sc):
        sc.wait_ge(t_s.h, lg_done)
        s_s.inc(sc.activation(rtr[:, 0:8 * TC], PB(6)[:, 0:8 * TC], AF.Sigmoid))
    sig_done = s_s.n

    @blk.vector
    def _(v):
        v.wait_ge(s_s.h, sig_done)
        v.wait_ge(g_s.h, g_const)
        SCO, SFC, TMP = 0, 128, 256
        for tc in range(TC):
            sco = rtr[:, SCO + tc * 8: SCO + tc * 8 + 8]
            sfc = rtr[:, SFC + tc * 8: SFC + tc * 8 + 8]
            tmp = rtr[:, TMP:TMP + 16]
            v_s.inc(v.tensor_add(sfc, sco, gbias[:]))
            v.wait_ge(v_s.h, v_s.n)
            t2 = rtr.ap()[:, TMP:TMP + 16].rearrange("p (g k) -> p g k", g=2, k=8)
            s2 = rtr.ap()[:, SFC + tc * 8: SFC + tc * 8 + 8].rearrange(
                "p (g k) -> p g k", g=2, k=4)
            pr = [(0, 1), (0, 2), (0, 3), (1, 2), (1, 3), (2, 3)]
            for j, (a, b) in enumerate(pr):
                v_s.inc(v.tensor_add(t2[:, :, j:j + 1], s2[:, :, a:a + 1],
                                     s2[:, :, b:b + 1]))
            v.wait_ge(v_s.h, v_s.n)
            for j in (1, 2, 3, 4, 5):
                v_s.inc(v.tensor_max(t2[:, :, 0:1], t2[:, :, 0:1], t2[:, :, j:j + 1]))
                v.wait_ge(v_s.h, v_s.n)
            v_s.inc(v.tensor_tensor(tmp[:, 12:13], tmp[:, 0:1], tmp[:, 8:9],
                                    ALU.is_ge))
            v.wait_ge(v_s.h, v_s.n)
            v_s.inc(v.tensor_scalar(tmp[:, 13:14], tmp[:, 12:13], -1.0, 1.0,
                                    ALU.mult, ALU.add))
            msk = rtr[:, TMP + 16:TMP + 24]
            v.wait_ge(v_s.h, v_s.n)
            v_s.inc(v.tensor_scalar(msk[:, 0:4], sfc[:, 0:4], tmp[:, 12:13],
                                    None, ALU.mult))
            v_s.inc(v.tensor_scalar(msk[:, 4:8], sfc[:, 4:8], tmp[:, 13:14],
                                    None, ALU.mult))
            v.wait_ge(v_s.h, v_s.n)
            v_s.inc(v.reduce_max(tmp[:, 14:15], msk, axis=AX))
            v.wait_ge(v_s.h, v_s.n)
            ge1 = rtr[:, TMP + 24:TMP + 32]
            v_s.inc(v.tensor_scalar(ge1, msk, tmp[:, 14:15], None, ALU.is_ge))
            v.wait_ge(v_s.h, v_s.n)
            v_s.inc(v.scalar_tensor_tensor(ge1, ge1, -3.0, msk, ALU.mult, ALU.add))
            v.wait_ge(v_s.h, v_s.n)
            v_s.inc(v.reduce_max(tmp[:, 15:16], ge1, axis=AX))
            v.wait_ge(v_s.h, v_s.n)
            sel = rtr[:, TMP + 32:TMP + 40]
            v_s.inc(v.tensor_scalar(sel, msk, tmp[:, 15:16], None, ALU.is_ge))
            v.wait_ge(v_s.h, v_s.n)
            we = rtr[:, TMP + 40:TMP + 48]
            v_s.inc(v.tensor_mul(we, sco, sel))
            v.wait_ge(v_s.h, v_s.n)
            v_s.inc(v.reduce_sum(tmp[:, 14:15], we, axis=AX))
            v.wait_ge(v_s.h, v_s.n)
            v_s.inc(v.tensor_scalar(tmp[:, 14:15], tmp[:, 14:15], 1e-20, None,
                                    ALU.add))
            v.wait_ge(v_s.h, v_s.n)
            v_s.inc(v.reciprocal(tmp[:, 15:16], tmp[:, 14:15]))
            v.wait_ge(v_s.h, v_s.n)
            v_s.inc(v.tensor_scalar(cwsq[:, tc * 8:(tc + 1) * 8], we,
                                    tmp[:, 15:16], 2.5, ALU.mult, ALU.mult))
    cw_done = v_s.n

    if "cw" in dumps:
        h = mkdump("cw", [128, 8 * TC])
        @blk.sync
        def _(sy):
            sy.wait_ge(v_s.h, cw_done)
            d_sy.inc(sy.dma_start(out=h, in_=cwsq[:]), 16)

    @blk.scalar
    def _(sc):
        sc.wait_ge(v_s.h, cw_done)
        s_s.inc(sc.activation(cwsq[:], cwsq[:], AF.Sqrt))
    sqcw_done = s_s.n

    def prep(jobs, first_sync_wait=None, first_tensor_wait=None):
        dmac, scc, trc, evc = [], [], [], []
        for ji, job in enumerate(jobs):
            @blk.sync
            def _(sy, ji=ji, job=job):
                if ji == 0 and first_sync_wait is not None:
                    sy.wait_ge(*first_sync_wait)
                if ji >= 1:
                    sy.wait_ge(t_s.h, trc[ji - 1])
                d_sy.inc(sy.dma_start(out=fstg[:, 0:job['width']], in_=job['src']), 16)
                dmac.append(d_sy.n)
            if job.get('scales'):
                @blk.vector
                def _(v, ji=ji, job=job):
                    v.wait_ge(d_sy.h, dmac[ji])
                    last = None
                    for (lo, hi, cc) in job['scales']:
                        last = v.tensor_scalar(fstg[:, lo:hi], fstg[:, lo:hi],
                                               coef[:, cc:cc + 1], None, ALU.mult)
                    v_s.inc(last)
                    scc.append(v_s.n)
            else:
                scc.append(None)
            @blk.tensor
            def _(t, ji=ji, job=job):
                if ji == 0 and first_tensor_wait is not None:
                    t.wait_ge(*first_tensor_wait)
                if scc[ji] is not None:
                    t.wait_ge(v_s.h, scc[ji])
                else:
                    t.wait_ge(d_sy.h, dmac[ji])
                if ji >= 1:
                    t.wait_ge(v_s.h, evc[ji - 1])
                last = None
                for bi, (slo, dlo) in enumerate(job['blocks']):
                    bank = 6 + (bi // 4)
                    last = t.transpose(PB(bank)[:, (bi % 4) * 128:(bi % 4) * 128 + 128],
                                       fstg[:, slo:slo + 128], id_f[:])
                t_s.inc(last)
                trc.append(t_s.n)
            @blk.vector
            def _(v, ji=ji, job=job):
                v.wait_ge(t_s.h, trc[ji])
                last = None
                dstt = arena if job['dst'] == 'arena' else aun
                for bi, (slo, dlo) in enumerate(job['blocks']):
                    bank = 6 + (bi // 4)
                    last = v.tensor_copy(dstt[:, dlo:dlo + 128],
                                         PB(bank)[:, (bi % 4) * 128:(bi % 4) * 128 + 128])
                v_s.inc(last)
                evc.append(v_s.n)
        return trc, evc

    def shup_jobs():
        return [dict(src=shup_h[ch * 128:(ch + 1) * 128, :], width=1024, dst='arena',
                     blocks=[(dc * 128, F_SHUP + dc * SI + ch * 128)
                             for dc in range(DC)])
                for ch in range(16)]

    def shdn_jobs():
        jobs = []
        for ld in range(16):
            hc, half = ld // 2, ld % 2
            jobs.append(dict(
                src=shdown_h[hc * 128:(hc + 1) * 128, half * 1024:(half + 1) * 1024],
                width=1024, dst='arena',
                scales=[(0, 1024, 40 + hc)],
                blocks=[(s * 128, F_SHDN + (half * 8 + s) * 1024 + hc * 128)
                        for s in range(8)]))
        return jobs

    def up_jobs(e, dstbase):
        return [dict(src=upw_h[e, ch * 128:(ch + 1) * 128, :], width=1024,
                     dst='arena',
                     blocks=[(dc * 128, dstbase + dc * 512 + ch * 128)
                             for dc in range(DC)])
                for ch in range(4)]

    def down_jobs(e, dstbase, dst):
        jobs = []
        for ld in range(4):
            h0 = ld * 2
            jobs.append(dict(
                src=downw_h[e, h0 * 128:(h0 + 2) * 128, :].rearrange(
                    "(c p) i -> p c i", p=128),
                width=1024, dst=dst,
                scales=[(0, 512, 40 + h0), (512, 1024, 40 + h0 + 1)],
                blocks=[(c * 512 + ic * 128, dstbase + ic * 1024 + (h0 + c) * 128)
                        for c in range(2) for ic in range(4)]))
        return jobs

    if NPAIR * T >= 16384:
        DNA_DST, F_DNA = 'aun', 0
    else:
        DNA_DST, F_DNA = 'arena', F_UPA + 16384
    jobsA = shup_jobs() + shdn_jobs()
    for e in range(4):
        jobsA += up_jobs(e, F_UPA + e * 4096)
        jobsA += down_jobs(e, F_DNA + e * 4096, DNA_DST)
    trcA, evcA = prep(jobsA, first_sync_wait=(v_s.h, ln2_done),
                      first_tensor_wait=(s_s.h, sig_done))
    prepA_done = v_s.n

    prev_dn = [None]

    def ffn_pass(items, wait_first, final):
        outev = []
        NI = len(items)
        for tc in range(TC):
            s_b = s_s.n
            v_b = v_s.n
            up_cnt, dn_cnt, tr_cnt = {}, {}, {}
            pdn = prev_dn[0]

            @blk.tensor
            def _(t, tc=tc, s_b=s_b, v_b=v_b):
                if tc == 0:
                    for w in wait_first:
                        t.wait_ge(*w)
                for ii, item in enumerate(items):
                    ubank = 2 + (ii % 2)
                    tbank = 4 + (ii % 2)
                    if ii >= 2:
                        t.wait_ge(s_s.h, s_b + (ii - 2) + 1)
                    elif tc >= 1:
                        t.wait_ge(s_s.h, s_b)
                    last = None
                    for dc in range(DC):
                        if item[0] == 'sh':
                            rhs = arena[:, F_SHUP + dc * SI + item[1] * 512:
                                        F_SHUP + dc * SI + (item[1] + 1) * 512]
                        else:
                            rhs = arena[:, item[2] + dc * 512:
                                        item[2] + (dc + 1) * 512]
                        last = t.matmul(PB(ubank),
                                        hT[:, dc * T + tc * 128: dc * T + (tc + 1) * 128],
                                        rhs, start=(dc == 0), stop=(dc == DC - 1))
                    up_cnt[ii] = t_s.inc(last)
                    t.wait_ge(v_s.h, v_b + 2 * ii + 1)
                    if ii >= 2:
                        t.wait_ge(v_s.h, v_b + 2 * (ii - 2) + 2)
                    elif tc >= 1:
                        t.wait_ge(v_s.h, v_b)
                    last = None
                    for ic in range(4):
                        last = t.transpose(
                            PB(tbank).bitcast(BF16)[:, ic * 128:(ic + 1) * 128],
                            ffr[:, (2 + ii % 2) * 512 + ic * 128:
                                (2 + ii % 2) * 512 + (ic + 1) * 128], id_b[:])
                    tr_cnt[ii] = t_s.inc(last)
                    t.wait_ge(v_s.h, v_b + 2 * ii + 2)
                    if ii == 0 and tc >= 1:
                        t.wait_ge(v_s.h, outev[tc - 1])
                    last = None
                    for hc in range(2):
                        for ic in range(4):
                            if item[0] == 'sh':
                                rhs = arena[:, F_SHDN + (item[1] * 4 + ic) * 1024 + hc * 512:
                                            F_SHDN + (item[1] * 4 + ic) * 1024 + (hc + 1) * 512]
                            else:
                                wsrc = arena if item[4] == 'arena' else aun
                                rhs = wsrc[:, item[3] + ic * 1024 + hc * 512:
                                           item[3] + ic * 1024 + (hc + 1) * 512]
                            last = t.matmul(PB(hc),
                                            ffr[:, (4 + ii % 2) * 512 + ic * 128:
                                                (4 + ii % 2) * 512 + (ic + 1) * 128],
                                            rhs,
                                            start=(ii == 0 and ic == 0),
                                            stop=(ii == NI - 1 and ic == 3))
                    dn_cnt[ii] = t_s.inc(last)

            @blk.scalar
            def _(sc, tc=tc, s_b=s_b, v_b=v_b):
                for ii, item in enumerate(items):
                    sc.wait_ge(t_s.h, up_cnt[ii])
                    if ii < 2 and tc >= 1:
                        sc.wait_ge(v_s.h, v_b)
                    if item[0] == 'e':
                        scale = cwsq[:, tc * 8 + item[1]: tc * 8 + item[1] + 1]
                    else:
                        scale = 1.0
                    s_s.inc(sc.activation(ffr[:, (ii % 2) * 512:(ii % 2) * 512 + 512],
                                          PB(2 + ii % 2), AF.Relu, scale=scale))
            assert s_s.n == s_b + NI

            @blk.vector
            def _(v, tc=tc, s_b=s_b, pdn=pdn):
                if pdn is not None:
                    v.wait_ge(t_s.h, pdn)
                for ii, item in enumerate(items):
                    v.wait_ge(s_s.h, s_b + ii + 1)
                    rw = ffr[:, (ii % 2) * 512:(ii % 2) * 512 + 512]
                    v_s.inc(v.tensor_mul(
                        ffr[:, (2 + ii % 2) * 512:(2 + ii % 2) * 512 + 512], rw, rw))
                    v.wait_ge(t_s.h, tr_cnt[ii])
                    v_s.inc(v.tensor_copy(
                        ffr[:, (4 + ii % 2) * 512:(4 + ii % 2) * 512 + 512],
                        PB(4 + ii % 2).bitcast(BF16)[:, 0:512]))
            assert v_s.n == v_b + 2 * NI

            @blk.vector
            def _(v, tc=tc):
                v.wait_ge(t_s.h, dn_cnt[NI - 1])
                if final == 'x':
                    for hc in range(2):
                        sl = x_res[:, tc * 1024 + hc * 512: tc * 1024 + (hc + 1) * 512]
                        v_s.inc(v.tensor_add(sl, sl, PB(hc)))
                else:
                    if tc >= 2:
                        v.wait_ge(d_out.h, 16 * (tc - 1))
                    os = arf[:, F_OUTS // 2 + (tc % 2) * 1024:
                             F_OUTS // 2 + (tc % 2) * 1024 + 1024]
                    for hc in range(2):
                        v_s.inc(v.tensor_tensor(
                            os[:, hc * 512:(hc + 1) * 512],
                            x_res[:, tc * 1024 + hc * 512: tc * 1024 + (hc + 1) * 512],
                            PB(hc), ALU.add))
                outev.append(v_s.n)
            prev_dn[0] = dn_cnt[NI - 1]

            if final == 'out':
                @blk.sync
                def _(sy, tc=tc):
                    sy.wait_ge(v_s.h, outev[tc])
                    os = arf[:, F_OUTS // 2 + (tc % 2) * 1024:
                             F_OUTS // 2 + (tc % 2) * 1024 + 1024]
                    d_out.inc(sy.dma_start(out=out_h[tc * 128:(tc + 1) * 128, :],
                                           in_=os), 16)
        return outev[-1]

    itemsS = [('sh', sic) for sic in range(4)]
    sS = ffn_pass(itemsS, [(v_s.h, prepA_done), (s_s.h, sqcw_done)], 'x')

    jobsB = []
    for e in range(4, 8):
        jobsB += up_jobs(e, F_UPB + (e - 4) * 4096)
        jobsB += down_jobs(e, F_DNB + (e - 4) * 4096, 'arena')
    trcB, evcB = prep(jobsB, first_sync_wait=(v_s.h, sS))
    prepB_done = v_s.n

    itemsR1 = [('e', e, F_UPA + e * 4096, F_DNA + e * 4096, DNA_DST) for e in range(4)]
    sR1 = ffn_pass(itemsR1, [(v_s.h, prepA_done), (v_s.h, sS)], 'x')

    itemsR2 = [('e', e, F_UPB + (e - 4) * 4096, F_DNB + (e - 4) * 4096, 'arena')
               for e in range(4, 8)]
    sR2 = ffn_pass(itemsR2, [(v_s.h, prepB_done), (v_s.h, sR1)], 'out')

    @blk.sync
    def _(sy):
        sy.wait_ge(d_out.h, 16 * TC)

    st.close()
    return nc, dmp


from concourse.bass_utils import run_bass_kernel_spmd


def kernel(L0_attn_x, L0_attn_w_qkv, L0_attn_w_o, L0_ffn_gate_weight,
           L0_ffn_gate_bias, L0_ffn_up_weights, L0_ffn_down_weights,
           L0_ffn_shared_up_weight, L0_ffn_shared_down_weight, L0_c, L0_wmod):
    B, T, Dm = L0_attn_x.shape
    assert (B, T, Dm) == (8, 2048, 1024)
    nc, _ = build(T=T)
    shared = dict(
        w_qkv=np.ascontiguousarray(L0_attn_w_qkv),
        w_o=np.ascontiguousarray(L0_attn_w_o),
        gate_w=np.ascontiguousarray(L0_ffn_gate_weight),
        gate_b=np.ascontiguousarray(L0_ffn_gate_bias),
        up_w=np.ascontiguousarray(L0_ffn_up_weights),
        down_w=np.ascontiguousarray(L0_ffn_down_weights),
        shup_w=np.ascontiguousarray(L0_ffn_shared_up_weight),
        shdown_w=np.ascontiguousarray(L0_ffn_shared_down_weight),
        wmod=np.ascontiguousarray(L0_wmod),
    )
    in_maps = []
    for i in range(8):
        m = dict(shared)
        m["x"] = np.ascontiguousarray(L0_attn_x[i])
        m["c"] = np.ascontiguousarray(L0_c[i])
        in_maps.append(m)
    res = run_bass_kernel_spmd(nc, in_maps, core_ids=list(range(8)))
    return np.stack([res.results[i]["out"] for i in range(8)], axis=0)


# version 05
# speedup vs baseline: 1.2519x; 1.2519x over previous
import numpy as np
from contextlib import ExitStack

import concourse.bass as bass
import concourse.mybir as mybir

F32 = mybir.dt.float32
BF16 = mybir.dt.bfloat16
AF = mybir.ActivationFunctionType
ALU = mybir.AluOpType
AX = mybir.AxisListType.X

D = 1024
DC = 8
NPAIR = 8
E = 8
I_FF = 512
SI = 2048
LN_EPS = 1e-5


class Sem:
    def __init__(self, stack, nc, name):
        self.h = stack.enter_context(nc.semaphore(name))
        self.n = 0

    def inc(self, inst, k=1):
        inst.then_inc(self.h, k)
        self.n += k
        return self.n


def build(T=2048, dumps=(), detect_races=False):
    TC = T // 128
    NQC = T // 512
    KC = T // 128
    assert T % 512 == 0

    nc = bass.Bass("TRN2", target_bir_lowering=False,
                   detect_race_conditions=detect_races)

    x_h = nc.dram_tensor("x", [T, D], F32, kind="ExternalInput").ap()
    wqkv_h = nc.dram_tensor("w_qkv", [3 * D, D], F32, kind="ExternalInput").ap()
    wo_h = nc.dram_tensor("w_o", [D, D], F32, kind="ExternalInput").ap()
    gatew_h = nc.dram_tensor("gate_w", [E, D], F32, kind="ExternalInput").ap()
    gateb_h = nc.dram_tensor("gate_b", [E], F32, kind="ExternalInput").ap()
    upw_h = nc.dram_tensor("up_w", [E, I_FF, D], F32, kind="ExternalInput").ap()
    downw_h = nc.dram_tensor("down_w", [E, D, I_FF], F32, kind="ExternalInput").ap()
    shup_h = nc.dram_tensor("shup_w", [SI, D], F32, kind="ExternalInput").ap()
    shdown_h = nc.dram_tensor("shdown_w", [D, SI], F32, kind="ExternalInput").ap()
    c_h = nc.dram_tensor("c", [D], F32, kind="ExternalInput").ap()
    wmod_h = nc.dram_tensor("wmod", [D, 6 * D], F32, kind="ExternalInput").ap()
    out_h = nc.dram_tensor("out", [T, D], F32, kind="ExternalOutput").ap()

    dmp = {}
    def mkdump(name, shape, dtype=F32):
        if name in dumps:
            dmp[name] = nc.dram_tensor("dmp_" + name, list(shape), dtype,
                                       kind="ExternalOutput").ap()
        return dmp.get(name)

    st = ExitStack()
    sbt = lambda name, shape, dt: st.enter_context(nc.sbuf_tensor(name, shape, dt))

    x_res = sbt("x_res", [128, TC * 1024], BF16)
    hT = sbt("hT", [128, DC * T], BF16)
    aun = sbt("aun", [128, NPAIR * T], BF16)
    ARN = 51200 if NPAIR * T >= 16384 else 81920
    arena = sbt("arena", [128, ARN], BF16)
    arf = arena.ap().bitcast(F32)

    modT = sbt("modT", [128, 48], F32)
    coef = sbt("coef", [128, 48], F32)
    lnst = sbt("lnst", [128, 4 * TC], F32)
    stat6 = sbt("stat6", [128, 16], F32)
    sil_c = sbt("sil_c", [128, 8], BF16)
    sil_f = sbt("sil_f", [128, 16], F32)
    id_f = sbt("id_f", [128, 128], F32)
    id_b = sbt("id_b", [128, 128], BF16)
    ones64 = sbt("ones64", [128, 1], BF16)
    gwt = sbt("gwt", [128, 64], BF16)
    gwf = sbt("gwf", [128, 64], F32)
    gbias = sbt("gbias", [128, 8], F32)
    gb1 = sbt("gb1", [1, 8], F32)
    rtr = sbt("rtr", [128, 384], F32)
    cwsq = sbt("cwsq", [128, 8 * TC], F32)
    negm = sbt("negm", [128, 4], F32)
    mtmp = sbt("mtmp", [128, 8], F32)
    mcol = sbt("mcol", [1, 256], F32)
    mrow = sbt("mrow", [1, 8], F32)
    mrowb = sbt("mrowb", [1, 1], BF16)
    gb1b = sbt("gb1b", [1, 8], BF16)
    onesrow = sbt("onesrow", [1, 128], BF16)
    ffr = sbt("ffr", [128, 6 * 512], BF16)

    O_WQKVT = 0
    O_WOT = 16384
    O_QK65 = 24576
    O_V65 = O_QK65 + 4 * T
    O_PT = O_V65 + 2 * TC * 130
    O_SUM = O_PT + 2048
    O_RCP = O_SUM
    O_SSTG = O_SUM + T
    O_H1S = O_SSTG + 1024
    O_IND = O_H1S + 1024
    O_MQR = O_IND + 1024
    O_HSTG = O_MQR + 256
    O_STGB = O_HSTG + 4096
    assert O_STGB + 2048 <= ARN, O_STGB

    P_WMF = 24576
    P_WMB = P_WMF + 12288
    P_WQS = P_WMB + 6144
    assert P_WQS + 4096 <= ARN

    W_E_UP = {0: 8192, 1: 12288, 2: 16384, 3: 20480,
              4: 24576, 5: 28672, 6: 8192, 7: 12288}
    W_E_DN = {4: 32768, 5: 36864, 6: 16384, 7: 20480}
    W_SHUP = 24576
    F_OUTS = 40960

    stgb = arf[:, O_STGB // 2: O_STGB // 2 + 1024]
    stga = stgb
    SLOT = [arf[:, i * 1024:(i + 1) * 1024] for i in range(4)]
    wmf = arf[:, P_WMF // 2: P_WMF // 2 + 6144]
    wmb = arena[:, P_WMB:P_WMB + 6144]
    wqs = [arf[:, P_WQS // 2 + i * 1024: P_WQS // 2 + (i + 1) * 1024] for i in (0, 1)]
    hpre = [stgb[:, 0:1024]]

    psum = st.enter_context(nc.psum_tensor("psum", [128, 4096], F32))
    PB = lambda b: psum[:, b * 512:(b + 1) * 512]
    SC = [psum[:, 0:1024], psum[:, 1024:2048]]
    AVP = [psum[:, 2048:2560], psum[:, 2560:3072]]

    t_s = Sem(st, nc, "t_s")
    v_s = Sem(st, nc, "v_s")
    s_s = Sem(st, nc, "s_s")
    g_s = Sem(st, nc, "g_s")
    d_sy = Sem(st, nc, "d_sy")
    d_sc = Sem(st, nc, "d_sc")
    d_gp = Sem(st, nc, "d_gp")
    d_out = Sem(st, nc, "d_out")

    blk = st.enter_context(nc.Block())

    @blk.scalar
    def _(sc):
        d_sc.inc(sc.dma_start(out=sil_f[:, 0:8],
                              in_=c_h.rearrange("(j p) -> p j", p=128),
                              allow_slow_non_contiguous=True), 16)
        d_sc.inc(sc.dma_start(out=gb1[:], in_=gateb_h[None, :]), 16)
    ld_c = d_sc.n

    @blk.gpsimd
    def _(g):
        d_gp.inc(g.dma_start(out=x_res[:],
                             in_=x_h.rearrange("(t p) d -> p t d", p=128)), 16)
    ld_x = d_gp.n

    @blk.gpsimd
    def _(g):
        g_s.inc(g.memset(id_f[:], 0.0))
        g_s.inc(g.memset(id_b[:], 0.0))
        g_s.inc(g.memset(ones64[:], 1.0))
        g.wait_ge(g_s.h, g_s.n)
        g_s.inc(g.affine_select(out=id_f[:], in_=id_f[:], compare_op=ALU.not_equal,
                                fill=1.0, base=0, pattern=[[-1, 128]],
                                channel_multiplier=1))
        g_s.inc(g.affine_select(out=id_b[:], in_=id_b[:], compare_op=ALU.not_equal,
                                fill=1.0, base=0, pattern=[[-1, 128]],
                                channel_multiplier=1))
        g_s.inc(g.memset(onesrow[:], 1.0))
    g_const = g_s.n

    @blk.vector
    def _(v):
        v.wait_ge(d_sc.h, ld_c)
        v_s.inc(v.tensor_copy(gb1b[:], gb1[:]))
    gb_cast = v_s.n

    @blk.tensor
    def _(t):
        t.wait_ge(v_s.h, gb_cast)
        t.wait_ge(g_s.h, g_const)
        t_s.inc(t.matmul(PB(7)[:, 500:508], onesrow[:], gb1b[:],
                         start=True, stop=True))
    gb_mm = t_s.n

    @blk.vector
    def _(v):
        v.wait_ge(t_s.h, gb_mm)
        v_s.inc(v.tensor_copy(gbias[:], PB(7)[:, 500:508]))
    gb_done = v_s.n

    @blk.scalar
    def _(sc):
        sc.wait_ge(d_sc.h, ld_c)
        s_s.inc(sc.activation(sil_f[:, 8:16], sil_f[:, 0:8], AF.Silu))
    sil_done = s_s.n

    @blk.vector
    def _(v):
        v.wait_ge(s_s.h, sil_done)
        v_s.inc(v.tensor_copy(sil_c[:], sil_f[:, 8:16]))
    silc_done = v_s.n

    wm_dma, wm_cast, wm_mm = [], [], []
    for dc in range(DC):
        @blk.sync
        def _(sy, dc=dc):
            if dc >= 1:
                sy.wait_ge(v_s.h, wm_cast[dc - 1])
            d_sy.inc(sy.dma_start(out=wmf[:], in_=wmod_h[dc * 128:(dc + 1) * 128, :]), 16)
            wm_dma.append(d_sy.n)

        @blk.vector
        def _(v, dc=dc):
            v.wait_ge(d_sy.h, wm_dma[dc])
            if dc >= 1:
                v.wait_ge(t_s.h, wm_mm[dc - 1])
            v_s.inc(v.tensor_copy(wmb[:], wmf[:]))
            wm_cast.append(v_s.n)

        @blk.tensor
        def _(t, dc=dc):
            t.wait_ge(v_s.h, wm_cast[dc])
            if dc == 0:
                t.wait_ge(v_s.h, silc_done)
            last = None
            for ncx in range(48):
                last = t.matmul(psum[:, ncx * 8 + dc:ncx * 8 + dc + 1],
                                wmb[:, ncx * 128:(ncx + 1) * 128],
                                sil_c[:, dc:dc + 1],
                                start=True, stop=True)
            t_s.inc(last)
            wm_mm.append(t_s.n)
    mod_mm = wm_mm[-1]

    @blk.vector
    def _(v):
        v.wait_ge(t_s.h, mod_mm)
        v_s.inc(v.tensor_reduce(modT[:],
                                psum.ap()[:, 0:384].rearrange("p (n d) -> p n d", d=8),
                                axis=AX, op=ALU.add))
    modT_done = v_s.n

    @blk.vector
    def _(v):
        v.wait_ge(v_s.h, modT_done)
        v_s.inc(v.tensor_copy(coef[:, 0:8], modT[:, 0:8]))
        v_s.inc(v.tensor_scalar(coef[:, 8:16], modT[:, 8:16], 1.0, None, ALU.add))
        v_s.inc(v.tensor_copy(coef[:, 24:32], modT[:, 24:32]))
        v_s.inc(v.tensor_scalar(coef[:, 32:40], modT[:, 32:40], 1.0, None, ALU.add))
    pre_coef = v_s.n

    @blk.scalar
    def _(sc):
        sc.wait_ge(v_s.h, modT_done)
        s_s.inc(sc.activation(coef[:, 16:24], modT[:, 16:24], AF.Abs))
        s_s.inc(sc.activation(coef[:, 40:48], modT[:, 40:48], AF.Abs))
    gabs_done = s_s.n

    @blk.vector
    def _(v):
        v.wait_ge(s_s.h, gabs_done)
        for src in (16, 40):
            v_s.inc(v.tensor_scalar(stat6[:, 0:8], modT[:, src:src + 8], 0.0,
                                    None, ALU.is_ge))
            v.wait_ge(v_s.h, v_s.n)
            v_s.inc(v.tensor_scalar(stat6[:, 8:16], stat6[:, 0:8], 2.0, -1.0,
                                    ALU.mult, ALU.add))
            v_s.inc(v.tensor_scalar(coef[:, src:src + 8], coef[:, src:src + 8],
                                    0.25, None, ALU.max))
            v.wait_ge(v_s.h, v_s.n)
            v_s.inc(v.tensor_mul(coef[:, src:src + 8], coef[:, src:src + 8],
                                 stat6[:, 8:16]))
    coef_done = v_s.n

    def ln_stats(lnbase, xwait=None):
        for tc in range(TC):
            @blk.vector
            def _(v, tc=tc):
                if tc == 0 and xwait is not None:
                    v.wait_ge(*xwait)
                xs = x_res[:, tc * 1024:(tc + 1) * 1024]
                v_s.inc(v.bn_stats(stat6[:, 0:6], xs[:, 0:512]))
                v_s.inc(v.bn_stats(stat6[:, 6:12], xs[:, 512:1024]))
                v.wait_ge(v_s.h, v_s.n)
                v_s.inc(v.bn_aggr(stat6[:, 12:14], stat6[:, 0:12]))
                v.wait_ge(v_s.h, v_s.n)
                v_s.inc(v.tensor_scalar(stat6[:, 14:15], stat6[:, 13:14], LN_EPS,
                                        None, ALU.add))
                v.wait_ge(v_s.h, v_s.n)
                v_s.inc(v.reciprocal(stat6[:, 15:16], stat6[:, 14:15]))

            @blk.scalar
            def _(sc, tc=tc):
                sc.wait_ge(v_s.h, v_s.n)
                s_s.inc(sc.activation(lnst[:, lnbase + 2 * tc:lnbase + 2 * tc + 1],
                                      stat6[:, 15:16], AF.Sqrt))

            @blk.vector
            def _(v, tc=tc):
                v.wait_ge(s_s.h, s_s.n)
                v_s.inc(v.tensor_mul(stat6[:, 14:15], stat6[:, 12:13],
                                     lnst[:, lnbase + 2 * tc:lnbase + 2 * tc + 1]))
                v.wait_ge(v_s.h, v_s.n)
                v_s.inc(v.tensor_scalar(
                    lnst[:, lnbase + 2 * tc + 1:lnbase + 2 * tc + 2],
                    stat6[:, 14:15], -1.0, None, ALU.mult))

    def ln_apply(lnbase, cA, cB, first_wait=()):
        tr_done, ev_done = [], []
        for tc in range(TC):
            slot = 0
            @blk.vector
            def _(v, tc=tc, slot=slot):
                if tc == 0:
                    v.wait_ge(v_s.h, coef_done)
                    for w in first_wait:
                        v.wait_ge(*w)
                if tc >= 1:
                    v.wait_ge(t_s.h, tr_done[tc - 1])
                rs = lnst[:, lnbase + 2 * tc:lnbase + 2 * tc + 1]
                nm = lnst[:, lnbase + 2 * tc + 1:lnbase + 2 * tc + 2]
                v_s.inc(v.tensor_scalar(hpre[slot],
                                        x_res[:, tc * 1024:(tc + 1) * 1024],
                                        rs, nm, ALU.mult, ALU.add))
            ap_done = v_s.n

            @blk.tensor
            def _(t, tc=tc, slot=slot, ap_done=ap_done):
                t.wait_ge(v_s.h, ap_done)
                if tc >= 2:
                    t.wait_ge(v_s.h, ev_done[tc - 2])
                last = None
                for dc in range(DC):
                    bank = 2 + 2 * (tc % 2) + (dc // 4)
                    last = t.transpose(
                        PB(bank)[:, (dc % 4) * 128:(dc % 4) * 128 + 128],
                        hpre[slot][:, dc * 128:(dc + 1) * 128], id_f[:])
                t_s.inc(last)
                tr_done.append(t_s.n)

            @blk.vector
            def _(v, tc=tc):
                v.wait_ge(t_s.h, tr_done[tc])
                last = None
                for dc in range(DC):
                    bank = 2 + 2 * (tc % 2) + (dc // 4)
                    last = v.tensor_scalar(
                        hT[:, dc * T + tc * 128: dc * T + (tc + 1) * 128],
                        PB(bank)[:, (dc % 4) * 128:(dc % 4) * 128 + 128],
                        coef[:, cA + dc:cA + dc + 1],
                        coef[:, cB + dc:cB + dc + 1],
                        ALU.mult, ALU.add)
                v_s.inc(last)
                ev_done.append(v_s.n)
        return ev_done[-1]

    ln_stats(0, xwait=(d_gp.h, ld_x))

    wq_dma, wq_tr, wq_ev = [], [], []
    for ch in range(24):
        slot = ch % 2
        @blk.sync
        def _(sy, ch=ch, slot=slot):
            if ch >= 2:
                sy.wait_ge(t_s.h, wq_tr[ch - 2])
            d_sy.inc(sy.dma_start(out=wqs[slot],
                                  in_=wqkv_h[ch * 128:(ch + 1) * 128, :]), 16)
            wq_dma.append(d_sy.n)

        @blk.tensor
        def _(t, ch=ch, slot=slot):
            t.wait_ge(d_sy.h, wq_dma[ch])
            if ch >= 1:
                t.wait_ge(v_s.h, wq_ev[ch - 1])
            last = None
            for dc in range(DC):
                bank = 6 + (dc // 4)
                last = t.transpose(PB(bank)[:, (dc % 4) * 128:(dc % 4) * 128 + 128],
                                   wqs[slot][:, dc * 128:(dc + 1) * 128], id_f[:])
            t_s.inc(last)
            wq_tr.append(t_s.n)

        @blk.vector
        def _(v, ch=ch):
            v.wait_ge(t_s.h, wq_tr[ch])
            last = None
            for dc in range(DC):
                bank = 6 + (dc // 4)
                last = v.tensor_copy(
                    arena[:, O_WQKVT + dc * 3072 + ch * 128:
                          O_WQKVT + dc * 3072 + (ch + 1) * 128],
                    PB(bank)[:, (dc % 4) * 128:(dc % 4) * 128 + 128])
            v_s.inc(last)
            wq_ev.append(v_s.n)
    wqkvT_done = v_s.n

    ln1_done = ln_apply(0, 8, 0)

    if "hT" in dumps:
        h = mkdump("hT", [128, DC * T], BF16)
        @blk.sync
        def _(sy):
            sy.wait_ge(v_s.h, ln1_done)
            d_sy.inc(sy.dma_start(out=h, in_=hT[:]), 16)

    @blk.gpsimd
    def _(g):
        g.wait_ge(t_s.h, wq_tr[-1])
        g.wait_ge(t_s.h, mod_mm)
        ap = arena.ap()[:, O_V65:O_V65 + 2 * TC * 130]
        g_s.inc(g.memset(ap.rearrange("p (b t h j) -> p b t h j", b=2, t=TC, h=2)
                         [:, :, :, :, 64:65], 1.0))
        g_s.inc(g.memset(arena[64:65, 24576 + 2 * T:24576 + 4 * T], 1.0))
        g_s.inc(g.memset(arena[0:16, O_IND:O_IND + 1024], 0.0))
        g.wait_ge(g_s.h, g_s.n)
        g_s.inc(g.affine_select(
            out=arena.ap()[0:16, O_IND:O_IND + 1024].rearrange(
                "p (a h j) -> p a h j", h=2, j=64),
            in_=arena.ap()[0:16, O_IND:O_IND + 1024].rearrange(
                "p (a h j) -> p a h j", h=2, j=64),
            compare_op=ALU.not_equal, fill=1.0, base=0,
            pattern=[[-2, 8], [-1, 2], [0, 64]], channel_multiplier=1))
    v65_init = g_s.n

    QB = [O_QK65 + 0 * T, O_QK65 + 1 * T]
    KB = [O_QK65 + 2 * T, O_QK65 + 3 * T]
    QC2 = max(1, NQC // 2)
    NB = 2 * QC2
    def qcs_of(b):
        qc2 = b % QC2
        return [q for q in (2 * qc2, 2 * qc2 + 1) if q < NQC]

    sqbf = arena[:, O_STGB:O_STGB + T]
    mqb = arena[:, O_MQR:O_MQR + 16]
    mqrow = arena[0:16, O_MQR + 128:O_MQR + 256]
    hstg = arena[:, O_HSTG:O_HSTG + 4096]
    qnf = rtr[:, 320:352]
    PT = [arena[:, O_PT:O_PT + 1024], arena[:, O_PT + 1024:O_PT + 2048]]
    sums_bf = arena[0:16, O_SUM:O_SUM + T]
    rcp_bf = arena[0:16, O_RCP:O_RCP + T]
    sstg = arena[:, O_SSTG:O_SSTG + 1024]
    h1s = [arena[0:64, O_H1S:O_H1S + 512], arena[0:64, O_H1S + 512:O_H1S + 1024]]
    h1_dma_all = [0, 0]
    sums_dma_all = [0, 0]
    SUMSCALE = 2097152.0

    def plan_qkv(p, evac_pred, hend_ref, odd_dma):
        tcnt = {}
        seq = ([('v', tc) for tc in range(TC)]
               + [('q', n) for n in range(NQC)] + [('k', n) for n in range(NQC)])
        jobs_t, jobs_v = [], []
        v_at_plan = v_s.n
        for si, (kind, idx) in enumerate(seq):
            bank = 6 + (si % 2)
            def tjob(t, kind=kind, idx=idx, si=si, bank=bank):
                if si >= 2:
                    t.wait_ge(v_s.h, evac_pred[si - 2])
                else:
                    t.wait_ge(v_s.h, v_at_plan)
                last = None
                if kind in ('q', 'k'):
                    ebase = (0 if kind == 'q' else 1024) + p * 128
                    for dc in range(DC):
                        last = t.matmul(
                            PB(bank),
                            arena[:, dc * 3072 + ebase: dc * 3072 + ebase + 128],
                            hT[:, dc * T + idx * 512: dc * T + (idx + 1) * 512],
                            start=(dc == 0), stop=(dc == DC - 1))
                else:
                    ebase = 2048 + p * 128
                    for dc in range(DC):
                        last = t.matmul(
                            PB(bank)[:, 0:128],
                            hT[:, dc * T + idx * 128: dc * T + (idx + 1) * 128],
                            arena[:, dc * 3072 + ebase: dc * 3072 + ebase + 128],
                            start=(dc == 0), stop=(dc == DC - 1))
                t_s.inc(last)
                tcnt[si] = t_s.n
            def vjob(v, kind=kind, idx=idx, si=si, bank=bank):
                v.wait_ge(t_s.h, tcnt[si])
                if kind == 'v':
                    dst = arena.ap()[:, O_V65 + (p % 2) * TC * 130 + idx * 130:
                                     O_V65 + (p % 2) * TC * 130 + (idx + 1) * 130]
                    v_s.inc(v.tensor_copy(
                        dst.rearrange("p (h j) -> p h j", h=2, j=65)[:, :, 0:64],
                        PB(bank)[:, 0:128].rearrange("p (h j) -> p h j", h=2, j=64)))
                else:
                    base = QB[0] if kind == 'q' else KB[0]
                    hoff = 0 if kind == 'q' else 2048
                    v.wait_ge(t_s.h, hend_ref['h0'])
                    v_s.inc(v.tensor_copy(
                        arena[0:64, base + idx * 512: base + (idx + 1) * 512],
                        PB(bank)[0:64, :]))
                    v.wait_ge(d_sy.h, odd_dma)
                    v_s.inc(v.tensor_copy(
                        hstg[64:128, hoff + idx * 512: hoff + (idx + 1) * 512],
                        PB(bank)[64:128, :]))
            jobs_t.append(tjob)
            jobs_v.append(vjob)
        nops = [1 if kind == 'v' else 2 for (kind, idx) in seq]
        return jobs_t, jobs_v, nops

    def emit_odd_dmas(pair_end_t):
        @blk.sync
        def _(sy):
            sy.wait_ge(t_s.h, pair_end_t)
            sy.wait_ge(v_s.h, v_s.n)
            d_sy.inc(sy.dma_start(out=arena[0:64, QB[1]:QB[1] + T],
                                  in_=hstg[64:128, 0:T]), 16)
            d_sy.inc(sy.dma_start(out=arena[0:64, KB[1]:KB[1] + T],
                                  in_=hstg[64:128, 2048:2048 + T]), 16)
        return d_sy.n

    def emit_norms(p, h, data_wait):
        res = {}
        @blk.vector
        def _(v):
            v.wait_ge(*data_wait)
            v_s.inc(v.tensor_mul(sqbf[0:64, :], arena[0:64, QB[h]:QB[h] + T],
                                 arena[0:64, QB[h]:QB[h] + T]))
        res['qsq'] = v_s.n
        @blk.tensor
        def _(t):
            t.wait_ge(v_s.h, res['qsq'])
            last = None
            for tcx in range(TC):
                last = t.matmul(PB(7)[:, tcx:tcx + 1],
                                sqbf[0:64, tcx * 128:(tcx + 1) * 128],
                                ones64[0:64, :], start=True, stop=True)
            t_s.inc(last)
        res['nq'] = t_s.n
        @blk.vector
        def _(v):
            v.wait_ge(t_s.h, res['nq'])
            v_s.inc(v.tensor_mul(sqbf[0:64, :], arena[0:64, KB[h]:KB[h] + T],
                                 arena[0:64, KB[h]:KB[h] + T]))
        res['ksq'] = v_s.n
        @blk.tensor
        def _(t):
            t.wait_ge(v_s.h, res['ksq'])
            last = None
            for tcx in range(TC):
                last = t.matmul(PB(7)[:, TC + tcx:TC + tcx + 1],
                                sqbf[0:64, tcx * 128:(tcx + 1) * 128],
                                ones64[0:64, :], start=True, stop=True)
            t_s.inc(last)
        res['nk'] = t_s.n
        @blk.vector
        def _(v):
            v.wait_ge(t_s.h, res['nk'])
            v_s.inc(v.reduce_max(mtmp[:, 0:1], PB(7)[:, TC:2 * TC], axis=AX))
        res['kred'] = v_s.n
        @blk.sync
        def _(sy):
            sy.wait_ge(v_s.h, res['kred'])
            d_sy.inc(sy.dma_start(out=mcol[0:1, 0:128], in_=mtmp[:, 0:1],
                                  allow_slow_non_contiguous=True), 16)
        res['coll'] = d_sy.n
        @blk.vector
        def _(v):
            v.wait_ge(d_sy.h, res['coll'])
            v_s.inc(v.reduce_max(mrow[0:1, 0:1], mcol[0:1, 0:128], axis=AX))
            v.wait_ge(v_s.h, v_s.n)
            v_s.inc(v.tensor_copy(mrowb[:], mrow[0:1, 0:1]))
        res['mkb'] = v_s.n
        @blk.tensor
        def _(t):
            t.wait_ge(v_s.h, res['mkb'])
            t_s.inc(t.matmul(PB(7)[:, 40:41], onesrow[:], mrowb[:],
                             start=True, stop=True))
        res['bc'] = t_s.n
        @blk.vector
        def _(v):
            v.wait_ge(t_s.h, res['bc'])
            v_s.inc(v.tensor_copy(mtmp[:, 4:5], PB(7)[:, 40:41]))
            v_s.inc(v.tensor_copy(qnf[:, 0:TC], PB(7)[:, 0:TC]))
        res['prep'] = v_s.n
        @blk.scalar
        def _(sc):
            sc.wait_ge(v_s.h, res['prep'])
            s_s.inc(sc.activation(qnf[:, 16:16 + TC], qnf[:, 0:TC], AF.Ln,
                                  scale=mtmp[:, 4:5]))
            s_s.inc(sc.activation(qnf[:, 16:16 + TC], qnf[:, 16:16 + TC],
                                  AF.Exp, scale=0.5))
        res['mq'] = s_s.n
        @blk.vector
        def _(v):
            v.wait_ge(s_s.h, res['mq'])
            v_s.inc(v.tensor_scalar(mqb[:, 0:TC], qnf[:, 16:16 + TC], -1.0,
                                    None, ALU.mult))
        res['neg'] = v_s.n
        @blk.tensor
        def _(t):
            t.wait_ge(v_s.h, res['neg'])
            t_s.inc(t.transpose(PB(7).bitcast(BF16)[0:TC, 256:384],
                                mqb[:, 0:TC], id_b[:]))
        res['tr'] = t_s.n
        @blk.vector
        def _(v):
            v.wait_ge(t_s.h, res['tr'])
            v_s.inc(v.tensor_copy(mqrow[0:TC, :], PB(7).bitcast(BF16)[0:TC, 256:384]))
        res['row'] = v_s.n
        @blk.sync
        def _(sy):
            sy.wait_ge(v_s.h, res['row'])
            d_sy.inc(sy.dma_start(out=arena[64:65, QB[h]:QB[h] + T],
                                  in_=mqrow[0:TC, :]), 16)
        res['dma'] = d_sy.n
        return res['dma']

    pre_base = v_s.n
    nseq = TC + 2 * NQC
    nops_all = [1 if i < TC else 2 for i in range(nseq)]
    odd_prev = [d_sy.n]
    jt0, jv0, nops0 = plan_qkv(0, [pre_base + sum(nops_all[:i + 1])
                                   for i in range(nseq)],
                               {'h0': 0}, d_sy.n)
    @blk.tensor
    def _(t):
        t.wait_ge(v_s.h, wqkvT_done)
        t.wait_ge(v_s.h, ln1_done)
        t.wait_ge(g_s.h, v65_init)
        for j in jt0:
            j(t)
    @blk.vector
    def _(v):
        v.wait_ge(g_s.h, v65_init)
        for j in jv0:
            j(v)
    odd0 = emit_odd_dmas(t_s.n)
    nd0 = emit_norms(0, 0, (v_s.h, v_s.n))
    nd1 = emit_norms(0, 1, (d_sy.h, odd0))
    pair_ready = {0: (t_s.n, v_s.n, s_s.n, nd1)}

    if "qkv0" in dumps:
        h = mkdump("qkv0", [128, 4 * T + TC * 130], BF16)
        @blk.sync
        def _(sy):
            sy.wait_ge(v_s.h, pair_ready[0][1])
            sy.wait_ge(d_sy.h, d_sy.n)
            d_sy.inc(sy.dma_start(out=h[:, 0:4 * T],
                                  in_=arena[:, O_QK65:O_QK65 + 4 * T]), 16)
            d_sy.inc(sy.dma_start(out=h[:, 4 * T:],
                                  in_=arena[:, O_V65:O_V65 + TC * 130]), 16)

    for p in range(NPAIR):
        t0, v0, s0, mdma = pair_ready[p]
        NG2 = NB * KC

        njobs = nseq if p + 1 < NPAIR else 0
        per_b = [[] for _ in range(NB)]
        if njobs:
            for i in range(TC):
                per_b[min(i * QC2 // TC, QC2 - 1)].append(i)
            for i in range(TC, nseq):
                per_b[QC2 + min((i - TC) * QC2 // (2 * NQC), QC2 - 1)].append(i)

        v_base = v_s.n
        nops = nops_all
        evac_done = []
        job_pred = {}
        acc = v_base
        for b in range(NB):
            acc += 2 * len(qcs_of(b))
            evac_done.append(acc)
            for i in per_b[b]:
                job_pred[i] = acc + sum(nops[i2] for i2 in per_b[b]
                                        if i2 <= i)
            acc += sum(nops[i2] for i2 in per_b[b])
        s_base = s_s.n
        hend = {'h0': 0}

        if p + 1 < NPAIR:
            jobs_t, jobs_v, _ = plan_qkv(
                p + 1, [job_pred[i] for i in range(njobs)],
                hend, odd_prev[0])
        else:
            jobs_t, jobs_v = [], []

        qk_cnt, av_cnt = {}, {}

        def emit_av(t, b, kc, h):
            g2 = b * KC + kc
            par = g2 % 2
            qcs = qcs_of(b)
            t.wait_ge(s_s.h, s_base + g2 + 1)
            if kc == 0 and b >= 1:
                t.wait_ge(v_s.h, evac_done[b - 1])
            vb = O_V65 + (p % 2) * TC * 130
            last = None
            for j, qc in enumerate(qcs):
                last = t.matmul(
                    AVP[j][0:65, :],
                    arena[:, vb + kc * 130 + h * 65: vb + kc * 130 + h * 65 + 65],
                    PT[par][:, j * 512:(j + 1) * 512],
                    start=(kc == 0), stop=(kc == KC - 1))
            av_cnt[g2] = t_s.inc(last)

        @blk.tensor
        def _(t, p=p):
            t.wait_ge(v_s.h, v0)
            t.wait_ge(d_sy.h, mdma)
            for b in range(NB):
                h = b // QC2
                qcs = qcs_of(b)
                for kc in range(KC):
                    g2 = b * KC + kc
                    par = g2 % 2
                    if g2 >= 2:
                        t.wait_ge(s_s.h, s_base + (g2 - 2) + 1)
                    elif p > 0:
                        t.wait_ge(s_s.h, s_base)
                    last = None
                    for j, qc in enumerate(qcs):
                        last = t.matmul(
                            SC[par][:, j * 512:(j + 1) * 512],
                            arena[0:65, KB[h] + kc * 128: KB[h] + (kc + 1) * 128],
                            arena[0:65, QB[h] + qc * 512: QB[h] + (qc + 1) * 512],
                            start=True, stop=True)
                    qk_cnt[g2] = t_s.inc(last)
                    if kc >= 1:
                        emit_av(t, b, kc - 1, h)
                emit_av(t, b, KC - 1, h)
                if b == QC2 - 1:
                    hend['h0'] = t_s.n
                if b == NB - 1:
                    hend['h1'] = t_s.n
                for i in per_b[b]:
                    jobs_t[i](t)

        @blk.scalar
        def _(sc, p=p):
            sc.wait_ge(t_s.h, t0)
            for b in range(NB):
                w = 512 * len(qcs_of(b))
                for kc in range(KC):
                    g2 = b * KC + kc
                    par = g2 % 2
                    sc.wait_ge(t_s.h, qk_cnt[g2])
                    if g2 >= 2:
                        sc.wait_ge(t_s.h, av_cnt[g2 - 2])
                    s_s.inc(sc.activation(PT[par][:, 0:w], SC[par][:, 0:w],
                                          AF.Exp, scale=0.125))
        assert s_s.n == s_base + NG2

        for b in range(NB):
            h = b // QC2
            @blk.vector
            def _(v, p=p, b=b, h=h):
                v.wait_ge(t_s.h, av_cnt[b * KC + KC - 1])
                for j, qc in enumerate(qcs_of(b)):
                    gi = 2 * p + h
                    if h == 0:
                        v_s.inc(v.tensor_copy(
                            aun[0:64, p * T + qc * 512: p * T + (qc + 1) * 512],
                            AVP[j][0:64, :]))
                    else:
                        v.wait_ge(d_sy.h, h1_dma_all[-2])
                        v_s.inc(v.tensor_copy(h1s[qc % 2], AVP[j][0:64, :]))
                    v.wait_ge(d_sy.h, sums_dma_all[-2])
                    v_s.inc(v.tensor_scalar(
                        sstg[64:65, (qc % 2) * 512:(qc % 2) * 512 + 512],
                        AVP[j][64:65, :], SUMSCALE, None, ALU.mult))
                for i in per_b[b]:
                    jobs_v[i](v)
            assert v_s.n == evac_done[b] + sum(nops[i2] for i2 in per_b[b])

            @blk.sync
            def _(sy, p=p, b=b, h=h):
                sy.wait_ge(v_s.h, v_s.n)
                for qc in qcs_of(b):
                    if h == 1:
                        d_sy.inc(sy.dma_start(
                            out=aun[64:128, p * T + qc * 512: p * T + (qc + 1) * 512],
                            in_=h1s[qc % 2]), 16)
                        h1_dma_all.append(d_sy.n)
                    d_sy.inc(sy.dma_start(
                        out=sums_bf[2 * p + h: 2 * p + h + 1, qc * 512:(qc + 1) * 512],
                        in_=sstg[64:65, (qc % 2) * 512:(qc % 2) * 512 + 512]), 16)
                    sums_dma_all.append(d_sy.n)

        if p + 1 < NPAIR:
            odd_n = emit_odd_dmas(hend['h1'])
            odd_prev[0] = odd_n
            ndh0 = emit_norms(p + 1, 0, (v_s.h, v_s.n))
            ndh1 = emit_norms(p + 1, 1, (d_sy.h, odd_n))
            pair_ready[p + 1] = (t_s.n, v_s.n, s_s.n, ndh1)

    attn_t, attn_v = t_s.n, v_s.n

    if "aun" in dumps:
        h = mkdump("aun", [128, NPAIR * T], BF16)
        @blk.sync
        def _(sy):
            sy.wait_ge(v_s.h, attn_v)
            d_sy.inc(sy.dma_start(out=h, in_=aun[:]), 16)
    if "sums" in dumps:
        h = mkdump("sums", [16, T], BF16)
        @blk.sync
        def _(sy):
            sy.wait_ge(d_sy.h, d_sy.n)
            d_sy.inc(sy.dma_start(out=h, in_=sums_bf[:]), 16)

    @blk.vector
    def _(v):
        v.wait_ge(d_sy.h, d_sy.n)
        with nc.allow_low_precision(reason="softmax denominators in bf16; error budget analyzed"):
            v_s.inc(v.reciprocal(rcp_bf[:], sums_bf[:]))
    rcp_done = v_s.n

    rs_pred = v_s.n
    rs_t = {}
    @blk.tensor
    def _(t):
        t.wait_ge(v_s.h, rcp_done)
        for p in range(NPAIR):
            for qc in range(NQC):
                i = p * NQC + qc
                bank = 4 + (i % 2)
                if i >= 2:
                    t.wait_ge(v_s.h, rs_pred + (i - 2) + 1)
                rs_t[i] = t_s.inc(t.matmul(
                    PB(bank), arena[0:16, O_IND + p * 128: O_IND + (p + 1) * 128],
                    rcp_bf[:, qc * 512:(qc + 1) * 512], start=True, stop=True))

    @blk.vector
    def _(v):
        for p in range(NPAIR):
            for qc in range(NQC):
                i = p * NQC + qc
                bank = 4 + (i % 2)
                v.wait_ge(t_s.h, rs_t[i])
                sl = aun[:, p * T + qc * 512: p * T + (qc + 1) * 512]
                v_s.inc(v.scalar_tensor_tensor(sl, PB(bank), SUMSCALE, sl,
                                               ALU.mult, ALU.mult))
    rescale_done = v_s.n
    assert rescale_done == rs_pred + NPAIR * NQC

    wo_dma, wo_sc, wo_tr, wo_ev = [], [], [], []
    for ch in range(8):
        @blk.sync
        def _(sy, ch=ch):
            if ch == 0:
                sy.wait_ge(t_s.h, attn_t)
            else:
                sy.wait_ge(t_s.h, wo_tr[ch - 1])
            d_sy.inc(sy.dma_start(out=stga, in_=wo_h[ch * 128:(ch + 1) * 128, :]), 16)
            wo_dma.append(d_sy.n)
        @blk.vector
        def _(v, ch=ch):
            v.wait_ge(d_sy.h, wo_dma[ch])
            v_s.inc(v.tensor_scalar(stga, stga, coef[:, 16 + ch:16 + ch + 1],
                                    None, ALU.mult))
            wo_sc.append(v_s.n)
        @blk.tensor
        def _(t, ch=ch):
            t.wait_ge(v_s.h, wo_sc[ch])
            if ch >= 1:
                t.wait_ge(v_s.h, wo_ev[ch - 1])
            last = None
            for dc in range(DC):
                bank = 6 + (dc // 4)
                last = t.transpose(PB(bank)[:, (dc % 4) * 128:(dc % 4) * 128 + 128],
                                   stga[:, dc * 128:(dc + 1) * 128], id_f[:])
            t_s.inc(last)
            wo_tr.append(t_s.n)
        @blk.vector
        def _(v, ch=ch):
            v.wait_ge(t_s.h, wo_tr[ch])
            last = None
            for dc in range(DC):
                bank = 6 + (dc // 4)
                last = v.tensor_copy(
                    arena[:, O_WOT + dc * 1024 + ch * 128:
                          O_WOT + dc * 1024 + (ch + 1) * 128],
                    PB(bank)[:, (dc % 4) * 128:(dc % 4) * 128 + 128])
            v_s.inc(last)
            wo_ev.append(v_s.n)
    wot_done = v_s.n

    op_pred = v_s.n
    op_t = {}
    @blk.tensor
    def _(t):
        t.wait_ge(v_s.h, rescale_done)
        t.wait_ge(v_s.h, wot_done)
        for tc in range(TC):
            for ec in range(2):
                i = tc * 2 + ec
                bank = i % 2
                if i >= 2:
                    t.wait_ge(v_s.h, op_pred + (i - 2) + 1)
                last = None
                for pc in range(NPAIR):
                    last = t.matmul(PB(bank),
                                    aun[:, pc * T + tc * 128: pc * T + (tc + 1) * 128],
                                    arena[:, O_WOT + pc * 1024 + ec * 512:
                                          O_WOT + pc * 1024 + (ec + 1) * 512],
                                    start=(pc == 0), stop=(pc == NPAIR - 1))
                op_t[i] = t_s.inc(last)

    @blk.vector
    def _(v):
        for tc in range(TC):
            for ec in range(2):
                i = tc * 2 + ec
                bank = i % 2
                v.wait_ge(t_s.h, op_t[i])
                sl = x_res[:, tc * 1024 + ec * 512: tc * 1024 + (ec + 1) * 512]
                v_s.inc(v.tensor_add(sl, sl, PB(bank)))
    x1_done = v_s.n

    if "x1" in dumps:
        h = mkdump("x1", [128, TC * 1024], BF16)
        @blk.sync
        def _(sy):
            sy.wait_ge(v_s.h, x1_done)
            d_sy.inc(sy.dma_start(out=h, in_=x_res[:]), 16)

    ln_stats(2 * TC, xwait=(v_s.h, x1_done))
    ln2_done = ln_apply(2 * TC, 32, 24, first_wait=[(v_s.h, x1_done)])

    if "h2T" in dumps:
        h = mkdump("h2T", [128, DC * T], BF16)
        @blk.sync
        def _(sy):
            sy.wait_ge(v_s.h, ln2_done)
            d_sy.inc(sy.dma_start(out=h, in_=hT[:]), 16)

    @blk.sync
    def _(sy):
        for dcx in range(DC):
            d_sy.inc(sy.dma_start(
                out=gwf[:, dcx * 8:(dcx + 1) * 8],
                in_=gatew_h[:, dcx * 128:(dcx + 1) * 128].rearrange("e p -> p e"),
                allow_slow_non_contiguous=True), 16)
    gw_dma = d_sy.n

    @blk.vector
    def _(v):
        v.wait_ge(d_sy.h, gw_dma)
        v_s.inc(v.tensor_copy(gwt[:], gwf[:]))
    gwt_done = v_s.n

    @blk.tensor
    def _(t):
        t.wait_ge(v_s.h, gwt_done)
        t.wait_ge(v_s.h, ln2_done)
        last = None
        for tc in range(TC):
            for dc in range(DC):
                last = t.matmul(PB(4)[:, tc * 8:(tc + 1) * 8],
                                hT[:, dc * T + tc * 128: dc * T + (tc + 1) * 128],
                                gwt[:, dc * 8:(dc + 1) * 8],
                                start=(dc == 0), stop=(dc == DC - 1))
        t_s.inc(last)
    lg_done = t_s.n

    @blk.scalar
    def _(sc):
        sc.wait_ge(t_s.h, lg_done)
        s_s.inc(sc.activation(rtr[:, 0:8 * TC], PB(4)[:, 0:8 * TC], AF.Sigmoid))
    sig_done = s_s.n

    @blk.vector
    def _(v):
        v.wait_ge(s_s.h, sig_done)
        v.wait_ge(g_s.h, g_const)
        SCO, SFC, TMP = 0, 128, 256
        for tc in range(TC):
            sco = rtr[:, SCO + tc * 8: SCO + tc * 8 + 8]
            sfc = rtr[:, SFC + tc * 8: SFC + tc * 8 + 8]
            tmp = rtr[:, TMP:TMP + 16]
            v_s.inc(v.tensor_add(sfc, sco, gbias[:]))
            v.wait_ge(v_s.h, v_s.n)
            t2 = rtr.ap()[:, TMP:TMP + 16].rearrange("p (g k) -> p g k", g=2, k=8)
            s2 = rtr.ap()[:, SFC + tc * 8: SFC + tc * 8 + 8].rearrange(
                "p (g k) -> p g k", g=2, k=4)
            pr = [(0, 1), (0, 2), (0, 3), (1, 2), (1, 3), (2, 3)]
            for j, (a, b) in enumerate(pr):
                v_s.inc(v.tensor_add(t2[:, :, j:j + 1], s2[:, :, a:a + 1],
                                     s2[:, :, b:b + 1]))
            v.wait_ge(v_s.h, v_s.n)
            for j in (1, 2, 3, 4, 5):
                v_s.inc(v.tensor_max(t2[:, :, 0:1], t2[:, :, 0:1], t2[:, :, j:j + 1]))
                v.wait_ge(v_s.h, v_s.n)
            v_s.inc(v.tensor_tensor(tmp[:, 12:13], tmp[:, 0:1], tmp[:, 8:9],
                                    ALU.is_ge))
            v.wait_ge(v_s.h, v_s.n)
            v_s.inc(v.tensor_scalar(tmp[:, 13:14], tmp[:, 12:13], -1.0, 1.0,
                                    ALU.mult, ALU.add))
            msk = rtr[:, TMP + 16:TMP + 24]
            v.wait_ge(v_s.h, v_s.n)
            v_s.inc(v.tensor_scalar(msk[:, 0:4], sfc[:, 0:4], tmp[:, 12:13],
                                    None, ALU.mult))
            v_s.inc(v.tensor_scalar(msk[:, 4:8], sfc[:, 4:8], tmp[:, 13:14],
                                    None, ALU.mult))
            v.wait_ge(v_s.h, v_s.n)
            v_s.inc(v.reduce_max(tmp[:, 14:15], msk, axis=AX))
            v.wait_ge(v_s.h, v_s.n)
            ge1 = rtr[:, TMP + 24:TMP + 32]
            v_s.inc(v.tensor_scalar(ge1, msk, tmp[:, 14:15], None, ALU.is_ge))
            v.wait_ge(v_s.h, v_s.n)
            v_s.inc(v.scalar_tensor_tensor(ge1, ge1, -3.0, msk, ALU.mult, ALU.add))
            v.wait_ge(v_s.h, v_s.n)
            v_s.inc(v.reduce_max(tmp[:, 15:16], ge1, axis=AX))
            v.wait_ge(v_s.h, v_s.n)
            sel = rtr[:, TMP + 32:TMP + 40]
            v_s.inc(v.tensor_scalar(sel, msk, tmp[:, 15:16], None, ALU.is_ge))
            v.wait_ge(v_s.h, v_s.n)
            we = rtr[:, TMP + 40:TMP + 48]
            v_s.inc(v.tensor_mul(we, sco, sel))
            v.wait_ge(v_s.h, v_s.n)
            v_s.inc(v.reduce_sum(tmp[:, 14:15], we, axis=AX))
            v.wait_ge(v_s.h, v_s.n)
            v_s.inc(v.tensor_scalar(tmp[:, 14:15], tmp[:, 14:15], 1e-20, None,
                                    ALU.add))
            v.wait_ge(v_s.h, v_s.n)
            v_s.inc(v.reciprocal(tmp[:, 15:16], tmp[:, 14:15]))
            v.wait_ge(v_s.h, v_s.n)
            v_s.inc(v.tensor_scalar(cwsq[:, tc * 8:(tc + 1) * 8], we,
                                    tmp[:, 15:16], 2.5, ALU.mult, ALU.mult))
    cw_done = v_s.n

    if "cw" in dumps:
        h = mkdump("cw", [128, 8 * TC])
        @blk.sync
        def _(sy):
            sy.wait_ge(v_s.h, cw_done)
            d_sy.inc(sy.dma_start(out=h, in_=cwsq[:]), 16)

    @blk.scalar
    def _(sc):
        sc.wait_ge(v_s.h, cw_done)
        s_s.inc(sc.activation(cwsq[:], cwsq[:], AF.Sqrt))
    sqcw_done = s_s.n

    prep_state = dict(j=0, g=0, dmac={}, scc={}, gtr={}, jtr={}, evc={})

    def prep(jobs, first_sync_wait=None, first_tensor_wait=None):
        dmac, scc, gtr, jtr, evc = (prep_state['dmac'], prep_state['scc'],
                                    prep_state['gtr'], prep_state['jtr'],
                                    prep_state['evc'])
        for job in jobs:
            j = prep_state['j']
            slot = SLOT[j % 4]
            @blk.sync
            def _(sy, j=j, job=job, slot=slot):
                if j == 0 and first_sync_wait is not None:
                    sy.wait_ge(*first_sync_wait)
                if job.get('gate') is not None:
                    sy.wait_ge(*job['gate'])
                if j >= 4:
                    sy.wait_ge(t_s.h, jtr[j - 4])
                d_sy.inc(sy.dma_start(out=slot[:, 0:job['width']], in_=job['src']), 16)
                dmac[j] = d_sy.n
            if job.get('scales'):
                @blk.vector
                def _(v, j=j, job=job, slot=slot):
                    v.wait_ge(d_sy.h, dmac[j])
                    last = None
                    for (lo, hi, cc) in job['scales']:
                        last = v.tensor_scalar(slot[:, lo:hi], slot[:, lo:hi],
                                               coef[:, cc:cc + 1], None, ALU.mult)
                    v_s.inc(last)
                    scc[j] = v_s.n
            else:
                scc[j] = None
            for (slo, dlo, dstr) in job['groups']:
                g = prep_state['g']
                bank = 6 + (g % 2)
                @blk.tensor
                def _(t, j=j, g=g, bank=bank, slo=slo, slot=slot):
                    if g == 0 and first_tensor_wait is not None:
                        t.wait_ge(*first_tensor_wait)
                    if scc[j] is not None:
                        t.wait_ge(v_s.h, scc[j])
                    else:
                        t.wait_ge(d_sy.h, dmac[j])
                    if g >= 2:
                        t.wait_ge(v_s.h, evc[g - 2])
                    last = None
                    for k in range(4):
                        last = t.transpose(PB(bank)[:, k * 128:(k + 1) * 128],
                                           slot[:, slo + k * 128: slo + (k + 1) * 128],
                                           id_f[:])
                    t_s.inc(last)
                    gtr[g] = t_s.n
                @blk.vector
                def _(v, j=j, g=g, bank=bank, dlo=dlo, dstr=dstr, job=job):
                    v.wait_ge(t_s.h, gtr[g])
                    dstt = arena if job['dst'] == 'arena' else aun
                    full = dstt.ap().rearrange("p (a c) -> p a c", c=dstr)
                    a0, c0 = dlo // dstr, dlo % dstr
                    dv = full[:, a0:a0 + 4, c0:c0 + 128]
                    sv = PB(bank).rearrange("p (g c) -> p g c", c=128)
                    v_s.inc(v.tensor_copy(dv, sv))
                    evc[g] = v_s.n
                prep_state['g'] = g + 1
            jtr[j] = gtr[prep_state['g'] - 1]
            prep_state['j'] = j + 1
        return prep_state['j'] - 1

    def shup_jobs(gate=None):
        return [dict(src=shup_h[ch * 128:(ch + 1) * 128, :], width=1024,
                     dst='arena', gate=gate,
                     groups=[(gg * 512, W_SHUP + gg * 4 * SI + ch * 128, SI)
                             for gg in range(2)])
                for ch in range(16)]

    def shdn_jobs(gate=None):
        jobs = []
        for ld in range(16):
            hc, half = ld // 2, ld % 2
            jobs.append(dict(
                src=shdown_h[hc * 128:(hc + 1) * 128, half * 1024:(half + 1) * 1024],
                width=1024, dst=SHD_DST, gate=gate,
                scales=[(0, 1024, 40 + hc)],
                groups=[(gg * 512, F_SHD + (half * 8 + gg * 4) * 1024 + hc * 128, 1024)
                        for gg in range(2)]))
        return jobs

    def up_jobs(e, dstbase, gate=None):
        return [dict(src=upw_h[e, ch * 128:(ch + 1) * 128, :], width=1024,
                     dst='arena', gate=gate,
                     groups=[(gg * 512, dstbase + gg * 4 * 512 + ch * 128, 512)
                             for gg in range(2)])
                for ch in range(4)]

    def down_jobs(e, dstbase, dst, gate=None):
        jobs = []
        for ld in range(4):
            h0 = ld * 2
            jobs.append(dict(
                src=downw_h[e, h0 * 128:(h0 + 2) * 128, :].rearrange(
                    "(c p) i -> p c i", p=128),
                width=1024, dst=dst, gate=gate,
                scales=[(0, 512, 40 + h0), (512, 1024, 40 + h0 + 1)],
                groups=[(c * 512, dstbase + (h0 + c) * 128, 1024)
                        for c in range(2)]))
        return jobs

    if NPAIR * T >= 16384:
        DNA_DST, F_DNA = 'aun', 0
        SHD_DST, F_SHD = 'aun', 0
    else:
        DNA_DST, F_DNA = 'arena', 45056
        SHD_DST, F_SHD = 'arena', 45056 + 16384

    jobs12 = []
    for e in range(4):
        jobs12 += up_jobs(e, W_E_UP[e])
        jobs12 += down_jobs(e, F_DNA + e * 4096, DNA_DST)
    for e in (4, 5):
        jobs12 += up_jobs(e, W_E_UP[e])
        jobs12 += down_jobs(e, W_E_DN[e], 'arena')
    prep(jobs12, first_sync_wait=(v_s.h, x1_done))
    prep12_done = v_s.n

    prev_dn = [None]

    def ffn_pass(items, wait_first, final):
        outev = []
        NI = len(items)
        for tc in range(TC):
            s_b = s_s.n
            v_b = v_s.n
            up_cnt, dn_cnt, tr_cnt = {}, {}, {}
            pdn = prev_dn[0]

            @blk.tensor
            def _(t, tc=tc, s_b=s_b, v_b=v_b):
                if tc == 0:
                    for w in wait_first:
                        t.wait_ge(*w)

                def emit_up(ii):
                    item = items[ii]
                    ubank = 2 + (ii % 2)
                    if ii >= 2:
                        t.wait_ge(s_s.h, s_b + (ii - 2) + 1)
                    elif tc >= 1:
                        t.wait_ge(s_s.h, s_b)
                    last = None
                    for dc in range(DC):
                        if item[0] == 'sh':
                            rhs = arena[:, W_SHUP + dc * SI + item[1] * 512:
                                        W_SHUP + dc * SI + (item[1] + 1) * 512]
                        else:
                            rhs = arena[:, item[2] + dc * 512:
                                        item[2] + (dc + 1) * 512]
                        last = t.matmul(PB(ubank),
                                        hT[:, dc * T + tc * 128: dc * T + (tc + 1) * 128],
                                        rhs, start=(dc == 0), stop=(dc == DC - 1))
                    up_cnt[ii] = t_s.inc(last)

                emit_up(0)
                for ii, item in enumerate(items):
                    tbank = 4 + (ii % 2)
                    if ii + 1 < NI:
                        emit_up(ii + 1)
                    t.wait_ge(v_s.h, v_b + 2 * ii + 1)
                    if ii >= 2:
                        t.wait_ge(v_s.h, v_b + 2 * (ii - 2) + 2)
                    elif tc >= 1:
                        t.wait_ge(v_s.h, v_b)
                    last = None
                    for ic in range(4):
                        last = t.transpose(
                            PB(tbank).bitcast(BF16)[:, ic * 128:(ic + 1) * 128],
                            ffr[:, (2 + ii % 2) * 512 + ic * 128:
                                (2 + ii % 2) * 512 + (ic + 1) * 128], id_b[:])
                    tr_cnt[ii] = t_s.inc(last)
                    t.wait_ge(v_s.h, v_b + 2 * ii + 2)
                    if ii == 0 and tc >= 1:
                        t.wait_ge(v_s.h, outev[tc - 1])
                    last = None
                    for hc in range(2):
                        for ic in range(4):
                            if item[0] == 'sh':
                                shsrc = aun if SHD_DST == 'aun' else arena
                                rhs = shsrc[:, F_SHD + (item[1] * 4 + ic) * 1024 + hc * 512:
                                            F_SHD + (item[1] * 4 + ic) * 1024 + (hc + 1) * 512]
                            else:
                                wsrc = arena if item[4] == 'arena' else aun
                                rhs = wsrc[:, item[3] + ic * 1024 + hc * 512:
                                           item[3] + ic * 1024 + (hc + 1) * 512]
                            last = t.matmul(PB(hc),
                                            ffr[:, (4 + ii % 2) * 512 + ic * 128:
                                                (4 + ii % 2) * 512 + (ic + 1) * 128],
                                            rhs,
                                            start=(ii == 0 and ic == 0),
                                            stop=(ii == NI - 1 and ic == 3))
                    dn_cnt[ii] = t_s.inc(last)

            @blk.scalar
            def _(sc, tc=tc, s_b=s_b, v_b=v_b):
                for ii, item in enumerate(items):
                    sc.wait_ge(t_s.h, up_cnt[ii])
                    if ii < 2 and tc >= 1:
                        sc.wait_ge(v_s.h, v_b)
                    if item[0] == 'e':
                        scale = cwsq[:, tc * 8 + item[1]: tc * 8 + item[1] + 1]
                    else:
                        scale = 1.0
                    s_s.inc(sc.activation(ffr[:, (ii % 2) * 512:(ii % 2) * 512 + 512],
                                          PB(2 + ii % 2), AF.Relu, scale=scale))
            assert s_s.n == s_b + NI

            @blk.vector
            def _(v, tc=tc, s_b=s_b, pdn=pdn):
                if pdn is not None:
                    v.wait_ge(t_s.h, pdn)
                for ii, item in enumerate(items):
                    v.wait_ge(s_s.h, s_b + ii + 1)
                    rw = ffr[:, (ii % 2) * 512:(ii % 2) * 512 + 512]
                    v_s.inc(v.tensor_mul(
                        ffr[:, (2 + ii % 2) * 512:(2 + ii % 2) * 512 + 512], rw, rw))
                    v.wait_ge(t_s.h, tr_cnt[ii])
                    v_s.inc(v.tensor_copy(
                        ffr[:, (4 + ii % 2) * 512:(4 + ii % 2) * 512 + 512],
                        PB(4 + ii % 2).bitcast(BF16)[:, 0:512]))
            assert v_s.n == v_b + 2 * NI

            @blk.vector
            def _(v, tc=tc):
                v.wait_ge(t_s.h, dn_cnt[NI - 1])
                if final == 'x':
                    for hc in range(2):
                        sl = x_res[:, tc * 1024 + hc * 512: tc * 1024 + (hc + 1) * 512]
                        v_s.inc(v.tensor_add(sl, sl, PB(hc)))
                else:
                    if tc >= 2:
                        v.wait_ge(d_out.h, 16 * (tc - 1))
                    os = arf[:, F_OUTS // 2 + (tc % 2) * 1024:
                             F_OUTS // 2 + (tc % 2) * 1024 + 1024]
                    for hc in range(2):
                        v_s.inc(v.tensor_tensor(
                            os[:, hc * 512:(hc + 1) * 512],
                            x_res[:, tc * 1024 + hc * 512: tc * 1024 + (hc + 1) * 512],
                            PB(hc), ALU.add))
                outev.append(v_s.n)
            prev_dn[0] = dn_cnt[NI - 1]

            if final == 'out':
                @blk.sync
                def _(sy, tc=tc):
                    sy.wait_ge(v_s.h, outev[tc])
                    os = arf[:, F_OUTS // 2 + (tc % 2) * 1024:
                             F_OUTS // 2 + (tc % 2) * 1024 + 1024]
                    d_out.inc(sy.dma_start(out=out_h[tc * 128:(tc + 1) * 128, :],
                                           in_=os), 16)
        return outev[-1]

    itemsR1 = [('e', e, W_E_UP[e], F_DNA + e * 4096, DNA_DST) for e in range(4)]
    sR1 = ffn_pass(itemsR1, [(v_s.h, prep12_done), (s_s.h, sqcw_done)], 'x')
    r1_dn_end = prev_dn[0]

    if "xr1" in dumps:
        hh_ = mkdump("xr1", [128, TC * 1024], BF16)
        @blk.sync
        def _(sy):
            sy.wait_ge(v_s.h, sR1)
            d_sy.inc(sy.dma_start(out=hh_, in_=x_res[:]), 16)

    jobs3 = []
    for e in (6, 7):
        jobs3 += up_jobs(e, W_E_UP[e], gate=(t_s.h, r1_dn_end))
        jobs3 += down_jobs(e, W_E_DN[e], 'arena', gate=(t_s.h, r1_dn_end))
    prep(jobs3)
    prep3_done = v_s.n

    itemsR2a = [('e', e, W_E_UP[e], W_E_DN[e], 'arena') for e in (4, 5)]
    sR2a = ffn_pass(itemsR2a, [(v_s.h, prep12_done), (v_s.h, sR1)], 'x')
    r2a_dn_end = prev_dn[0]

    jobs4 = (shup_jobs(gate=(t_s.h, r2a_dn_end))
             + shdn_jobs(gate=(t_s.h, r2a_dn_end)))
    prep(jobs4)
    prep4_done = v_s.n

    itemsR2b = [('e', e, W_E_UP[e], W_E_DN[e], 'arena') for e in (6, 7)]
    sR2b = ffn_pass(itemsR2b, [(v_s.h, prep3_done), (v_s.h, sR2a)], 'x')

    if "x2" in dumps:
        hh_ = mkdump("x2", [128, TC * 1024], BF16)
        @blk.sync
        def _(sy):
            sy.wait_ge(v_s.h, sR2b)
            d_sy.inc(sy.dma_start(out=hh_, in_=x_res[:]), 16)

    itemsS = [('sh', sic) for sic in range(4)]
    sS = ffn_pass(itemsS, [(v_s.h, prep4_done), (v_s.h, sR2b)], 'out')

    @blk.sync
    def _(sy):
        sy.wait_ge(d_out.h, 16 * TC)

    st.close()
    return nc, dmp


from concourse.bass_utils import run_bass_kernel_spmd


def kernel(L0_attn_x, L0_attn_w_qkv, L0_attn_w_o, L0_ffn_gate_weight,
           L0_ffn_gate_bias, L0_ffn_up_weights, L0_ffn_down_weights,
           L0_ffn_shared_up_weight, L0_ffn_shared_down_weight, L0_c, L0_wmod):
    B, T, Dm = L0_attn_x.shape
    assert (B, T, Dm) == (8, 2048, 1024)
    nc, _ = build(T=T)
    shared = dict(
        w_qkv=np.ascontiguousarray(L0_attn_w_qkv),
        w_o=np.ascontiguousarray(L0_attn_w_o),
        gate_w=np.ascontiguousarray(L0_ffn_gate_weight),
        gate_b=np.ascontiguousarray(L0_ffn_gate_bias),
        up_w=np.ascontiguousarray(L0_ffn_up_weights),
        down_w=np.ascontiguousarray(L0_ffn_down_weights),
        shup_w=np.ascontiguousarray(L0_ffn_shared_up_weight),
        shdown_w=np.ascontiguousarray(L0_ffn_shared_down_weight),
        wmod=np.ascontiguousarray(L0_wmod),
    )
    in_maps = []
    for i in range(8):
        m = dict(shared)
        m["x"] = np.ascontiguousarray(L0_attn_x[i])
        m["c"] = np.ascontiguousarray(L0_c[i])
        in_maps.append(m)
    res = run_bass_kernel_spmd(nc, in_maps, core_ids=list(range(8)))
    return np.stack([res.results[i]["out"] for i in range(8)], axis=0)


# version 06
# speedup vs baseline: 1.3362x; 1.0673x over previous
import numpy as np
from contextlib import ExitStack

import concourse.bass as bass
import concourse.mybir as mybir

F32 = mybir.dt.float32
BF16 = mybir.dt.bfloat16
AF = mybir.ActivationFunctionType
ALU = mybir.AluOpType
AX = mybir.AxisListType.X

D = 1024
DC = 8
NPAIR = 8
E = 8
I_FF = 512
SI = 2048
LN_EPS = 1e-5


class Sem:
    def __init__(self, stack, nc, name):
        self.h = stack.enter_context(nc.semaphore(name))
        self.n = 0

    def inc(self, inst, k=1):
        inst.then_inc(self.h, k)
        self.n += k
        return self.n


def build(T=2048, dumps=(), detect_races=False):
    TC = T // 128
    NQC = T // 512
    KC = T // 128
    assert T % 512 == 0

    nc = bass.Bass("TRN2", target_bir_lowering=False,
                   detect_race_conditions=detect_races)

    x_h = nc.dram_tensor("x", [T, D], F32, kind="ExternalInput").ap()
    wqkv_h = nc.dram_tensor("w_qkv", [3 * D, D], F32, kind="ExternalInput").ap()
    wo_h = nc.dram_tensor("w_o", [D, D], F32, kind="ExternalInput").ap()
    gatew_h = nc.dram_tensor("gate_w", [E, D], F32, kind="ExternalInput").ap()
    gateb_h = nc.dram_tensor("gate_b", [E], F32, kind="ExternalInput").ap()
    upw_h = nc.dram_tensor("up_w", [E, I_FF, D], F32, kind="ExternalInput").ap()
    downw_h = nc.dram_tensor("down_w", [E, D, I_FF], F32, kind="ExternalInput").ap()
    shup_h = nc.dram_tensor("shup_w", [SI, D], F32, kind="ExternalInput").ap()
    shdown_h = nc.dram_tensor("shdown_w", [D, SI], F32, kind="ExternalInput").ap()
    c_h = nc.dram_tensor("c", [D], F32, kind="ExternalInput").ap()
    wmod_h = nc.dram_tensor("wmod", [D, 6 * D], F32, kind="ExternalInput").ap()
    out_h = nc.dram_tensor("out", [T, D], F32, kind="ExternalOutput").ap()

    dmp = {}
    def mkdump(name, shape, dtype=F32):
        if name in dumps:
            dmp[name] = nc.dram_tensor("dmp_" + name, list(shape), dtype,
                                       kind="ExternalOutput").ap()
        return dmp.get(name)

    st = ExitStack()
    sbt = lambda name, shape, dt: st.enter_context(nc.sbuf_tensor(name, shape, dt))

    x_res = sbt("x_res", [128, TC * 1024], BF16)
    hT = sbt("hT", [128, DC * T], BF16)
    aun = sbt("aun", [128, NPAIR * T], BF16)
    ARN = 51200 if NPAIR * T >= 16384 else 81920
    arena = sbt("arena", [128, ARN], BF16)
    arf = arena.ap().bitcast(F32)

    modT = sbt("modT", [128, 48], F32)
    coef = sbt("coef", [128, 48], F32)
    lnst = sbt("lnst", [128, 4 * TC], F32)
    stat6 = sbt("stat6", [128, 16], F32)
    sil_c = sbt("sil_c", [128, 8], BF16)
    sil_f = sbt("sil_f", [128, 16], F32)
    id_f = sbt("id_f", [128, 128], F32)
    id_b = sbt("id_b", [128, 128], BF16)
    ones64 = sbt("ones64", [128, 1], BF16)
    gwt = sbt("gwt", [128, 64], BF16)
    gwf = sbt("gwf", [128, 64], F32)
    gbias = sbt("gbias", [128, 8], F32)
    gb1 = sbt("gb1", [1, 8], F32)
    rtr = sbt("rtr", [128, 384], F32)
    cwsq = sbt("cwsq", [128, 8 * TC], F32)
    negm = sbt("negm", [128, 4], F32)
    mtmp = sbt("mtmp", [128, 8], F32)
    mcol = sbt("mcol", [1, 256], F32)
    mrow = sbt("mrow", [1, 8], F32)
    mrowb = sbt("mrowb", [1, 1], BF16)
    gb1b = sbt("gb1b", [1, 8], BF16)
    onesrow = sbt("onesrow", [1, 128], BF16)
    ffr = sbt("ffr", [128, 6 * 512], BF16)

    O_WQKVT = 0
    O_WOT = 16384
    O_QK65 = 24576
    O_V65 = O_QK65 + 4 * T
    O_PT = O_V65 + 2 * TC * 130
    O_SUM = O_PT + 2048
    O_RCP = O_SUM
    O_SSTG = O_SUM + T
    O_H1S = O_SSTG + 1024
    O_IND = O_H1S + 1024
    O_MQR = O_IND + 1024
    O_HSTG = O_MQR + 256
    O_STGB = O_HSTG + 4096
    assert O_STGB + 2048 <= ARN, O_STGB

    P_WMF = 24576
    P_WMB = P_WMF + 12288
    P_WQS = P_WMB + 6144
    assert P_WQS + 4096 <= ARN

    W_E_UP = {0: 8192, 1: 12288, 2: 16384, 3: 20480,
              4: 24576, 5: 28672, 6: 8192, 7: 12288}
    W_E_DN = {4: 32768, 5: 36864, 6: 16384, 7: 20480}
    W_SHUP = 24576
    F_OUTS = 40960

    stgb = arf[:, O_STGB // 2: O_STGB // 2 + 1024]
    stga = stgb
    SLOT = [arf[:, i * 1024:(i + 1) * 1024] for i in range(4)]
    wmf = arf[:, P_WMF // 2: P_WMF // 2 + 6144]
    wmb = arena[:, P_WMB:P_WMB + 6144]
    wqs = [arf[:, P_WQS // 2 + i * 1024: P_WQS // 2 + (i + 1) * 1024] for i in (0, 1)]
    hpre = [stgb[:, 0:1024]]

    psum = st.enter_context(nc.psum_tensor("psum", [128, 4096], F32))
    PB = lambda b: psum[:, b * 512:(b + 1) * 512]
    SC = [psum[:, 0:1024], psum[:, 1024:2048]]
    AVP = [psum[:, 2048:2560], psum[:, 2560:3072]]

    t_s = Sem(st, nc, "t_s")
    v_s = Sem(st, nc, "v_s")
    s_s = Sem(st, nc, "s_s")
    g_s = Sem(st, nc, "g_s")
    d_sy = Sem(st, nc, "d_sy")
    d_sc = Sem(st, nc, "d_sc")
    d_gp = Sem(st, nc, "d_gp")
    d_out = Sem(st, nc, "d_out")

    blk = st.enter_context(nc.Block())

    @blk.scalar
    def _(sc):
        d_sc.inc(sc.dma_start(out=sil_f[:, 0:8],
                              in_=c_h.rearrange("(j p) -> p j", p=128),
                              allow_slow_non_contiguous=True), 16)
        d_sc.inc(sc.dma_start(out=gb1[:], in_=gateb_h[None, :]), 16)
    ld_c = d_sc.n

    @blk.gpsimd
    def _(g):
        d_gp.inc(g.dma_start(out=x_res[:],
                             in_=x_h.rearrange("(t p) d -> p t d", p=128)), 16)
    ld_x = d_gp.n

    @blk.gpsimd
    def _(g):
        g_s.inc(g.memset(id_f[:], 0.0))
        g_s.inc(g.memset(id_b[:], 0.0))
        g_s.inc(g.memset(ones64[:], 1.0))
        g.wait_ge(g_s.h, g_s.n)
        g_s.inc(g.affine_select(out=id_f[:], in_=id_f[:], compare_op=ALU.not_equal,
                                fill=1.0, base=0, pattern=[[-1, 128]],
                                channel_multiplier=1))
        g_s.inc(g.affine_select(out=id_b[:], in_=id_b[:], compare_op=ALU.not_equal,
                                fill=1.0, base=0, pattern=[[-1, 128]],
                                channel_multiplier=1))
        g_s.inc(g.memset(onesrow[:], 1.0))
    g_const = g_s.n

    @blk.vector
    def _(v):
        v.wait_ge(d_sc.h, ld_c)
        v_s.inc(v.tensor_copy(gb1b[:], gb1[:]))
    gb_cast = v_s.n

    @blk.tensor
    def _(t):
        t.wait_ge(v_s.h, gb_cast)
        t.wait_ge(g_s.h, g_const)
        t_s.inc(t.matmul(PB(7)[:, 500:508], onesrow[:], gb1b[:],
                         start=True, stop=True))
    gb_mm = t_s.n

    @blk.vector
    def _(v):
        v.wait_ge(t_s.h, gb_mm)
        v_s.inc(v.tensor_copy(gbias[:], PB(7)[:, 500:508]))
    gb_done = v_s.n

    @blk.scalar
    def _(sc):
        sc.wait_ge(d_sc.h, ld_c)
        s_s.inc(sc.activation(sil_f[:, 8:16], sil_f[:, 0:8], AF.Silu))
    sil_done = s_s.n

    @blk.vector
    def _(v):
        v.wait_ge(s_s.h, sil_done)
        v_s.inc(v.tensor_copy(sil_c[:], sil_f[:, 8:16]))
    silc_done = v_s.n

    wm_dma, wm_cast, wm_mm = [], [], []
    for dc in range(DC):
        @blk.sync
        def _(sy, dc=dc):
            if dc >= 1:
                sy.wait_ge(v_s.h, wm_cast[dc - 1])
            d_sy.inc(sy.dma_start(out=wmf[:], in_=wmod_h[dc * 128:(dc + 1) * 128, :]), 16)
            wm_dma.append(d_sy.n)

        @blk.vector
        def _(v, dc=dc):
            v.wait_ge(d_sy.h, wm_dma[dc])
            if dc >= 1:
                v.wait_ge(t_s.h, wm_mm[dc - 1])
            v_s.inc(v.tensor_copy(wmb[:], wmf[:]))
            wm_cast.append(v_s.n)

        @blk.tensor
        def _(t, dc=dc):
            t.wait_ge(v_s.h, wm_cast[dc])
            if dc == 0:
                t.wait_ge(v_s.h, silc_done)
            last = None
            for ncx in range(48):
                last = t.matmul(psum[:, ncx * 8 + dc:ncx * 8 + dc + 1],
                                wmb[:, ncx * 128:(ncx + 1) * 128],
                                sil_c[:, dc:dc + 1],
                                start=True, stop=True)
            t_s.inc(last)
            wm_mm.append(t_s.n)
    mod_mm = wm_mm[-1]

    @blk.vector
    def _(v):
        v.wait_ge(t_s.h, mod_mm)
        v_s.inc(v.tensor_reduce(modT[:],
                                psum.ap()[:, 0:384].rearrange("p (n d) -> p n d", d=8),
                                axis=AX, op=ALU.add))
    modT_done = v_s.n

    @blk.vector
    def _(v):
        v.wait_ge(v_s.h, modT_done)
        v_s.inc(v.tensor_copy(coef[:, 0:8], modT[:, 0:8]))
        v_s.inc(v.tensor_scalar(coef[:, 8:16], modT[:, 8:16], 1.0, None, ALU.add))
        v_s.inc(v.tensor_copy(coef[:, 24:32], modT[:, 24:32]))
        v_s.inc(v.tensor_scalar(coef[:, 32:40], modT[:, 32:40], 1.0, None, ALU.add))
    pre_coef = v_s.n

    @blk.scalar
    def _(sc):
        sc.wait_ge(v_s.h, modT_done)
        s_s.inc(sc.activation(coef[:, 16:24], modT[:, 16:24], AF.Abs))
        s_s.inc(sc.activation(coef[:, 40:48], modT[:, 40:48], AF.Abs))
    gabs_done = s_s.n

    @blk.vector
    def _(v):
        v.wait_ge(s_s.h, gabs_done)
        for src in (16, 40):
            v_s.inc(v.tensor_scalar(stat6[:, 0:8], modT[:, src:src + 8], 0.0,
                                    None, ALU.is_ge))
            v.wait_ge(v_s.h, v_s.n)
            v_s.inc(v.tensor_scalar(stat6[:, 8:16], stat6[:, 0:8], 2.0, -1.0,
                                    ALU.mult, ALU.add))
            v_s.inc(v.tensor_scalar(coef[:, src:src + 8], coef[:, src:src + 8],
                                    0.25, None, ALU.max))
            v.wait_ge(v_s.h, v_s.n)
            v_s.inc(v.tensor_mul(coef[:, src:src + 8], coef[:, src:src + 8],
                                 stat6[:, 8:16]))
    coef_done = v_s.n

    def ln_stats(lnbase, xwait=None):
        @blk.vector
        def _(v):
            if xwait is not None:
                v.wait_ge(*xwait)
            for tc in range(TC):
                xs = x_res[:, tc * 1024:(tc + 1) * 1024]
                v_s.inc(v.bn_stats(stat6[:, 0:6], xs[:, 0:512]))
                v_s.inc(v.bn_stats(stat6[:, 6:12], xs[:, 512:1024]))
                v.wait_ge(v_s.h, v_s.n)
                v_s.inc(v.bn_aggr(stat6[:, 12:14], stat6[:, 0:12]))
                v.wait_ge(v_s.h, v_s.n)
                v_s.inc(v.tensor_copy(lnst[:, lnbase + TC + tc:lnbase + TC + tc + 1],
                                      stat6[:, 12:13]))
                v_s.inc(v.tensor_scalar(stat6[:, 14:15], stat6[:, 13:14], LN_EPS,
                                        None, ALU.add))
                v.wait_ge(v_s.h, v_s.n)
                v_s.inc(v.reciprocal(lnst[:, lnbase + tc:lnbase + tc + 1],
                                     stat6[:, 14:15]))
        rec_done = v_s.n

        @blk.scalar
        def _(sc):
            sc.wait_ge(v_s.h, rec_done)
            s_s.inc(sc.activation(lnst[:, lnbase:lnbase + TC],
                                  lnst[:, lnbase:lnbase + TC], AF.Sqrt))
        sq_done = s_s.n

        @blk.vector
        def _(v):
            v.wait_ge(s_s.h, sq_done)
            v_s.inc(v.tensor_mul(lnst[:, lnbase + TC:lnbase + 2 * TC],
                                 lnst[:, lnbase + TC:lnbase + 2 * TC],
                                 lnst[:, lnbase:lnbase + TC]))
            v.wait_ge(v_s.h, v_s.n)
            v_s.inc(v.tensor_scalar(lnst[:, lnbase + TC:lnbase + 2 * TC],
                                    lnst[:, lnbase + TC:lnbase + 2 * TC],
                                    -1.0, None, ALU.mult))

    def ln_apply(lnbase, cA, cB, first_wait=()):
        tr_done, ev_done = [], []
        for tc in range(TC):
            slot = 0
            @blk.vector
            def _(v, tc=tc, slot=slot):
                if tc == 0:
                    v.wait_ge(v_s.h, coef_done)
                    for w in first_wait:
                        v.wait_ge(*w)
                if tc >= 1:
                    v.wait_ge(t_s.h, tr_done[tc - 1])
                rs = lnst[:, lnbase + tc:lnbase + tc + 1]
                nm = lnst[:, lnbase + TC + tc:lnbase + TC + tc + 1]
                v_s.inc(v.tensor_scalar(hpre[slot],
                                        x_res[:, tc * 1024:(tc + 1) * 1024],
                                        rs, nm, ALU.mult, ALU.add))
            ap_done = v_s.n

            @blk.tensor
            def _(t, tc=tc, slot=slot, ap_done=ap_done):
                t.wait_ge(v_s.h, ap_done)
                if tc >= 2:
                    t.wait_ge(v_s.h, ev_done[tc - 2])
                last = None
                for dc in range(DC):
                    bank = 2 + 2 * (tc % 2) + (dc // 4)
                    last = t.transpose(
                        PB(bank)[:, (dc % 4) * 128:(dc % 4) * 128 + 128],
                        hpre[slot][:, dc * 128:(dc + 1) * 128], id_f[:])
                t_s.inc(last)
                tr_done.append(t_s.n)

            @blk.vector
            def _(v, tc=tc):
                v.wait_ge(t_s.h, tr_done[tc])
                last = None
                for dc in range(DC):
                    bank = 2 + 2 * (tc % 2) + (dc // 4)
                    last = v.tensor_scalar(
                        hT[:, dc * T + tc * 128: dc * T + (tc + 1) * 128],
                        PB(bank)[:, (dc % 4) * 128:(dc % 4) * 128 + 128],
                        coef[:, cA + dc:cA + dc + 1],
                        coef[:, cB + dc:cB + dc + 1],
                        ALU.mult, ALU.add)
                v_s.inc(last)
                ev_done.append(v_s.n)
        return ev_done[-1]

    ln_stats(0, xwait=(d_gp.h, ld_x))

    wq_dma, wq_tr, wq_ev = [], [], []
    for ch in range(24):
        slot = ch % 2
        @blk.sync
        def _(sy, ch=ch, slot=slot):
            if ch >= 2:
                sy.wait_ge(t_s.h, wq_tr[ch - 2])
            d_sy.inc(sy.dma_start(out=wqs[slot],
                                  in_=wqkv_h[ch * 128:(ch + 1) * 128, :]), 16)
            wq_dma.append(d_sy.n)

        @blk.tensor
        def _(t, ch=ch, slot=slot):
            t.wait_ge(d_sy.h, wq_dma[ch])
            if ch >= 1:
                t.wait_ge(v_s.h, wq_ev[ch - 1])
            last = None
            for dc in range(DC):
                bank = 6 + (dc // 4)
                last = t.transpose(PB(bank)[:, (dc % 4) * 128:(dc % 4) * 128 + 128],
                                   wqs[slot][:, dc * 128:(dc + 1) * 128], id_f[:])
            t_s.inc(last)
            wq_tr.append(t_s.n)

        @blk.vector
        def _(v, ch=ch):
            v.wait_ge(t_s.h, wq_tr[ch])
            last = None
            for dc in range(DC):
                bank = 6 + (dc // 4)
                last = v.tensor_copy(
                    arena[:, O_WQKVT + dc * 3072 + ch * 128:
                          O_WQKVT + dc * 3072 + (ch + 1) * 128],
                    PB(bank)[:, (dc % 4) * 128:(dc % 4) * 128 + 128])
            v_s.inc(last)
            wq_ev.append(v_s.n)
    wqkvT_done = v_s.n

    ln1_done = ln_apply(0, 8, 0)

    if "hT" in dumps:
        h = mkdump("hT", [128, DC * T], BF16)
        @blk.sync
        def _(sy):
            sy.wait_ge(v_s.h, ln1_done)
            d_sy.inc(sy.dma_start(out=h, in_=hT[:]), 16)

    @blk.gpsimd
    def _(g):
        g.wait_ge(t_s.h, wq_tr[-1])
        g.wait_ge(t_s.h, mod_mm)
        ap = arena.ap()[:, O_V65:O_V65 + 2 * TC * 130]
        g_s.inc(g.memset(ap.rearrange("p (b t h j) -> p b t h j", b=2, t=TC, h=2)
                         [:, :, :, :, 64:65], 1.0))
        g_s.inc(g.memset(arena[64:65, 24576 + 2 * T:24576 + 4 * T], 1.0))
        g_s.inc(g.memset(arena[0:16, O_IND:O_IND + 1024], 0.0))
        g.wait_ge(g_s.h, g_s.n)
        g_s.inc(g.affine_select(
            out=arena.ap()[0:16, O_IND:O_IND + 1024].rearrange(
                "p (a h j) -> p a h j", h=2, j=64),
            in_=arena.ap()[0:16, O_IND:O_IND + 1024].rearrange(
                "p (a h j) -> p a h j", h=2, j=64),
            compare_op=ALU.not_equal, fill=1.0, base=0,
            pattern=[[-2, 8], [-1, 2], [0, 64]], channel_multiplier=1))
    v65_init = g_s.n

    QB = [O_QK65 + 0 * T, O_QK65 + 1 * T]
    KB = [O_QK65 + 2 * T, O_QK65 + 3 * T]
    QC2 = max(1, NQC // 2)
    NB = 2 * QC2
    def qcs_of(b):
        qc2 = b % QC2
        return [q for q in (2 * qc2, 2 * qc2 + 1) if q < NQC]

    sqbf = arena[:, O_STGB:O_STGB + T]
    mqb = arena[:, O_MQR:O_MQR + 16]
    mqrow = arena[0:16, O_MQR + 128:O_MQR + 256]
    hstg = arena[:, O_HSTG:O_HSTG + 4096]
    qnf = rtr[:, 320:352]
    PT = [arena[:, O_PT:O_PT + 1024], arena[:, O_PT + 1024:O_PT + 2048]]
    sums_bf = arena[0:16, O_SUM:O_SUM + T]
    rcp_bf = arena[0:16, O_RCP:O_RCP + T]
    sstg = arena[:, O_SSTG:O_SSTG + 1024]
    h1s = arena[0:64, O_H1S:O_H1S + 1024]
    h1_dma_all = [0, 0]
    sums_dma_all = [0, 0]
    SUMSCALE = 2097152.0

    def plan_qkv(p, evac_pred, hend_ref, odd_dma):
        tcnt = {}
        seq = ([('v', tc) for tc in range(TC)]
               + [('q', n) for n in range(NQC)] + [('k', n) for n in range(NQC)])
        jobs_t, jobs_v = [], []
        v_at_plan = v_s.n
        for si, (kind, idx) in enumerate(seq):
            bank = 6 + (si % 2)
            def tjob(t, kind=kind, idx=idx, si=si, bank=bank):
                if si >= 2:
                    t.wait_ge(v_s.h, evac_pred[si - 2])
                else:
                    t.wait_ge(v_s.h, v_at_plan)
                last = None
                if kind in ('q', 'k'):
                    ebase = (0 if kind == 'q' else 1024) + p * 128
                    for dc in range(DC):
                        last = t.matmul(
                            PB(bank),
                            arena[:, dc * 3072 + ebase: dc * 3072 + ebase + 128],
                            hT[:, dc * T + idx * 512: dc * T + (idx + 1) * 512],
                            start=(dc == 0), stop=(dc == DC - 1))
                else:
                    ebase = 2048 + p * 128
                    for dc in range(DC):
                        last = t.matmul(
                            PB(bank)[:, 0:128],
                            hT[:, dc * T + idx * 128: dc * T + (idx + 1) * 128],
                            arena[:, dc * 3072 + ebase: dc * 3072 + ebase + 128],
                            start=(dc == 0), stop=(dc == DC - 1))
                t_s.inc(last)
                tcnt[si] = t_s.n
            def vjob(v, kind=kind, idx=idx, si=si, bank=bank):
                v.wait_ge(t_s.h, tcnt[si])
                if kind == 'v':
                    dst = arena.ap()[:, O_V65 + (p % 2) * TC * 130 + idx * 130:
                                     O_V65 + (p % 2) * TC * 130 + (idx + 1) * 130]
                    v_s.inc(v.tensor_copy(
                        dst.rearrange("p (h j) -> p h j", h=2, j=65)[:, :, 0:64],
                        PB(bank)[:, 0:128].rearrange("p (h j) -> p h j", h=2, j=64)))
                else:
                    base = QB[0] if kind == 'q' else KB[0]
                    hoff = 0 if kind == 'q' else 2048
                    v.wait_ge(t_s.h, hend_ref['h0'])
                    v_s.inc(v.tensor_copy(
                        arena[0:64, base + idx * 512: base + (idx + 1) * 512],
                        PB(bank)[0:64, :]))
                    v.wait_ge(d_sy.h, odd_dma)
                    v_s.inc(v.tensor_copy(
                        hstg[64:128, hoff + idx * 512: hoff + (idx + 1) * 512],
                        PB(bank)[64:128, :]))
            jobs_t.append(tjob)
            jobs_v.append(vjob)
        nops = [1 if kind == 'v' else 2 for (kind, idx) in seq]
        return jobs_t, jobs_v, nops

    def emit_odd_dmas(pair_end_t):
        @blk.sync
        def _(sy):
            sy.wait_ge(t_s.h, pair_end_t)
            sy.wait_ge(v_s.h, v_s.n)
            d_sy.inc(sy.dma_start(out=arena[0:64, QB[1]:QB[1] + T],
                                  in_=hstg[64:128, 0:T]), 16)
            d_sy.inc(sy.dma_start(out=arena[0:64, KB[1]:KB[1] + T],
                                  in_=hstg[64:128, 2048:2048 + T]), 16)
        return d_sy.n

    def emit_norms(p, h, data_wait):
        res = {}
        @blk.vector
        def _(v):
            v.wait_ge(*data_wait)
            v_s.inc(v.tensor_mul(sqbf[0:64, :], arena[0:64, QB[h]:QB[h] + T],
                                 arena[0:64, QB[h]:QB[h] + T]))
        res['qsq'] = v_s.n
        @blk.tensor
        def _(t):
            t.wait_ge(v_s.h, res['qsq'])
            last = None
            for tcx in range(TC):
                last = t.matmul(PB(7)[:, tcx:tcx + 1],
                                sqbf[0:64, tcx * 128:(tcx + 1) * 128],
                                ones64[0:64, :], start=True, stop=True)
            t_s.inc(last)
        res['nq'] = t_s.n
        @blk.vector
        def _(v):
            v.wait_ge(t_s.h, res['nq'])
            v_s.inc(v.tensor_mul(sqbf[0:64, :], arena[0:64, KB[h]:KB[h] + T],
                                 arena[0:64, KB[h]:KB[h] + T]))
        res['ksq'] = v_s.n
        @blk.tensor
        def _(t):
            t.wait_ge(v_s.h, res['ksq'])
            last = None
            for tcx in range(TC):
                last = t.matmul(PB(7)[:, TC + tcx:TC + tcx + 1],
                                sqbf[0:64, tcx * 128:(tcx + 1) * 128],
                                ones64[0:64, :], start=True, stop=True)
            t_s.inc(last)
        res['nk'] = t_s.n
        @blk.vector
        def _(v):
            v.wait_ge(t_s.h, res['nk'])
            v_s.inc(v.reduce_max(mtmp[:, 0:1], PB(7)[:, TC:2 * TC], axis=AX))
        res['kred'] = v_s.n
        @blk.sync
        def _(sy):
            sy.wait_ge(v_s.h, res['kred'])
            d_sy.inc(sy.dma_start(out=mcol[0:1, 0:128], in_=mtmp[:, 0:1],
                                  allow_slow_non_contiguous=True), 16)
        res['coll'] = d_sy.n
        @blk.vector
        def _(v):
            v.wait_ge(d_sy.h, res['coll'])
            v_s.inc(v.reduce_max(mrow[0:1, 0:1], mcol[0:1, 0:128], axis=AX))
            v.wait_ge(v_s.h, v_s.n)
            v_s.inc(v.tensor_copy(mrowb[:], mrow[0:1, 0:1]))
        res['mkb'] = v_s.n
        @blk.tensor
        def _(t):
            t.wait_ge(v_s.h, res['mkb'])
            t_s.inc(t.matmul(PB(7)[:, 40:41], onesrow[:], mrowb[:],
                             start=True, stop=True))
        res['bc'] = t_s.n
        @blk.vector
        def _(v):
            v.wait_ge(t_s.h, res['bc'])
            v_s.inc(v.tensor_copy(mtmp[:, 4:5], PB(7)[:, 40:41]))
            v_s.inc(v.tensor_copy(qnf[:, 0:TC], PB(7)[:, 0:TC]))
        res['prep'] = v_s.n
        @blk.scalar
        def _(sc):
            sc.wait_ge(v_s.h, res['prep'])
            s_s.inc(sc.activation(qnf[:, 16:16 + TC], qnf[:, 0:TC], AF.Ln,
                                  scale=mtmp[:, 4:5]))
            s_s.inc(sc.activation(qnf[:, 16:16 + TC], qnf[:, 16:16 + TC],
                                  AF.Exp, scale=0.5))
        res['mq'] = s_s.n
        @blk.vector
        def _(v):
            v.wait_ge(s_s.h, res['mq'])
            v_s.inc(v.tensor_scalar(mqb[:, 0:TC], qnf[:, 16:16 + TC], -1.0,
                                    None, ALU.mult))
        res['neg'] = v_s.n
        @blk.tensor
        def _(t):
            t.wait_ge(v_s.h, res['neg'])
            t_s.inc(t.transpose(PB(7).bitcast(BF16)[0:TC, 256:384],
                                mqb[:, 0:TC], id_b[:]))
        res['tr'] = t_s.n
        @blk.vector
        def _(v):
            v.wait_ge(t_s.h, res['tr'])
            v_s.inc(v.tensor_copy(mqrow[0:TC, :], PB(7).bitcast(BF16)[0:TC, 256:384]))
        res['row'] = v_s.n
        @blk.sync
        def _(sy):
            sy.wait_ge(v_s.h, res['row'])
            d_sy.inc(sy.dma_start(out=arena[64:65, QB[h]:QB[h] + T],
                                  in_=mqrow[0:TC, :]), 16)
        res['dma'] = d_sy.n
        return res['dma']

    pre_base = v_s.n
    nseq = TC + 2 * NQC
    nops_all = [1 if i < TC else 2 for i in range(nseq)]
    odd_prev = [d_sy.n]
    jt0, jv0, nops0 = plan_qkv(0, [pre_base + sum(nops_all[:i + 1])
                                   for i in range(nseq)],
                               {'h0': 0}, d_sy.n)
    @blk.tensor
    def _(t):
        t.wait_ge(v_s.h, wqkvT_done)
        t.wait_ge(v_s.h, ln1_done)
        t.wait_ge(g_s.h, v65_init)
        for j in jt0:
            j(t)
    @blk.vector
    def _(v):
        v.wait_ge(g_s.h, v65_init)
        for j in jv0:
            j(v)
    odd0 = emit_odd_dmas(t_s.n)
    nd0 = emit_norms(0, 0, (v_s.h, v_s.n))
    nd1 = emit_norms(0, 1, (d_sy.h, odd0))
    pair_ready = {0: (t_s.n, v_s.n, s_s.n, nd1)}

    if "qkv0" in dumps:
        h = mkdump("qkv0", [128, 4 * T + TC * 130], BF16)
        @blk.sync
        def _(sy):
            sy.wait_ge(v_s.h, pair_ready[0][1])
            sy.wait_ge(d_sy.h, d_sy.n)
            d_sy.inc(sy.dma_start(out=h[:, 0:4 * T],
                                  in_=arena[:, O_QK65:O_QK65 + 4 * T]), 16)
            d_sy.inc(sy.dma_start(out=h[:, 4 * T:],
                                  in_=arena[:, O_V65:O_V65 + TC * 130]), 16)

    for p in range(NPAIR):
        t0, v0, s0, mdma = pair_ready[p]
        NG2 = NB * KC

        njobs = nseq if p + 1 < NPAIR else 0
        per_b = [[] for _ in range(NB)]
        if njobs:
            for i in range(TC):
                per_b[min(i * QC2 // TC, QC2 - 1)].append(i)
            for i in range(TC, nseq):
                per_b[QC2 + min((i - TC) * QC2 // (2 * NQC), QC2 - 1)].append(i)

        v_base = v_s.n
        nops = nops_all
        evac_done = []
        job_pred = {}
        acc = v_base
        for b in range(NB):
            acc += 2
            evac_done.append(acc)
            for i in per_b[b]:
                job_pred[i] = acc + sum(nops[i2] for i2 in per_b[b]
                                        if i2 <= i)
            acc += sum(nops[i2] for i2 in per_b[b])
        s_base = s_s.n
        hend = {'h0': 0}

        if p + 1 < NPAIR:
            jobs_t, jobs_v, _ = plan_qkv(
                p + 1, [job_pred[i] for i in range(njobs)],
                hend, odd_prev[0])
        else:
            jobs_t, jobs_v = [], []

        qk_cnt, av_cnt = {}, {}

        def emit_av(t, b, kc, h):
            g2 = b * KC + kc
            par = g2 % 2
            qcs = qcs_of(b)
            t.wait_ge(s_s.h, s_base + g2 + 1)
            if kc == 0 and b >= 1:
                t.wait_ge(v_s.h, evac_done[b - 1])
            vb = O_V65 + (p % 2) * TC * 130
            last = None
            for j, qc in enumerate(qcs):
                last = t.matmul(
                    AVP[j][0:65, :],
                    arena[:, vb + kc * 130 + h * 65: vb + kc * 130 + h * 65 + 65],
                    PT[par][:, j * 512:(j + 1) * 512],
                    start=(kc == 0), stop=(kc == KC - 1))
            av_cnt[g2] = t_s.inc(last)

        @blk.tensor
        def _(t, p=p):
            t.wait_ge(v_s.h, v0)
            t.wait_ge(d_sy.h, mdma)
            for b in range(NB):
                h = b // QC2
                qcs = qcs_of(b)
                for kc in range(KC):
                    g2 = b * KC + kc
                    par = g2 % 2
                    if g2 >= 2:
                        t.wait_ge(s_s.h, s_base + (g2 - 2) + 1)
                    elif p > 0:
                        t.wait_ge(s_s.h, s_base)
                    last = None
                    for j, qc in enumerate(qcs):
                        last = t.matmul(
                            SC[par][:, j * 512:(j + 1) * 512],
                            arena[0:65, KB[h] + kc * 128: KB[h] + (kc + 1) * 128],
                            arena[0:65, QB[h] + qc * 512: QB[h] + (qc + 1) * 512],
                            start=True, stop=True)
                    qk_cnt[g2] = t_s.inc(last)
                    if kc >= 1:
                        emit_av(t, b, kc - 1, h)
                emit_av(t, b, KC - 1, h)
                if b == QC2 - 1:
                    hend['h0'] = t_s.n
                if b == NB - 1:
                    hend['h1'] = t_s.n
                for i in per_b[b]:
                    jobs_t[i](t)

        @blk.scalar
        def _(sc, p=p):
            sc.wait_ge(t_s.h, t0)
            for b in range(NB):
                w = 512 * len(qcs_of(b))
                for kc in range(KC):
                    g2 = b * KC + kc
                    par = g2 % 2
                    sc.wait_ge(t_s.h, qk_cnt[g2])
                    if g2 >= 2:
                        sc.wait_ge(t_s.h, av_cnt[g2 - 2])
                    s_s.inc(sc.activation(PT[par][:, 0:w], SC[par][:, 0:w],
                                          AF.Exp, scale=0.125))
        assert s_s.n == s_base + NG2

        for b in range(NB):
            h = b // QC2
            @blk.vector
            def _(v, p=p, b=b, h=h):
                v.wait_ge(t_s.h, av_cnt[b * KC + KC - 1])
                qcs = qcs_of(b)
                w = 512 * len(qcs)
                q0 = qcs[0]
                if h == 0:
                    v_s.inc(v.tensor_copy(
                        aun[0:64, p * T + q0 * 512: p * T + q0 * 512 + w],
                        psum[0:64, 2048:2048 + w]))
                else:
                    v.wait_ge(d_sy.h, h1_dma_all[-1])
                    v_s.inc(v.tensor_copy(h1s[:, 0:w], psum[0:64, 2048:2048 + w]))
                v.wait_ge(d_sy.h, sums_dma_all[-1])
                v_s.inc(v.tensor_scalar(sstg[64:65, 0:w],
                                        psum[64:65, 2048:2048 + w],
                                        SUMSCALE, None, ALU.mult))
                for i in per_b[b]:
                    jobs_v[i](v)
            assert v_s.n == evac_done[b] + sum(nops[i2] for i2 in per_b[b])

            @blk.sync
            def _(sy, p=p, b=b, h=h):
                sy.wait_ge(v_s.h, v_s.n)
                qcs = qcs_of(b)
                w = 512 * len(qcs)
                q0 = qcs[0]
                if h == 1:
                    d_sy.inc(sy.dma_start(
                        out=aun[64:128, p * T + q0 * 512: p * T + q0 * 512 + w],
                        in_=h1s[:, 0:w]), 16)
                    h1_dma_all.append(d_sy.n)
                d_sy.inc(sy.dma_start(
                    out=sums_bf[2 * p + h: 2 * p + h + 1, q0 * 512: q0 * 512 + w],
                    in_=sstg[64:65, 0:w]), 16)
                sums_dma_all.append(d_sy.n)

        if p + 1 < NPAIR:
            odd_n = emit_odd_dmas(hend['h1'])
            odd_prev[0] = odd_n
            ndh0 = emit_norms(p + 1, 0, (v_s.h, v_s.n))
            ndh1 = emit_norms(p + 1, 1, (d_sy.h, odd_n))
            pair_ready[p + 1] = (t_s.n, v_s.n, s_s.n, ndh1)

    attn_t, attn_v = t_s.n, v_s.n

    if "aun" in dumps:
        h = mkdump("aun", [128, NPAIR * T], BF16)
        @blk.sync
        def _(sy):
            sy.wait_ge(v_s.h, attn_v)
            d_sy.inc(sy.dma_start(out=h, in_=aun[:]), 16)
    if "sums" in dumps:
        h = mkdump("sums", [16, T], BF16)
        @blk.sync
        def _(sy):
            sy.wait_ge(d_sy.h, d_sy.n)
            d_sy.inc(sy.dma_start(out=h, in_=sums_bf[:]), 16)

    @blk.vector
    def _(v):
        v.wait_ge(d_sy.h, d_sy.n)
        with nc.allow_low_precision(reason="softmax denominators in bf16; error budget analyzed"):
            v_s.inc(v.reciprocal(rcp_bf[:], sums_bf[:]))
    rcp_done = v_s.n

    rs_pred = v_s.n
    rs_t = {}
    @blk.tensor
    def _(t):
        t.wait_ge(v_s.h, rcp_done)
        for p in range(NPAIR):
            for qc in range(NQC):
                i = p * NQC + qc
                bank = 4 + (i % 2)
                if i >= 2:
                    t.wait_ge(v_s.h, rs_pred + (i - 2) + 1)
                rs_t[i] = t_s.inc(t.matmul(
                    PB(bank), arena[0:16, O_IND + p * 128: O_IND + (p + 1) * 128],
                    rcp_bf[:, qc * 512:(qc + 1) * 512], start=True, stop=True))

    @blk.vector
    def _(v):
        for p in range(NPAIR):
            for qc in range(NQC):
                i = p * NQC + qc
                bank = 4 + (i % 2)
                v.wait_ge(t_s.h, rs_t[i])
                sl = aun[:, p * T + qc * 512: p * T + (qc + 1) * 512]
                v_s.inc(v.scalar_tensor_tensor(sl, PB(bank), SUMSCALE, sl,
                                               ALU.mult, ALU.mult))
    rescale_done = v_s.n
    assert rescale_done == rs_pred + NPAIR * NQC

    wo_dma, wo_sc, wo_tr, wo_ev = [], [], [], []
    for ch in range(8):
        @blk.sync
        def _(sy, ch=ch):
            if ch == 0:
                sy.wait_ge(t_s.h, attn_t)
            else:
                sy.wait_ge(t_s.h, wo_tr[ch - 1])
            d_sy.inc(sy.dma_start(out=stga, in_=wo_h[ch * 128:(ch + 1) * 128, :]), 16)
            wo_dma.append(d_sy.n)
        @blk.vector
        def _(v, ch=ch):
            v.wait_ge(d_sy.h, wo_dma[ch])
            v_s.inc(v.tensor_scalar(stga, stga, coef[:, 16 + ch:16 + ch + 1],
                                    None, ALU.mult))
            wo_sc.append(v_s.n)
        @blk.tensor
        def _(t, ch=ch):
            t.wait_ge(v_s.h, wo_sc[ch])
            if ch >= 1:
                t.wait_ge(v_s.h, wo_ev[ch - 1])
            last = None
            for dc in range(DC):
                bank = 6 + (dc // 4)
                last = t.transpose(PB(bank)[:, (dc % 4) * 128:(dc % 4) * 128 + 128],
                                   stga[:, dc * 128:(dc + 1) * 128], id_f[:])
            t_s.inc(last)
            wo_tr.append(t_s.n)
        @blk.vector
        def _(v, ch=ch):
            v.wait_ge(t_s.h, wo_tr[ch])
            last = None
            for dc in range(DC):
                bank = 6 + (dc // 4)
                last = v.tensor_copy(
                    arena[:, O_WOT + dc * 1024 + ch * 128:
                          O_WOT + dc * 1024 + (ch + 1) * 128],
                    PB(bank)[:, (dc % 4) * 128:(dc % 4) * 128 + 128])
            v_s.inc(last)
            wo_ev.append(v_s.n)
    wot_done = v_s.n

    op_pred = v_s.n
    op_t = {}
    @blk.tensor
    def _(t):
        t.wait_ge(v_s.h, rescale_done)
        t.wait_ge(v_s.h, wot_done)
        for tc in range(TC):
            for ec in range(2):
                i = tc * 2 + ec
                bank = i % 2
                if i >= 2:
                    t.wait_ge(v_s.h, op_pred + (i - 2) + 1)
                last = None
                for pc in range(NPAIR):
                    last = t.matmul(PB(bank),
                                    aun[:, pc * T + tc * 128: pc * T + (tc + 1) * 128],
                                    arena[:, O_WOT + pc * 1024 + ec * 512:
                                          O_WOT + pc * 1024 + (ec + 1) * 512],
                                    start=(pc == 0), stop=(pc == NPAIR - 1))
                op_t[i] = t_s.inc(last)

    @blk.vector
    def _(v):
        for tc in range(TC):
            for ec in range(2):
                i = tc * 2 + ec
                bank = i % 2
                v.wait_ge(t_s.h, op_t[i])
                sl = x_res[:, tc * 1024 + ec * 512: tc * 1024 + (ec + 1) * 512]
                v_s.inc(v.tensor_add(sl, sl, PB(bank)))
    x1_done = v_s.n

    if "x1" in dumps:
        h = mkdump("x1", [128, TC * 1024], BF16)
        @blk.sync
        def _(sy):
            sy.wait_ge(v_s.h, x1_done)
            d_sy.inc(sy.dma_start(out=h, in_=x_res[:]), 16)

    ln_stats(2 * TC, xwait=(v_s.h, x1_done))
    ln2_done = ln_apply(2 * TC, 32, 24, first_wait=[(v_s.h, x1_done)])

    if "h2T" in dumps:
        h = mkdump("h2T", [128, DC * T], BF16)
        @blk.sync
        def _(sy):
            sy.wait_ge(v_s.h, ln2_done)
            d_sy.inc(sy.dma_start(out=h, in_=hT[:]), 16)

    @blk.sync
    def _(sy):
        for dcx in range(DC):
            d_sy.inc(sy.dma_start(
                out=gwf[:, dcx * 8:(dcx + 1) * 8],
                in_=gatew_h[:, dcx * 128:(dcx + 1) * 128].rearrange("e p -> p e"),
                allow_slow_non_contiguous=True), 16)
    gw_dma = d_sy.n

    @blk.vector
    def _(v):
        v.wait_ge(d_sy.h, gw_dma)
        v_s.inc(v.tensor_copy(gwt[:], gwf[:]))
    gwt_done = v_s.n

    @blk.tensor
    def _(t):
        t.wait_ge(v_s.h, gwt_done)
        t.wait_ge(v_s.h, ln2_done)
        last = None
        for tc in range(TC):
            for dc in range(DC):
                last = t.matmul(PB(4)[:, tc * 8:(tc + 1) * 8],
                                hT[:, dc * T + tc * 128: dc * T + (tc + 1) * 128],
                                gwt[:, dc * 8:(dc + 1) * 8],
                                start=(dc == 0), stop=(dc == DC - 1))
        t_s.inc(last)
    lg_done = t_s.n

    @blk.scalar
    def _(sc):
        sc.wait_ge(t_s.h, lg_done)
        s_s.inc(sc.activation(rtr[:, 0:8 * TC], PB(4)[:, 0:8 * TC], AF.Sigmoid))
    sig_done = s_s.n

    @blk.vector
    def _(v):
        v.wait_ge(s_s.h, sig_done)
        v.wait_ge(g_s.h, g_const)
        SCO, SFC, TMP = 0, 128, 256
        for tc in range(TC):
            sco = rtr[:, SCO + tc * 8: SCO + tc * 8 + 8]
            sfc = rtr[:, SFC + tc * 8: SFC + tc * 8 + 8]
            tmp = rtr[:, TMP:TMP + 16]
            v_s.inc(v.tensor_add(sfc, sco, gbias[:]))
            v.wait_ge(v_s.h, v_s.n)
            t2 = rtr.ap()[:, TMP:TMP + 16].rearrange("p (g k) -> p g k", g=2, k=8)
            s2 = rtr.ap()[:, SFC + tc * 8: SFC + tc * 8 + 8].rearrange(
                "p (g k) -> p g k", g=2, k=4)
            pr = [(0, 1), (0, 2), (0, 3), (1, 2), (1, 3), (2, 3)]
            for j, (a, b) in enumerate(pr):
                v_s.inc(v.tensor_add(t2[:, :, j:j + 1], s2[:, :, a:a + 1],
                                     s2[:, :, b:b + 1]))
            v.wait_ge(v_s.h, v_s.n)
            for j in (1, 2, 3, 4, 5):
                v_s.inc(v.tensor_max(t2[:, :, 0:1], t2[:, :, 0:1], t2[:, :, j:j + 1]))
                v.wait_ge(v_s.h, v_s.n)
            v_s.inc(v.tensor_tensor(tmp[:, 12:13], tmp[:, 0:1], tmp[:, 8:9],
                                    ALU.is_ge))
            v.wait_ge(v_s.h, v_s.n)
            v_s.inc(v.tensor_scalar(tmp[:, 13:14], tmp[:, 12:13], -1.0, 1.0,
                                    ALU.mult, ALU.add))
            msk = rtr[:, TMP + 16:TMP + 24]
            v.wait_ge(v_s.h, v_s.n)
            v_s.inc(v.tensor_scalar(msk[:, 0:4], sfc[:, 0:4], tmp[:, 12:13],
                                    None, ALU.mult))
            v_s.inc(v.tensor_scalar(msk[:, 4:8], sfc[:, 4:8], tmp[:, 13:14],
                                    None, ALU.mult))
            v.wait_ge(v_s.h, v_s.n)
            v_s.inc(v.reduce_max(tmp[:, 14:15], msk, axis=AX))
            v.wait_ge(v_s.h, v_s.n)
            ge1 = rtr[:, TMP + 24:TMP + 32]
            v_s.inc(v.tensor_scalar(ge1, msk, tmp[:, 14:15], None, ALU.is_ge))
            v.wait_ge(v_s.h, v_s.n)
            v_s.inc(v.scalar_tensor_tensor(ge1, ge1, -3.0, msk, ALU.mult, ALU.add))
            v.wait_ge(v_s.h, v_s.n)
            v_s.inc(v.reduce_max(tmp[:, 15:16], ge1, axis=AX))
            v.wait_ge(v_s.h, v_s.n)
            sel = rtr[:, TMP + 32:TMP + 40]
            v_s.inc(v.tensor_scalar(sel, msk, tmp[:, 15:16], None, ALU.is_ge))
            v.wait_ge(v_s.h, v_s.n)
            we = rtr[:, TMP + 40:TMP + 48]
            v_s.inc(v.tensor_mul(we, sco, sel))
            v.wait_ge(v_s.h, v_s.n)
            v_s.inc(v.reduce_sum(tmp[:, 14:15], we, axis=AX))
            v.wait_ge(v_s.h, v_s.n)
            v_s.inc(v.tensor_scalar(tmp[:, 14:15], tmp[:, 14:15], 1e-20, None,
                                    ALU.add))
            v.wait_ge(v_s.h, v_s.n)
            v_s.inc(v.reciprocal(tmp[:, 15:16], tmp[:, 14:15]))
            v.wait_ge(v_s.h, v_s.n)
            v_s.inc(v.tensor_scalar(cwsq[:, tc * 8:(tc + 1) * 8], we,
                                    tmp[:, 15:16], 2.5, ALU.mult, ALU.mult))
    cw_done = v_s.n

    if "cw" in dumps:
        h = mkdump("cw", [128, 8 * TC])
        @blk.sync
        def _(sy):
            sy.wait_ge(v_s.h, cw_done)
            d_sy.inc(sy.dma_start(out=h, in_=cwsq[:]), 16)

    @blk.scalar
    def _(sc):
        sc.wait_ge(v_s.h, cw_done)
        s_s.inc(sc.activation(cwsq[:], cwsq[:], AF.Sqrt))
    sqcw_done = s_s.n

    prep_state = dict(j=0, g=0, dmac={}, scc={}, gtr={}, jtr={}, evc={})

    def prep(jobs, first_sync_wait=None, first_tensor_wait=None):
        dmac, scc, gtr, jtr, evc = (prep_state['dmac'], prep_state['scc'],
                                    prep_state['gtr'], prep_state['jtr'],
                                    prep_state['evc'])
        for job in jobs:
            j = prep_state['j']
            slot = SLOT[j % 4]
            @blk.sync
            def _(sy, j=j, job=job, slot=slot):
                if j == 0 and first_sync_wait is not None:
                    sy.wait_ge(*first_sync_wait)
                if job.get('gate') is not None:
                    sy.wait_ge(*job['gate'])
                if j >= 4:
                    sy.wait_ge(t_s.h, jtr[j - 4])
                d_sy.inc(sy.dma_start(out=slot[:, 0:job['width']], in_=job['src']), 16)
                dmac[j] = d_sy.n
            if job.get('scales'):
                @blk.vector
                def _(v, j=j, job=job, slot=slot):
                    v.wait_ge(d_sy.h, dmac[j])
                    last = None
                    for (lo, hi, cc) in job['scales']:
                        last = v.tensor_scalar(slot[:, lo:hi], slot[:, lo:hi],
                                               coef[:, cc:cc + 1], None, ALU.mult)
                    v_s.inc(last)
                    scc[j] = v_s.n
            else:
                scc[j] = None
            for (slo, dlo, dstr) in job['groups']:
                g = prep_state['g']
                bank = 6 + (g % 2)
                @blk.tensor
                def _(t, j=j, g=g, bank=bank, slo=slo, slot=slot):
                    if g == 0 and first_tensor_wait is not None:
                        t.wait_ge(*first_tensor_wait)
                    if scc[j] is not None:
                        t.wait_ge(v_s.h, scc[j])
                    else:
                        t.wait_ge(d_sy.h, dmac[j])
                    if g >= 2:
                        t.wait_ge(v_s.h, evc[g - 2])
                    last = None
                    for k in range(4):
                        last = t.transpose(PB(bank)[:, k * 128:(k + 1) * 128],
                                           slot[:, slo + k * 128: slo + (k + 1) * 128],
                                           id_f[:])
                    t_s.inc(last)
                    gtr[g] = t_s.n
                @blk.vector
                def _(v, j=j, g=g, bank=bank, dlo=dlo, dstr=dstr, job=job):
                    v.wait_ge(t_s.h, gtr[g])
                    dstt = arena if job['dst'] == 'arena' else aun
                    full = dstt.ap().rearrange("p (a c) -> p a c", c=dstr)
                    a0, c0 = dlo // dstr, dlo % dstr
                    dv = full[:, a0:a0 + 4, c0:c0 + 128]
                    sv = PB(bank).rearrange("p (g c) -> p g c", c=128)
                    v_s.inc(v.tensor_copy(dv, sv))
                    evc[g] = v_s.n
                prep_state['g'] = g + 1
            jtr[j] = gtr[prep_state['g'] - 1]
            prep_state['j'] = j + 1
        return prep_state['j'] - 1

    def shup_jobs(gate=None):
        return [dict(src=shup_h[ch * 128:(ch + 1) * 128, :], width=1024,
                     dst='arena', gate=gate,
                     groups=[(gg * 512, W_SHUP + gg * 4 * SI + ch * 128, SI)
                             for gg in range(2)])
                for ch in range(16)]

    def shdn_jobs(gate=None):
        jobs = []
        for ld in range(16):
            hc, half = ld // 2, ld % 2
            jobs.append(dict(
                src=shdown_h[hc * 128:(hc + 1) * 128, half * 1024:(half + 1) * 1024],
                width=1024, dst=SHD_DST, gate=gate,
                scales=[(0, 1024, 40 + hc)],
                groups=[(gg * 512, F_SHD + (half * 8 + gg * 4) * 1024 + hc * 128, 1024)
                        for gg in range(2)]))
        return jobs

    def up_jobs(e, dstbase, gate=None):
        return [dict(src=upw_h[e, ch * 128:(ch + 1) * 128, :], width=1024,
                     dst='arena', gate=gate,
                     groups=[(gg * 512, dstbase + gg * 4 * 512 + ch * 128, 512)
                             for gg in range(2)])
                for ch in range(4)]

    def down_jobs(e, dstbase, dst, gate=None):
        jobs = []
        for ld in range(4):
            h0 = ld * 2
            jobs.append(dict(
                src=downw_h[e, h0 * 128:(h0 + 2) * 128, :].rearrange(
                    "(c p) i -> p c i", p=128),
                width=1024, dst=dst, gate=gate,
                scales=[(0, 512, 40 + h0), (512, 1024, 40 + h0 + 1)],
                groups=[(c * 512, dstbase + (h0 + c) * 128, 1024)
                        for c in range(2)]))
        return jobs

    if NPAIR * T >= 16384:
        DNA_DST, F_DNA = 'aun', 0
        SHD_DST, F_SHD = 'aun', 0
    else:
        DNA_DST, F_DNA = 'arena', 45056
        SHD_DST, F_SHD = 'arena', 45056 + 16384

    jobs12 = []
    for e in range(4):
        jobs12 += up_jobs(e, W_E_UP[e])
        jobs12 += down_jobs(e, F_DNA + e * 4096, DNA_DST)
    for e in (4, 5):
        jobs12 += up_jobs(e, W_E_UP[e])
        jobs12 += down_jobs(e, W_E_DN[e], 'arena')
    prep(jobs12, first_sync_wait=(v_s.h, x1_done))
    prep12_done = v_s.n

    prev_dn = [None]

    def ffn_pass(items, wait_first, final):
        outev = []
        NI = len(items)
        s_pass = s_s.n
        all_up, all_tr = {}, {}
        v_bs = {}
        for tc in range(TC):
            s_b = s_pass + tc * NI
            v_b = v_s.n
            v_bs[tc] = v_b
            up_cnt, dn_cnt, tr_cnt = {}, {}, {}
            pdn = prev_dn[0]

            @blk.tensor
            def _(t, tc=tc, s_b=s_b, v_b=v_b):
                if tc == 0:
                    for w in wait_first:
                        t.wait_ge(*w)

                def emit_up(ii):
                    item = items[ii]
                    ubank = 2 + (ii % 2)
                    if ii >= 2:
                        t.wait_ge(s_s.h, s_b + (ii - 2) + 1)
                    elif tc >= 1:
                        t.wait_ge(s_s.h, s_b)
                    last = None
                    for dc in range(DC):
                        if item[0] == 'sh':
                            rhs = arena[:, W_SHUP + dc * SI + item[1] * 512:
                                        W_SHUP + dc * SI + (item[1] + 1) * 512]
                        else:
                            rhs = arena[:, item[2] + dc * 512:
                                        item[2] + (dc + 1) * 512]
                        last = t.matmul(PB(ubank),
                                        hT[:, dc * T + tc * 128: dc * T + (tc + 1) * 128],
                                        rhs, start=(dc == 0), stop=(dc == DC - 1))
                    up_cnt[ii] = t_s.inc(last)

                emit_up(0)
                for ii, item in enumerate(items):
                    tbank = 4 + (ii % 2)
                    if ii + 1 < NI:
                        emit_up(ii + 1)
                    t.wait_ge(v_s.h, v_b + 2 * ii + 1)
                    if ii >= 2:
                        t.wait_ge(v_s.h, v_b + 2 * (ii - 2) + 2)
                    elif tc >= 1:
                        t.wait_ge(v_s.h, v_b)
                    last = None
                    for ic in range(4):
                        last = t.transpose(
                            PB(tbank).bitcast(BF16)[:, ic * 128:(ic + 1) * 128],
                            ffr[:, (2 + ii % 2) * 512 + ic * 128:
                                (2 + ii % 2) * 512 + (ic + 1) * 128], id_b[:])
                    tr_cnt[ii] = t_s.inc(last)
                    t.wait_ge(v_s.h, v_b + 2 * ii + 2)
                    if ii == 0 and tc >= 1:
                        t.wait_ge(v_s.h, outev[tc - 1])
                    last = None
                    for hc in range(2):
                        for ic in range(4):
                            if item[0] == 'sh':
                                shsrc = aun if SHD_DST == 'aun' else arena
                                rhs = shsrc[:, F_SHD + (item[1] * 4 + ic) * 1024 + hc * 512:
                                            F_SHD + (item[1] * 4 + ic) * 1024 + (hc + 1) * 512]
                            else:
                                wsrc = arena if item[4] == 'arena' else aun
                                rhs = wsrc[:, item[3] + ic * 1024 + hc * 512:
                                           item[3] + ic * 1024 + (hc + 1) * 512]
                            last = t.matmul(PB(hc),
                                            ffr[:, (4 + ii % 2) * 512 + ic * 128:
                                                (4 + ii % 2) * 512 + (ic + 1) * 128],
                                            rhs,
                                            start=(ii == 0 and ic == 0),
                                            stop=(ii == NI - 1 and ic == 3))
                    dn_cnt[ii] = t_s.inc(last)

            all_up[tc] = dict(up_cnt)
            all_tr[tc] = dict(tr_cnt)

            @blk.vector
            def _(v, tc=tc, s_b=s_b, pdn=pdn):
                if pdn is not None:
                    v.wait_ge(t_s.h, pdn)
                for ii, item in enumerate(items):
                    v.wait_ge(s_s.h, s_b + ii + 1)
                    rw = ffr[:, (ii % 2) * 512:(ii % 2) * 512 + 512]
                    v_s.inc(v.tensor_mul(
                        ffr[:, (2 + ii % 2) * 512:(2 + ii % 2) * 512 + 512], rw, rw))
                    v.wait_ge(t_s.h, tr_cnt[ii])
                    v_s.inc(v.tensor_copy(
                        ffr[:, (4 + ii % 2) * 512:(4 + ii % 2) * 512 + 512],
                        PB(4 + ii % 2).bitcast(BF16)[:, 0:512]))
            assert v_s.n == v_b + 2 * NI

            @blk.vector
            def _(v, tc=tc):
                v.wait_ge(t_s.h, dn_cnt[NI - 1])
                if final == 'x':
                    for hc in range(2):
                        sl = x_res[:, tc * 1024 + hc * 512: tc * 1024 + (hc + 1) * 512]
                        v_s.inc(v.tensor_add(sl, sl, PB(hc)))
                else:
                    if tc >= 2:
                        v.wait_ge(d_out.h, 16 * (tc - 1))
                    os = arf[:, F_OUTS // 2 + (tc % 2) * 1024:
                             F_OUTS // 2 + (tc % 2) * 1024 + 1024]
                    for hc in range(2):
                        v_s.inc(v.tensor_tensor(
                            os[:, hc * 512:(hc + 1) * 512],
                            x_res[:, tc * 1024 + hc * 512: tc * 1024 + (hc + 1) * 512],
                            PB(hc), ALU.add))
                outev.append(v_s.n)
            prev_dn[0] = dn_cnt[NI - 1]

            if final == 'out':
                @blk.sync
                def _(sy, tc=tc):
                    sy.wait_ge(v_s.h, outev[tc])
                    os = arf[:, F_OUTS // 2 + (tc % 2) * 1024:
                             F_OUTS // 2 + (tc % 2) * 1024 + 1024]
                    d_out.inc(sy.dma_start(out=out_h[tc * 128:(tc + 1) * 128, :],
                                           in_=os), 16)

        @blk.scalar
        def _(sc):
            for tc in range(TC):
                for ii, item in enumerate(items):
                    sc.wait_ge(t_s.h, all_up[tc][ii])
                    if ii < 2 and tc >= 1:
                        sc.wait_ge(v_s.h, v_bs[tc])
                    if item[0] == 'e':
                        scale = cwsq[:, tc * 8 + item[1]: tc * 8 + item[1] + 1]
                    else:
                        scale = 1.0
                    s_s.inc(sc.activation(ffr[:, (ii % 2) * 512:(ii % 2) * 512 + 512],
                                          PB(2 + ii % 2), AF.Relu, scale=scale))
        assert s_s.n == s_pass + TC * NI
        return outev[-1]

    itemsR1 = [('e', e, W_E_UP[e], F_DNA + e * 4096, DNA_DST) for e in range(4)]
    sR1 = ffn_pass(itemsR1, [(v_s.h, prep12_done), (s_s.h, sqcw_done)], 'x')
    r1_dn_end = prev_dn[0]

    if "xr1" in dumps:
        hh_ = mkdump("xr1", [128, TC * 1024], BF16)
        @blk.sync
        def _(sy):
            sy.wait_ge(v_s.h, sR1)
            d_sy.inc(sy.dma_start(out=hh_, in_=x_res[:]), 16)

    jobs3 = []
    for e in (6, 7):
        jobs3 += up_jobs(e, W_E_UP[e], gate=(t_s.h, r1_dn_end))
        jobs3 += down_jobs(e, W_E_DN[e], 'arena', gate=(t_s.h, r1_dn_end))
    prep(jobs3)
    prep3_done = v_s.n

    itemsR2a = [('e', e, W_E_UP[e], W_E_DN[e], 'arena') for e in (4, 5)]
    sR2a = ffn_pass(itemsR2a, [(v_s.h, prep12_done), (v_s.h, sR1)], 'x')
    r2a_dn_end = prev_dn[0]

    jobs4 = (shup_jobs(gate=(t_s.h, r2a_dn_end))
             + shdn_jobs(gate=(t_s.h, r2a_dn_end)))
    prep(jobs4)
    prep4_done = v_s.n

    itemsR2b = [('e', e, W_E_UP[e], W_E_DN[e], 'arena') for e in (6, 7)]
    sR2b = ffn_pass(itemsR2b, [(v_s.h, prep3_done), (v_s.h, sR2a)], 'x')

    if "x2" in dumps:
        hh_ = mkdump("x2", [128, TC * 1024], BF16)
        @blk.sync
        def _(sy):
            sy.wait_ge(v_s.h, sR2b)
            d_sy.inc(sy.dma_start(out=hh_, in_=x_res[:]), 16)

    itemsS = [('sh', sic) for sic in range(4)]
    sS = ffn_pass(itemsS, [(v_s.h, prep4_done), (v_s.h, sR2b)], 'out')

    @blk.sync
    def _(sy):
        sy.wait_ge(d_out.h, 16 * TC)

    st.close()
    return nc, dmp


from concourse.bass_utils import run_bass_kernel_spmd


def kernel(L0_attn_x, L0_attn_w_qkv, L0_attn_w_o, L0_ffn_gate_weight,
           L0_ffn_gate_bias, L0_ffn_up_weights, L0_ffn_down_weights,
           L0_ffn_shared_up_weight, L0_ffn_shared_down_weight, L0_c, L0_wmod):
    B, T, Dm = L0_attn_x.shape
    assert (B, T, Dm) == (8, 2048, 1024)
    nc, _ = build(T=T)
    shared = dict(
        w_qkv=np.ascontiguousarray(L0_attn_w_qkv),
        w_o=np.ascontiguousarray(L0_attn_w_o),
        gate_w=np.ascontiguousarray(L0_ffn_gate_weight),
        gate_b=np.ascontiguousarray(L0_ffn_gate_bias),
        up_w=np.ascontiguousarray(L0_ffn_up_weights),
        down_w=np.ascontiguousarray(L0_ffn_down_weights),
        shup_w=np.ascontiguousarray(L0_ffn_shared_up_weight),
        shdown_w=np.ascontiguousarray(L0_ffn_shared_down_weight),
        wmod=np.ascontiguousarray(L0_wmod),
    )
    in_maps = []
    for i in range(8):
        m = dict(shared)
        m["x"] = np.ascontiguousarray(L0_attn_x[i])
        m["c"] = np.ascontiguousarray(L0_c[i])
        in_maps.append(m)
    res = run_bass_kernel_spmd(nc, in_maps, core_ids=list(range(8)))
    return np.stack([res.results[i]["out"] for i in range(8)], axis=0)


# version 08
# speedup vs baseline: 1.4129x; 1.0574x over previous
import numpy as np
from contextlib import ExitStack

import concourse.bass as bass
import concourse.mybir as mybir

F32 = mybir.dt.float32
BF16 = mybir.dt.bfloat16
AF = mybir.ActivationFunctionType
ALU = mybir.AluOpType
AX = mybir.AxisListType.X

D = 1024
DC = 8
NPAIR = 8
E = 8
I_FF = 512
SI = 2048
LN_EPS = 1e-5


class Sem:
    def __init__(self, stack, nc, name):
        self.h = stack.enter_context(nc.semaphore(name))
        self.n = 0

    def inc(self, inst, k=1):
        inst.then_inc(self.h, k)
        self.n += k
        return self.n


def build(T=2048, dumps=(), detect_races=False):
    TC = T // 128
    NQC = T // 512
    KC = T // 128
    assert T % 512 == 0

    nc = bass.Bass("TRN2", target_bir_lowering=False,
                   detect_race_conditions=detect_races)

    x_h = nc.dram_tensor("x", [T, D], F32, kind="ExternalInput").ap()
    wqkv_h = nc.dram_tensor("w_qkv", [3 * D, D], F32, kind="ExternalInput").ap()
    wo_h = nc.dram_tensor("w_o", [D, D], F32, kind="ExternalInput").ap()
    gatew_h = nc.dram_tensor("gate_w", [E, D], F32, kind="ExternalInput").ap()
    gateb_h = nc.dram_tensor("gate_b", [E], F32, kind="ExternalInput").ap()
    upw_h = nc.dram_tensor("up_w", [E, I_FF, D], F32, kind="ExternalInput").ap()
    downw_h = nc.dram_tensor("down_w", [E, D, I_FF], F32, kind="ExternalInput").ap()
    shup_h = nc.dram_tensor("shup_w", [SI, D], F32, kind="ExternalInput").ap()
    shdown_h = nc.dram_tensor("shdown_w", [D, SI], F32, kind="ExternalInput").ap()
    c_h = nc.dram_tensor("c", [D], F32, kind="ExternalInput").ap()
    wmod_h = nc.dram_tensor("wmod", [D, 6 * D], F32, kind="ExternalInput").ap()
    out_h = nc.dram_tensor("out", [T, D], F32, kind="ExternalOutput").ap()

    dmp = {}
    def mkdump(name, shape, dtype=F32):
        if name in dumps:
            dmp[name] = nc.dram_tensor("dmp_" + name, list(shape), dtype,
                                       kind="ExternalOutput").ap()
        return dmp.get(name)

    st = ExitStack()
    sbt = lambda name, shape, dt: st.enter_context(nc.sbuf_tensor(name, shape, dt))

    x_res = sbt("x_res", [128, TC * 1024], BF16)
    hT = sbt("hT", [128, DC * T], BF16)
    aun = sbt("aun", [128, NPAIR * T], BF16)
    ARN = 51200 if NPAIR * T >= 16384 else 81920
    arena = sbt("arena", [128, ARN], BF16)
    arf = arena.ap().bitcast(F32)

    modT = sbt("modT", [128, 48], F32)
    coef = sbt("coef", [128, 48], F32)
    lnst = sbt("lnst", [128, 4 * TC], F32)
    stat6 = sbt("stat6", [128, 16], F32)
    sil_c = sbt("sil_c", [128, 8], BF16)
    sil_f = sbt("sil_f", [128, 16], F32)
    id_f = sbt("id_f", [128, 128], F32)
    id_b = sbt("id_b", [128, 128], BF16)
    ones64 = sbt("ones64", [128, 1], BF16)
    gwt = sbt("gwt", [128, 64], BF16)
    gwf = sbt("gwf", [128, 64], F32)
    gbias = sbt("gbias", [128, 8], F32)
    gb1 = sbt("gb1", [1, 8], F32)
    rtr = sbt("rtr", [128, 384], F32)
    cwsq = sbt("cwsq", [128, 8 * TC], F32)
    negm = sbt("negm", [128, 4], F32)
    mtmp = sbt("mtmp", [128, 8], F32)
    mcol = sbt("mcol", [1, 256], F32)
    mrow = sbt("mrow", [1, 8], F32)
    mrowb = sbt("mrowb", [1, 1], BF16)
    gb1b = sbt("gb1b", [1, 8], BF16)
    onesrow = sbt("onesrow", [1, 128], BF16)
    ffr = sbt("ffr", [128, 6 * 512], BF16)

    O_WQKVT = 0
    O_WOT = 16384
    O_QK65 = 24576
    O_V65 = O_QK65 + 4 * T
    O_PT = O_V65 + 2 * TC * 130
    O_SUM = O_PT + 2048
    O_RCP = O_SUM
    O_SSTG = O_SUM + T
    O_H1S = O_SSTG + 1024
    O_IND = O_H1S + 1024
    O_MQR = O_IND + 1024
    O_HSTG = O_MQR + 256
    O_STGB = O_HSTG + 4096
    assert O_STGB + 2048 <= ARN, O_STGB

    P_WMF = 24576
    P_WMB = P_WMF + 12288
    P_WQS = P_WMB + 6144
    assert P_WQS + 4096 <= ARN

    W_E_UP = {0: 8192, 1: 12288, 2: 16384, 3: 20480,
              4: 24576, 5: 28672, 6: 8192, 7: 12288}
    W_E_DN = {4: 32768, 5: 36864, 6: 16384, 7: 20480}
    W_SHUP = 24576
    F_OUTS = 40960

    stgb = arf[:, O_STGB // 2: O_STGB // 2 + 1024]
    stga = stgb
    SLOT = [arf[:, i * 1024:(i + 1) * 1024] for i in range(4)]
    wmf2 = [arf[:, P_WMF // 2: P_WMF // 2 + 3072],
            arf[:, P_WMF // 2 + 3072: P_WMF // 2 + 6144]]
    wmb2 = [arena[:, P_WMB:P_WMB + 3072],
            arena[:, P_WMB + 3072:P_WMB + 6144]]
    wqs = [arf[:, P_WQS // 2 + i * 1024: P_WQS // 2 + (i + 1) * 1024] for i in (0, 1)]
    hpre = [stgb[:, 0:1024], arf[:, P_WMF // 2: P_WMF // 2 + 1024]]

    psum = st.enter_context(nc.psum_tensor("psum", [128, 4096], F32))
    PB = lambda b: psum[:, b * 512:(b + 1) * 512]
    SC = [psum[:, 0:1024], psum[:, 1024:2048]]
    AVP = [psum[:, 2048:2560], psum[:, 2560:3072]]

    t_s = Sem(st, nc, "t_s")
    v_s = Sem(st, nc, "v_s")
    s_s = Sem(st, nc, "s_s")
    g_s = Sem(st, nc, "g_s")
    d_sy = Sem(st, nc, "d_sy")
    d_sc = Sem(st, nc, "d_sc")
    d_gp = Sem(st, nc, "d_gp")
    d_out = Sem(st, nc, "d_out")

    blk = st.enter_context(nc.Block())

    @blk.scalar
    def _(sc):
        d_sc.inc(sc.dma_start(out=sil_f[:, 0:8],
                              in_=c_h.rearrange("(j p) -> p j", p=128),
                              allow_slow_non_contiguous=True), 16)
        d_sc.inc(sc.dma_start(out=gb1[:], in_=gateb_h[None, :]), 16)
    ld_c = d_sc.n

    @blk.gpsimd
    def _(g):
        d_gp.inc(g.dma_start(out=x_res[:],
                             in_=x_h.rearrange("(t p) d -> p t d", p=128)), 16)
    ld_x = d_gp.n

    @blk.gpsimd
    def _(g):
        g_s.inc(g.memset(id_f[:], 0.0))
        g_s.inc(g.memset(id_b[:], 0.0))
        g_s.inc(g.memset(ones64[:], 1.0))
        g.wait_ge(g_s.h, g_s.n)
        g_s.inc(g.affine_select(out=id_f[:], in_=id_f[:], compare_op=ALU.not_equal,
                                fill=1.0, base=0, pattern=[[-1, 128]],
                                channel_multiplier=1))
        g_s.inc(g.affine_select(out=id_b[:], in_=id_b[:], compare_op=ALU.not_equal,
                                fill=1.0, base=0, pattern=[[-1, 128]],
                                channel_multiplier=1))
        g_s.inc(g.memset(onesrow[:], 1.0))
    g_const = g_s.n

    @blk.vector
    def _(v):
        v.wait_ge(d_sc.h, ld_c)
        v_s.inc(v.tensor_copy(gb1b[:], gb1[:]))
    gb_cast = v_s.n

    @blk.tensor
    def _(t):
        t.wait_ge(v_s.h, gb_cast)
        t.wait_ge(g_s.h, g_const)
        t_s.inc(t.matmul(PB(7)[:, 500:508], onesrow[:], gb1b[:],
                         start=True, stop=True))
    gb_mm = t_s.n

    @blk.vector
    def _(v):
        v.wait_ge(t_s.h, gb_mm)
        v_s.inc(v.tensor_copy(gbias[:], PB(7)[:, 500:508]))
    gb_done = v_s.n

    @blk.scalar
    def _(sc):
        sc.wait_ge(d_sc.h, ld_c)
        s_s.inc(sc.activation(sil_f[:, 8:16], sil_f[:, 0:8], AF.Silu))
    sil_done = s_s.n

    @blk.vector
    def _(v):
        v.wait_ge(s_s.h, sil_done)
        v_s.inc(v.tensor_copy(sil_c[:], sil_f[:, 8:16]))
    silc_done = v_s.n

    wm_dma, wm_cast, wm_mm = [], [], []
    for hc in range(2 * DC):
        dc, half = hc // 2, hc % 2
        @blk.sync
        def _(sy, hc=hc, dc=dc, half=half):
            if hc >= 2:
                sy.wait_ge(v_s.h, wm_cast[hc - 2])
            d_sy.inc(sy.dma_start(
                out=wmf2[hc % 2][:],
                in_=wmod_h[dc * 128:(dc + 1) * 128, half * 3072:(half + 1) * 3072]), 16)
            wm_dma.append(d_sy.n)

        @blk.vector
        def _(v, hc=hc):
            v.wait_ge(d_sy.h, wm_dma[hc])
            if hc >= 2:
                v.wait_ge(t_s.h, wm_mm[hc - 2])
            v_s.inc(v.tensor_copy(wmb2[hc % 2][:], wmf2[hc % 2][:]))
            wm_cast.append(v_s.n)

        @blk.tensor
        def _(t, hc=hc, dc=dc, half=half):
            t.wait_ge(v_s.h, wm_cast[hc])
            if hc == 0:
                t.wait_ge(v_s.h, silc_done)
            last = None
            for nx in range(24):
                ncx = half * 24 + nx
                last = t.matmul(psum[:, ncx * 8 + dc:ncx * 8 + dc + 1],
                                wmb2[hc % 2][:, nx * 128:(nx + 1) * 128],
                                sil_c[:, dc:dc + 1],
                                start=True, stop=True)
            t_s.inc(last)
            wm_mm.append(t_s.n)
    mod_mm = wm_mm[-1]

    @blk.vector
    def _(v):
        v.wait_ge(t_s.h, mod_mm)
        v_s.inc(v.tensor_reduce(modT[:],
                                psum.ap()[:, 0:384].rearrange("p (n d) -> p n d", d=8),
                                axis=AX, op=ALU.add))
    modT_done = v_s.n

    @blk.vector
    def _(v):
        v.wait_ge(v_s.h, modT_done)
        v_s.inc(v.tensor_copy(coef[:, 0:8], modT[:, 0:8]))
        v_s.inc(v.tensor_scalar(coef[:, 8:16], modT[:, 8:16], 1.0, None, ALU.add))
        v_s.inc(v.tensor_copy(coef[:, 24:32], modT[:, 24:32]))
        v_s.inc(v.tensor_scalar(coef[:, 32:40], modT[:, 32:40], 1.0, None, ALU.add))
    pre_coef = v_s.n

    @blk.scalar
    def _(sc):
        sc.wait_ge(v_s.h, modT_done)
        s_s.inc(sc.activation(coef[:, 16:24], modT[:, 16:24], AF.Abs))
        s_s.inc(sc.activation(coef[:, 40:48], modT[:, 40:48], AF.Abs))
    gabs_done = s_s.n

    @blk.vector
    def _(v):
        v.wait_ge(s_s.h, gabs_done)
        for src in (16, 40):
            v_s.inc(v.tensor_scalar(stat6[:, 0:8], modT[:, src:src + 8], 0.0,
                                    None, ALU.is_ge))
            v.wait_ge(v_s.h, v_s.n)
            v_s.inc(v.tensor_scalar(stat6[:, 8:16], stat6[:, 0:8], 2.0, -1.0,
                                    ALU.mult, ALU.add))
            v_s.inc(v.tensor_scalar(coef[:, src:src + 8], coef[:, src:src + 8],
                                    0.25, None, ALU.max))
            v.wait_ge(v_s.h, v_s.n)
            v_s.inc(v.tensor_mul(coef[:, src:src + 8], coef[:, src:src + 8],
                                 stat6[:, 8:16]))
    coef_done = v_s.n

    def ln_stats(lnbase, xwait=None):
        @blk.vector
        def _(v):
            if xwait is not None:
                v.wait_ge(*xwait)
            for tc in range(TC):
                xs = x_res[:, tc * 1024:(tc + 1) * 1024]
                v_s.inc(v.bn_stats(stat6[:, 0:6], xs[:, 0:512]))
                v_s.inc(v.bn_stats(stat6[:, 6:12], xs[:, 512:1024]))
                v.wait_ge(v_s.h, v_s.n)
                v_s.inc(v.bn_aggr(stat6[:, 12:14], stat6[:, 0:12]))
                v.wait_ge(v_s.h, v_s.n)
                v_s.inc(v.tensor_copy(lnst[:, lnbase + TC + tc:lnbase + TC + tc + 1],
                                      stat6[:, 12:13]))
                v_s.inc(v.tensor_scalar(stat6[:, 14:15], stat6[:, 13:14], LN_EPS,
                                        None, ALU.add))
                v.wait_ge(v_s.h, v_s.n)
                v_s.inc(v.reciprocal(lnst[:, lnbase + tc:lnbase + tc + 1],
                                     stat6[:, 14:15]))
        rec_done = v_s.n

        @blk.scalar
        def _(sc):
            sc.wait_ge(v_s.h, rec_done)
            s_s.inc(sc.activation(lnst[:, lnbase:lnbase + TC],
                                  lnst[:, lnbase:lnbase + TC], AF.Sqrt))
        sq_done = s_s.n

        @blk.vector
        def _(v):
            v.wait_ge(s_s.h, sq_done)
            v_s.inc(v.tensor_mul(lnst[:, lnbase + TC:lnbase + 2 * TC],
                                 lnst[:, lnbase + TC:lnbase + 2 * TC],
                                 lnst[:, lnbase:lnbase + TC]))
            v.wait_ge(v_s.h, v_s.n)
            v_s.inc(v.tensor_scalar(lnst[:, lnbase + TC:lnbase + 2 * TC],
                                    lnst[:, lnbase + TC:lnbase + 2 * TC],
                                    -1.0, None, ALU.mult))

    def ln_apply(lnbase, cA, cB, first_wait=(), interleave=None):
        tr_done, ev_done = [], []
        for tc in range(TC):
            slot = tc % 2
            if interleave is not None:
                interleave(tc)
            @blk.vector
            def _(v, tc=tc, slot=slot):
                if tc == 0:
                    v.wait_ge(v_s.h, coef_done)
                    for w in first_wait:
                        v.wait_ge(*w)
                if tc >= 2:
                    v.wait_ge(t_s.h, tr_done[tc - 2])
                rs = lnst[:, lnbase + tc:lnbase + tc + 1]
                nm = lnst[:, lnbase + TC + tc:lnbase + TC + tc + 1]
                v_s.inc(v.tensor_scalar(hpre[slot],
                                        x_res[:, tc * 1024:(tc + 1) * 1024],
                                        rs, nm, ALU.mult, ALU.add))
            ap_done = v_s.n

            @blk.tensor
            def _(t, tc=tc, slot=slot, ap_done=ap_done):
                t.wait_ge(v_s.h, ap_done)
                if tc >= 2:
                    t.wait_ge(v_s.h, ev_done[tc - 2])
                last = None
                for dc in range(DC):
                    bank = 2 + 2 * (tc % 2) + (dc // 4)
                    last = t.transpose(
                        PB(bank)[:, (dc % 4) * 128:(dc % 4) * 128 + 128],
                        hpre[slot][:, dc * 128:(dc + 1) * 128], id_f[:])
                t_s.inc(last)
                tr_done.append(t_s.n)

            @blk.vector
            def _(v, tc=tc):
                v.wait_ge(t_s.h, tr_done[tc])
                last = None
                for dc in range(DC):
                    bank = 2 + 2 * (tc % 2) + (dc // 4)
                    last = v.tensor_scalar(
                        hT[:, dc * T + tc * 128: dc * T + (tc + 1) * 128],
                        PB(bank)[:, (dc % 4) * 128:(dc % 4) * 128 + 128],
                        coef[:, cA + dc:cA + dc + 1],
                        coef[:, cB + dc:cB + dc + 1],
                        ALU.mult, ALU.add)
                v_s.inc(last)
                ev_done.append(v_s.n)
        return ev_done[-1]

    ln_stats(0, xwait=(d_gp.h, ld_x))

    wq_dma, wq_tr, wq_ev = [], [], []

    def emit_wq(ch):
        slot = ch % 2
        @blk.sync
        def _(sy, ch=ch, slot=slot):
            if ch >= 2:
                sy.wait_ge(t_s.h, wq_tr[ch - 2])
            d_sy.inc(sy.dma_start(out=wqs[slot],
                                  in_=wqkv_h[ch * 128:(ch + 1) * 128, :]), 16)
            wq_dma.append(d_sy.n)

        @blk.tensor
        def _(t, ch=ch, slot=slot):
            t.wait_ge(d_sy.h, wq_dma[ch])
            if ch >= 1:
                t.wait_ge(v_s.h, wq_ev[ch - 1])
            last = None
            for dc in range(DC):
                bank = 6 + (dc // 4)
                last = t.transpose(PB(bank)[:, (dc % 4) * 128:(dc % 4) * 128 + 128],
                                   wqs[slot][:, dc * 128:(dc + 1) * 128], id_f[:])
            t_s.inc(last)
            wq_tr.append(t_s.n)

        @blk.vector
        def _(v, ch=ch):
            v.wait_ge(t_s.h, wq_tr[ch])
            last = None
            for dc in range(DC):
                bank = 6 + (dc // 4)
                last = v.tensor_copy(
                    arena[:, O_WQKVT + dc * 3072 + ch * 128:
                          O_WQKVT + dc * 3072 + (ch + 1) * 128],
                    PB(bank)[:, (dc % 4) * 128:(dc % 4) * 128 + 128])
            v_s.inc(last)
            wq_ev.append(v_s.n)

    wq_next = [0]

    def wq_interleave(tc):
        n = 2 if tc < TC // 2 else 1
        for _i in range(n):
            if wq_next[0] < 24:
                emit_wq(wq_next[0])
                wq_next[0] += 1

    ln1_done = ln_apply(0, 8, 0, interleave=wq_interleave)
    while wq_next[0] < 24:
        emit_wq(wq_next[0])
        wq_next[0] += 1
    wqkvT_done = wq_ev[-1]

    if "hT" in dumps:
        h = mkdump("hT", [128, DC * T], BF16)
        @blk.sync
        def _(sy):
            sy.wait_ge(v_s.h, ln1_done)
            d_sy.inc(sy.dma_start(out=h, in_=hT[:]), 16)

    @blk.gpsimd
    def _(g):
        g.wait_ge(t_s.h, wq_tr[-1])
        g.wait_ge(t_s.h, mod_mm)
        ap = arena.ap()[:, O_V65:O_V65 + 2 * TC * 130]
        g_s.inc(g.memset(ap.rearrange("p (b t h j) -> p b t h j", b=2, t=TC, h=2)
                         [:, :, :, :, 64:65], 1.0))
        g_s.inc(g.memset(arena[64:65, 24576 + 2 * T:24576 + 4 * T], 1.0))
        g_s.inc(g.memset(arena[0:16, O_IND:O_IND + 1024], 0.0))
        g.wait_ge(g_s.h, g_s.n)
        g_s.inc(g.affine_select(
            out=arena.ap()[0:16, O_IND:O_IND + 1024].rearrange(
                "p (a h j) -> p a h j", h=2, j=64),
            in_=arena.ap()[0:16, O_IND:O_IND + 1024].rearrange(
                "p (a h j) -> p a h j", h=2, j=64),
            compare_op=ALU.not_equal, fill=1.0, base=0,
            pattern=[[-2, 8], [-1, 2], [0, 64]], channel_multiplier=1))
    v65_init = g_s.n

    QB = [O_QK65 + 0 * T, O_QK65 + 1 * T]
    KB = [O_QK65 + 2 * T, O_QK65 + 3 * T]
    QC2 = max(1, NQC // 2)
    NB = 2 * QC2
    def qcs_of(b):
        qc2 = b % QC2
        return [q for q in (2 * qc2, 2 * qc2 + 1) if q < NQC]

    sqbf = arena[:, O_STGB:O_STGB + T]
    mqb = arena[:, O_MQR:O_MQR + 16]
    mqrow = arena[0:16, O_MQR + 128:O_MQR + 256]
    hstg = arena[:, O_HSTG:O_HSTG + 4096]
    qnf = rtr[:, 320:352]
    PT = [arena[:, O_PT:O_PT + 1024], arena[:, O_PT + 1024:O_PT + 2048]]
    sums_bf = arena[0:16, O_SUM:O_SUM + T]
    rcp_bf = arena[0:16, O_RCP:O_RCP + T]
    sstg = arena[:, O_SSTG:O_SSTG + 1024]
    h1s = arena[0:64, O_H1S:O_H1S + 1024]
    h1_dma_all = [0, 0]
    sums_dma_all = [0, 0]
    SUMSCALE = 2097152.0

    def plan_qkv(p, evac_pred, hend_ref, odd_dma):
        tcnt = {}
        seq = ([('v', tc) for tc in range(TC)]
               + [('q', n) for n in range(NQC)] + [('k', n) for n in range(NQC)])
        jobs_t, jobs_v = [], []
        v_at_plan = v_s.n
        for si, (kind, idx) in enumerate(seq):
            bank = 6 + (si % 2)
            def tjob(t, kind=kind, idx=idx, si=si, bank=bank):
                if si >= 2:
                    t.wait_ge(v_s.h, evac_pred[si - 2])
                else:
                    t.wait_ge(v_s.h, v_at_plan)
                last = None
                if kind in ('q', 'k'):
                    ebase = (0 if kind == 'q' else 1024) + p * 128
                    for dc in range(DC):
                        last = t.matmul(
                            PB(bank),
                            arena[:, dc * 3072 + ebase: dc * 3072 + ebase + 128],
                            hT[:, dc * T + idx * 512: dc * T + (idx + 1) * 512],
                            start=(dc == 0), stop=(dc == DC - 1))
                else:
                    ebase = 2048 + p * 128
                    for dc in range(DC):
                        last = t.matmul(
                            PB(bank)[:, 0:128],
                            hT[:, dc * T + idx * 128: dc * T + (idx + 1) * 128],
                            arena[:, dc * 3072 + ebase: dc * 3072 + ebase + 128],
                            start=(dc == 0), stop=(dc == DC - 1))
                t_s.inc(last)
                tcnt[si] = t_s.n
            def vjob(v, kind=kind, idx=idx, si=si, bank=bank):
                v.wait_ge(t_s.h, tcnt[si])
                if kind == 'v':
                    dst = arena.ap()[:, O_V65 + (p % 2) * TC * 130 + idx * 130:
                                     O_V65 + (p % 2) * TC * 130 + (idx + 1) * 130]
                    v_s.inc(v.tensor_copy(
                        dst.rearrange("p (h j) -> p h j", h=2, j=65)[:, :, 0:64],
                        PB(bank)[:, 0:128].rearrange("p (h j) -> p h j", h=2, j=64)))
                else:
                    base = QB[0] if kind == 'q' else KB[0]
                    hoff = 0 if kind == 'q' else 2048
                    v.wait_ge(t_s.h, hend_ref['h0'])
                    v_s.inc(v.tensor_copy(
                        arena[0:64, base + idx * 512: base + (idx + 1) * 512],
                        PB(bank)[0:64, :]))
                    v.wait_ge(d_sy.h, odd_dma)
                    v_s.inc(v.tensor_copy(
                        hstg[64:128, hoff + idx * 512: hoff + (idx + 1) * 512],
                        PB(bank)[64:128, :]))
            jobs_t.append(tjob)
            jobs_v.append(vjob)
        nops = [1 if kind == 'v' else 2 for (kind, idx) in seq]
        return jobs_t, jobs_v, nops

    def emit_odd_dmas(pair_end_t):
        @blk.sync
        def _(sy):
            sy.wait_ge(t_s.h, pair_end_t)
            sy.wait_ge(v_s.h, v_s.n)
            d_sy.inc(sy.dma_start(out=arena[0:64, QB[1]:QB[1] + T],
                                  in_=hstg[64:128, 0:T]), 16)
            d_sy.inc(sy.dma_start(out=arena[0:64, KB[1]:KB[1] + T],
                                  in_=hstg[64:128, 2048:2048 + T]), 16)
        return d_sy.n

    def emit_norms(p, h, data_wait):
        res = {}
        @blk.vector
        def _(v):
            v.wait_ge(*data_wait)
            v_s.inc(v.tensor_mul(sqbf[0:64, :], arena[0:64, QB[h]:QB[h] + T],
                                 arena[0:64, QB[h]:QB[h] + T]))
        res['qsq'] = v_s.n
        @blk.tensor
        def _(t):
            t.wait_ge(v_s.h, res['qsq'])
            last = None
            for tcx in range(TC):
                last = t.matmul(PB(7)[:, tcx:tcx + 1],
                                sqbf[0:64, tcx * 128:(tcx + 1) * 128],
                                ones64[0:64, :], start=True, stop=True)
            t_s.inc(last)
        res['nq'] = t_s.n
        @blk.vector
        def _(v):
            v.wait_ge(t_s.h, res['nq'])
            v_s.inc(v.tensor_mul(sqbf[0:64, :], arena[0:64, KB[h]:KB[h] + T],
                                 arena[0:64, KB[h]:KB[h] + T]))
        res['ksq'] = v_s.n
        @blk.tensor
        def _(t):
            t.wait_ge(v_s.h, res['ksq'])
            last = None
            for tcx in range(TC):
                last = t.matmul(PB(7)[:, TC + tcx:TC + tcx + 1],
                                sqbf[0:64, tcx * 128:(tcx + 1) * 128],
                                ones64[0:64, :], start=True, stop=True)
            t_s.inc(last)
        res['nk'] = t_s.n
        @blk.vector
        def _(v):
            v.wait_ge(t_s.h, res['nk'])
            v_s.inc(v.reduce_max(mtmp[:, 0:1], PB(7)[:, TC:2 * TC], axis=AX))
        res['kred'] = v_s.n
        @blk.sync
        def _(sy):
            sy.wait_ge(v_s.h, res['kred'])
            d_sy.inc(sy.dma_start(out=mcol[0:1, 0:128], in_=mtmp[:, 0:1],
                                  allow_slow_non_contiguous=True), 16)
        res['coll'] = d_sy.n
        @blk.vector
        def _(v):
            v.wait_ge(d_sy.h, res['coll'])
            v_s.inc(v.reduce_max(mrow[0:1, 0:1], mcol[0:1, 0:128], axis=AX))
            v.wait_ge(v_s.h, v_s.n)
            v_s.inc(v.tensor_copy(mrowb[:], mrow[0:1, 0:1]))
        res['mkb'] = v_s.n
        @blk.tensor
        def _(t):
            t.wait_ge(v_s.h, res['mkb'])
            t_s.inc(t.matmul(PB(7)[:, 40:41], onesrow[:], mrowb[:],
                             start=True, stop=True))
        res['bc'] = t_s.n
        @blk.vector
        def _(v):
            v.wait_ge(t_s.h, res['bc'])
            v_s.inc(v.tensor_copy(mtmp[:, 4:5], PB(7)[:, 40:41]))
            v_s.inc(v.tensor_copy(qnf[:, 0:TC], PB(7)[:, 0:TC]))
        res['prep'] = v_s.n
        @blk.scalar
        def _(sc):
            sc.wait_ge(v_s.h, res['prep'])
            s_s.inc(sc.activation(qnf[:, 16:16 + TC], qnf[:, 0:TC], AF.Ln,
                                  scale=mtmp[:, 4:5]))
            s_s.inc(sc.activation(qnf[:, 16:16 + TC], qnf[:, 16:16 + TC],
                                  AF.Exp, scale=0.5))
        res['mq'] = s_s.n
        @blk.vector
        def _(v):
            v.wait_ge(s_s.h, res['mq'])
            v_s.inc(v.tensor_scalar(mqb[:, 0:TC], qnf[:, 16:16 + TC], -1.0,
                                    None, ALU.mult))
        res['neg'] = v_s.n
        @blk.tensor
        def _(t):
            t.wait_ge(v_s.h, res['neg'])
            t_s.inc(t.transpose(PB(7).bitcast(BF16)[0:TC, 256:384],
                                mqb[:, 0:TC], id_b[:]))
        res['tr'] = t_s.n
        @blk.vector
        def _(v):
            v.wait_ge(t_s.h, res['tr'])
            v_s.inc(v.tensor_copy(mqrow[0:TC, :], PB(7).bitcast(BF16)[0:TC, 256:384]))
        res['row'] = v_s.n
        @blk.sync
        def _(sy):
            sy.wait_ge(v_s.h, res['row'])
            d_sy.inc(sy.dma_start(out=arena[64:65, QB[h]:QB[h] + T],
                                  in_=mqrow[0:TC, :]), 16)
        res['dma'] = d_sy.n
        return res['dma']

    pre_base = v_s.n
    nseq = TC + 2 * NQC
    nops_all = [1 if i < TC else 2 for i in range(nseq)]
    odd_prev = [d_sy.n]
    jt0, jv0, nops0 = plan_qkv(0, [pre_base + sum(nops_all[:i + 1])
                                   for i in range(nseq)],
                               {'h0': 0}, d_sy.n)
    @blk.tensor
    def _(t):
        t.wait_ge(v_s.h, wqkvT_done)
        t.wait_ge(v_s.h, ln1_done)
        t.wait_ge(g_s.h, v65_init)
        for j in jt0:
            j(t)
    @blk.vector
    def _(v):
        v.wait_ge(g_s.h, v65_init)
        for j in jv0:
            j(v)
    odd0 = emit_odd_dmas(t_s.n)
    nd0 = emit_norms(0, 0, (v_s.h, v_s.n))
    nd1 = emit_norms(0, 1, (d_sy.h, odd0))
    pair_ready = {0: (t_s.n, v_s.n, s_s.n, nd1)}

    if "qkv0" in dumps:
        h = mkdump("qkv0", [128, 4 * T + TC * 130], BF16)
        @blk.sync
        def _(sy):
            sy.wait_ge(v_s.h, pair_ready[0][1])
            sy.wait_ge(d_sy.h, d_sy.n)
            d_sy.inc(sy.dma_start(out=h[:, 0:4 * T],
                                  in_=arena[:, O_QK65:O_QK65 + 4 * T]), 16)
            d_sy.inc(sy.dma_start(out=h[:, 4 * T:],
                                  in_=arena[:, O_V65:O_V65 + TC * 130]), 16)

    for p in range(NPAIR):
        t0, v0, s0, mdma = pair_ready[p]
        NG2 = NB * KC

        njobs = nseq if p + 1 < NPAIR else 0
        per_b = [[] for _ in range(NB)]
        if njobs:
            for i in range(TC):
                per_b[min(i * QC2 // TC, QC2 - 1)].append(i)
            for i in range(TC, nseq):
                per_b[QC2 + min((i - TC) * QC2 // (2 * NQC), QC2 - 1)].append(i)

        v_base = v_s.n
        nops = nops_all
        evac_done = []
        job_pred = {}
        acc = v_base
        for b in range(NB):
            acc += 2
            evac_done.append(acc)
            for i in per_b[b]:
                job_pred[i] = acc + sum(nops[i2] for i2 in per_b[b]
                                        if i2 <= i)
            acc += sum(nops[i2] for i2 in per_b[b])
        s_base = s_s.n
        hend = {'h0': 0}

        if p + 1 < NPAIR:
            jobs_t, jobs_v, _ = plan_qkv(
                p + 1, [job_pred[i] for i in range(njobs)],
                hend, odd_prev[0])
        else:
            jobs_t, jobs_v = [], []

        qk_cnt, av_cnt = {}, {}

        def emit_av(t, b, kc, h):
            g2 = b * KC + kc
            par = g2 % 2
            qcs = qcs_of(b)
            t.wait_ge(s_s.h, s_base + g2 + 1)
            if kc == 0 and b >= 1:
                t.wait_ge(v_s.h, evac_done[b - 1])
            vb = O_V65 + (p % 2) * TC * 130
            last = None
            for j, qc in enumerate(qcs):
                last = t.matmul(
                    AVP[j][0:65, :],
                    arena[:, vb + kc * 130 + h * 65: vb + kc * 130 + h * 65 + 65],
                    PT[par][:, j * 512:(j + 1) * 512],
                    start=(kc == 0), stop=(kc == KC - 1))
            av_cnt[g2] = t_s.inc(last)

        @blk.tensor
        def _(t, p=p):
            t.wait_ge(v_s.h, v0)
            t.wait_ge(d_sy.h, mdma)
            pend = []
            for b in range(NB):
                h = b // QC2
                qcs = qcs_of(b)
                for kc in range(KC):
                    g2 = b * KC + kc
                    par = g2 % 2
                    if g2 >= 2:
                        t.wait_ge(s_s.h, s_base + (g2 - 2) + 1)
                    elif p > 0:
                        t.wait_ge(s_s.h, s_base)
                    last = None
                    for j, qc in enumerate(qcs):
                        last = t.matmul(
                            SC[par][:, j * 512:(j + 1) * 512],
                            arena[0:65, KB[h] + kc * 128: KB[h] + (kc + 1) * 128],
                            arena[0:65, QB[h] + qc * 512: QB[h] + (qc + 1) * 512],
                            start=True, stop=True)
                    qk_cnt[g2] = t_s.inc(last)
                    if kc >= 1:
                        emit_av(t, b, kc - 1, h)
                    if pend and kc % 2 == 1:
                        jobs_t[pend.pop(0)](t)
                emit_av(t, b, KC - 1, h)
                for i in pend:
                    jobs_t[i](t)
                pend = list(per_b[b])
                if b == QC2 - 1:
                    hend['h0'] = t_s.n
                if b == NB - 1:
                    hend['h1'] = t_s.n
            for i in pend:
                jobs_t[i](t)

        @blk.scalar
        def _(sc, p=p):
            sc.wait_ge(t_s.h, t0)
            for b in range(NB):
                w = 512 * len(qcs_of(b))
                for kc in range(KC):
                    g2 = b * KC + kc
                    par = g2 % 2
                    sc.wait_ge(t_s.h, qk_cnt[g2])
                    if g2 >= 2:
                        sc.wait_ge(t_s.h, av_cnt[g2 - 2])
                    s_s.inc(sc.activation(PT[par][:, 0:w], SC[par][:, 0:w],
                                          AF.Exp, scale=0.125))
        assert s_s.n == s_base + NG2

        for b in range(NB):
            h = b // QC2
            @blk.vector
            def _(v, p=p, b=b, h=h):
                v.wait_ge(t_s.h, av_cnt[b * KC + KC - 1])
                qcs = qcs_of(b)
                w = 512 * len(qcs)
                q0 = qcs[0]
                if h == 0:
                    v_s.inc(v.tensor_copy(
                        aun[0:64, p * T + q0 * 512: p * T + q0 * 512 + w],
                        psum[0:64, 2048:2048 + w]))
                else:
                    v.wait_ge(d_sy.h, h1_dma_all[-1])
                    v_s.inc(v.tensor_copy(h1s[:, 0:w], psum[0:64, 2048:2048 + w]))
                v.wait_ge(d_sy.h, sums_dma_all[-1])
                v_s.inc(v.tensor_scalar(sstg[64:65, 0:w],
                                        psum[64:65, 2048:2048 + w],
                                        SUMSCALE, None, ALU.mult))
                for i in per_b[b]:
                    jobs_v[i](v)
            assert v_s.n == evac_done[b] + sum(nops[i2] for i2 in per_b[b])

            @blk.sync
            def _(sy, p=p, b=b, h=h):
                sy.wait_ge(v_s.h, v_s.n)
                qcs = qcs_of(b)
                w = 512 * len(qcs)
                q0 = qcs[0]
                if h == 1:
                    d_sy.inc(sy.dma_start(
                        out=aun[64:128, p * T + q0 * 512: p * T + q0 * 512 + w],
                        in_=h1s[:, 0:w]), 16)
                    h1_dma_all.append(d_sy.n)
                d_sy.inc(sy.dma_start(
                    out=sums_bf[2 * p + h: 2 * p + h + 1, q0 * 512: q0 * 512 + w],
                    in_=sstg[64:65, 0:w]), 16)
                sums_dma_all.append(d_sy.n)

        if p + 1 < NPAIR:
            odd_n = emit_odd_dmas(hend['h1'])
            odd_prev[0] = odd_n
            ndh0 = emit_norms(p + 1, 0, (v_s.h, v_s.n))
            ndh1 = emit_norms(p + 1, 1, (d_sy.h, odd_n))
            pair_ready[p + 1] = (t_s.n, v_s.n, s_s.n, ndh1)

    attn_t, attn_v = t_s.n, v_s.n

    if "aun" in dumps:
        h = mkdump("aun", [128, NPAIR * T], BF16)
        @blk.sync
        def _(sy):
            sy.wait_ge(v_s.h, attn_v)
            d_sy.inc(sy.dma_start(out=h, in_=aun[:]), 16)
    if "sums" in dumps:
        h = mkdump("sums", [16, T], BF16)
        @blk.sync
        def _(sy):
            sy.wait_ge(d_sy.h, d_sy.n)
            d_sy.inc(sy.dma_start(out=h, in_=sums_bf[:]), 16)

    @blk.vector
    def _(v):
        v.wait_ge(d_sy.h, d_sy.n)
        with nc.allow_low_precision(reason="softmax denominators in bf16; error budget analyzed"):
            v_s.inc(v.reciprocal(rcp_bf[:], sums_bf[:]))
    rcp_done = v_s.n

    rs_pred = v_s.n
    rs_t = {}
    @blk.tensor
    def _(t):
        t.wait_ge(v_s.h, rcp_done)
        for p in range(NPAIR):
            for qc in range(NQC):
                i = p * NQC + qc
                bank = 4 + (i % 2)
                if i >= 2:
                    t.wait_ge(v_s.h, rs_pred + (i - 2) + 1)
                rs_t[i] = t_s.inc(t.matmul(
                    PB(bank), arena[0:16, O_IND + p * 128: O_IND + (p + 1) * 128],
                    rcp_bf[:, qc * 512:(qc + 1) * 512], start=True, stop=True))

    @blk.vector
    def _(v):
        for p in range(NPAIR):
            for qc in range(NQC):
                i = p * NQC + qc
                bank = 4 + (i % 2)
                v.wait_ge(t_s.h, rs_t[i])
                sl = aun[:, p * T + qc * 512: p * T + (qc + 1) * 512]
                v_s.inc(v.scalar_tensor_tensor(sl, PB(bank), SUMSCALE, sl,
                                               ALU.mult, ALU.mult))
    rescale_done = v_s.n
    assert rescale_done == rs_pred + NPAIR * NQC

    wo_dma, wo_sc, wo_tr, wo_ev = [], [], [], []
    for ch in range(8):
        @blk.sync
        def _(sy, ch=ch):
            if ch == 0:
                sy.wait_ge(t_s.h, attn_t)
            else:
                sy.wait_ge(t_s.h, wo_tr[ch - 1])
            d_sy.inc(sy.dma_start(out=stga, in_=wo_h[ch * 128:(ch + 1) * 128, :]), 16)
            wo_dma.append(d_sy.n)
        @blk.vector
        def _(v, ch=ch):
            v.wait_ge(d_sy.h, wo_dma[ch])
            v_s.inc(v.tensor_scalar(stga, stga, coef[:, 16 + ch:16 + ch + 1],
                                    None, ALU.mult))
            wo_sc.append(v_s.n)
        @blk.tensor
        def _(t, ch=ch):
            t.wait_ge(v_s.h, wo_sc[ch])
            if ch >= 1:
                t.wait_ge(v_s.h, wo_ev[ch - 1])
            last = None
            for dc in range(DC):
                bank = 6 + (dc // 4)
                last = t.transpose(PB(bank)[:, (dc % 4) * 128:(dc % 4) * 128 + 128],
                                   stga[:, dc * 128:(dc + 1) * 128], id_f[:])
            t_s.inc(last)
            wo_tr.append(t_s.n)
        @blk.vector
        def _(v, ch=ch):
            v.wait_ge(t_s.h, wo_tr[ch])
            last = None
            for dc in range(DC):
                bank = 6 + (dc // 4)
                last = v.tensor_copy(
                    arena[:, O_WOT + dc * 1024 + ch * 128:
                          O_WOT + dc * 1024 + (ch + 1) * 128],
                    PB(bank)[:, (dc % 4) * 128:(dc % 4) * 128 + 128])
            v_s.inc(last)
            wo_ev.append(v_s.n)
    wot_done = v_s.n

    op_pred = v_s.n
    op_t = {}
    @blk.tensor
    def _(t):
        t.wait_ge(v_s.h, rescale_done)
        t.wait_ge(v_s.h, wot_done)
        for tc in range(TC):
            for ec in range(2):
                i = tc * 2 + ec
                bank = i % 2
                if i >= 2:
                    t.wait_ge(v_s.h, op_pred + (i - 2) + 1)
                last = None
                for pc in range(NPAIR):
                    last = t.matmul(PB(bank),
                                    aun[:, pc * T + tc * 128: pc * T + (tc + 1) * 128],
                                    arena[:, O_WOT + pc * 1024 + ec * 512:
                                          O_WOT + pc * 1024 + (ec + 1) * 512],
                                    start=(pc == 0), stop=(pc == NPAIR - 1))
                op_t[i] = t_s.inc(last)

    @blk.vector
    def _(v):
        for tc in range(TC):
            for ec in range(2):
                i = tc * 2 + ec
                bank = i % 2
                v.wait_ge(t_s.h, op_t[i])
                sl = x_res[:, tc * 1024 + ec * 512: tc * 1024 + (ec + 1) * 512]
                v_s.inc(v.tensor_add(sl, sl, PB(bank)))
    x1_done = v_s.n

    if "x1" in dumps:
        h = mkdump("x1", [128, TC * 1024], BF16)
        @blk.sync
        def _(sy):
            sy.wait_ge(v_s.h, x1_done)
            d_sy.inc(sy.dma_start(out=h, in_=x_res[:]), 16)

    ln_stats(2 * TC, xwait=(v_s.h, x1_done))
    ln2_done = ln_apply(2 * TC, 32, 24, first_wait=[(v_s.h, x1_done)])

    if "h2T" in dumps:
        h = mkdump("h2T", [128, DC * T], BF16)
        @blk.sync
        def _(sy):
            sy.wait_ge(v_s.h, ln2_done)
            d_sy.inc(sy.dma_start(out=h, in_=hT[:]), 16)

    @blk.sync
    def _(sy):
        for dcx in range(DC):
            d_sy.inc(sy.dma_start(
                out=gwf[:, dcx * 8:(dcx + 1) * 8],
                in_=gatew_h[:, dcx * 128:(dcx + 1) * 128].rearrange("e p -> p e"),
                allow_slow_non_contiguous=True), 16)
    gw_dma = d_sy.n

    @blk.vector
    def _(v):
        v.wait_ge(d_sy.h, gw_dma)
        v_s.inc(v.tensor_copy(gwt[:], gwf[:]))
    gwt_done = v_s.n

    @blk.tensor
    def _(t):
        t.wait_ge(v_s.h, gwt_done)
        t.wait_ge(v_s.h, ln2_done)
        last = None
        for tc in range(TC):
            for dc in range(DC):
                last = t.matmul(PB(4)[:, tc * 8:(tc + 1) * 8],
                                hT[:, dc * T + tc * 128: dc * T + (tc + 1) * 128],
                                gwt[:, dc * 8:(dc + 1) * 8],
                                start=(dc == 0), stop=(dc == DC - 1))
        t_s.inc(last)
    lg_done = t_s.n

    @blk.scalar
    def _(sc):
        sc.wait_ge(t_s.h, lg_done)
        s_s.inc(sc.activation(rtr[:, 0:8 * TC], PB(4)[:, 0:8 * TC], AF.Sigmoid))
    sig_done = s_s.n

    @blk.vector
    def _(v):
        v.wait_ge(s_s.h, sig_done)
        v.wait_ge(g_s.h, g_const)
        SCO, SFC, TMP = 0, 128, 256
        for tc in range(TC):
            sco = rtr[:, SCO + tc * 8: SCO + tc * 8 + 8]
            sfc = rtr[:, SFC + tc * 8: SFC + tc * 8 + 8]
            tmp = rtr[:, TMP:TMP + 16]
            v_s.inc(v.tensor_add(sfc, sco, gbias[:]))
            v.wait_ge(v_s.h, v_s.n)
            t2 = rtr.ap()[:, TMP:TMP + 16].rearrange("p (g k) -> p g k", g=2, k=8)
            s2 = rtr.ap()[:, SFC + tc * 8: SFC + tc * 8 + 8].rearrange(
                "p (g k) -> p g k", g=2, k=4)
            pr = [(0, 1), (0, 2), (0, 3), (1, 2), (1, 3), (2, 3)]
            for j, (a, b) in enumerate(pr):
                v_s.inc(v.tensor_add(t2[:, :, j:j + 1], s2[:, :, a:a + 1],
                                     s2[:, :, b:b + 1]))
            v.wait_ge(v_s.h, v_s.n)
            for j in (1, 2, 3, 4, 5):
                v_s.inc(v.tensor_max(t2[:, :, 0:1], t2[:, :, 0:1], t2[:, :, j:j + 1]))
                v.wait_ge(v_s.h, v_s.n)
            v_s.inc(v.tensor_tensor(tmp[:, 12:13], tmp[:, 0:1], tmp[:, 8:9],
                                    ALU.is_ge))
            v.wait_ge(v_s.h, v_s.n)
            v_s.inc(v.tensor_scalar(tmp[:, 13:14], tmp[:, 12:13], -1.0, 1.0,
                                    ALU.mult, ALU.add))
            msk = rtr[:, TMP + 16:TMP + 24]
            v.wait_ge(v_s.h, v_s.n)
            v_s.inc(v.tensor_scalar(msk[:, 0:4], sfc[:, 0:4], tmp[:, 12:13],
                                    None, ALU.mult))
            v_s.inc(v.tensor_scalar(msk[:, 4:8], sfc[:, 4:8], tmp[:, 13:14],
                                    None, ALU.mult))
            v.wait_ge(v_s.h, v_s.n)
            v_s.inc(v.reduce_max(tmp[:, 14:15], msk, axis=AX))
            v.wait_ge(v_s.h, v_s.n)
            ge1 = rtr[:, TMP + 24:TMP + 32]
            v_s.inc(v.tensor_scalar(ge1, msk, tmp[:, 14:15], None, ALU.is_ge))
            v.wait_ge(v_s.h, v_s.n)
            v_s.inc(v.scalar_tensor_tensor(ge1, ge1, -3.0, msk, ALU.mult, ALU.add))
            v.wait_ge(v_s.h, v_s.n)
            v_s.inc(v.reduce_max(tmp[:, 15:16], ge1, axis=AX))
            v.wait_ge(v_s.h, v_s.n)
            sel = rtr[:, TMP + 32:TMP + 40]
            v_s.inc(v.tensor_scalar(sel, msk, tmp[:, 15:16], None, ALU.is_ge))
            v.wait_ge(v_s.h, v_s.n)
            we = rtr[:, TMP + 40:TMP + 48]
            v_s.inc(v.tensor_mul(we, sco, sel))
            v.wait_ge(v_s.h, v_s.n)
            v_s.inc(v.reduce_sum(tmp[:, 14:15], we, axis=AX))
            v.wait_ge(v_s.h, v_s.n)
            v_s.inc(v.tensor_scalar(tmp[:, 14:15], tmp[:, 14:15], 1e-20, None,
                                    ALU.add))
            v.wait_ge(v_s.h, v_s.n)
            v_s.inc(v.reciprocal(tmp[:, 15:16], tmp[:, 14:15]))
            v.wait_ge(v_s.h, v_s.n)
            v_s.inc(v.tensor_scalar(cwsq[:, tc * 8:(tc + 1) * 8], we,
                                    tmp[:, 15:16], 2.5, ALU.mult, ALU.mult))
    cw_done = v_s.n

    if "cw" in dumps:
        h = mkdump("cw", [128, 8 * TC])
        @blk.sync
        def _(sy):
            sy.wait_ge(v_s.h, cw_done)
            d_sy.inc(sy.dma_start(out=h, in_=cwsq[:]), 16)

    prep_state = dict(j=0, g=0, dmac={}, scc={}, gtr={}, jtr={}, evc={})

    def prep(jobs, first_sync_wait=None, first_tensor_wait=None):
        dmac, scc, gtr, jtr, evc = (prep_state['dmac'], prep_state['scc'],
                                    prep_state['gtr'], prep_state['jtr'],
                                    prep_state['evc'])
        s_pre = s_s.n
        g_pre = prep_state['g']
        evacs = []
        for job in jobs:
            j = prep_state['j']
            slot = SLOT[j % 4]
            @blk.sync
            def _(sy, j=j, job=job, slot=slot):
                if j == 0 and first_sync_wait is not None:
                    sy.wait_ge(*first_sync_wait)
                if job.get('gate') is not None:
                    sy.wait_ge(*job['gate'])
                if j >= 4:
                    sy.wait_ge(t_s.h, jtr[j - 4])
                d_sy.inc(sy.dma_start(out=slot[:, 0:job['width']], in_=job['src']), 16)
                dmac[j] = d_sy.n
            if job.get('scales'):
                @blk.vector
                def _(v, j=j, job=job, slot=slot):
                    v.wait_ge(d_sy.h, dmac[j])
                    last = None
                    for (lo, hi, cc) in job['scales']:
                        last = v.tensor_scalar(slot[:, lo:hi], slot[:, lo:hi],
                                               coef[:, cc:cc + 1], None, ALU.mult)
                    v_s.inc(last)
                    scc[j] = v_s.n
            else:
                scc[j] = None
            for (slo, dlo, dstr) in job['groups']:
                g = prep_state['g']
                bank = 6 + (g % 2)
                @blk.tensor
                def _(t, j=j, g=g, bank=bank, slo=slo, slot=slot):
                    if g == 0 and first_tensor_wait is not None:
                        t.wait_ge(*first_tensor_wait)
                    if scc[j] is not None:
                        t.wait_ge(v_s.h, scc[j])
                    else:
                        t.wait_ge(d_sy.h, dmac[j])
                    if g >= 2:
                        t.wait_ge(s_s.h, evc[g - 2])
                    last = None
                    for k in range(4):
                        last = t.transpose(PB(bank)[:, k * 128:(k + 1) * 128],
                                           slot[:, slo + k * 128: slo + (k + 1) * 128],
                                           id_f[:])
                    t_s.inc(last)
                    gtr[g] = t_s.n
                evc[g] = s_pre + (g - g_pre) + 1
                evacs.append((g, bank, dlo, dstr, job['dst']))
                prep_state['g'] = g + 1
            jtr[j] = gtr[prep_state['g'] - 1]
            prep_state['j'] = j + 1

        @blk.scalar
        def _(sc):
            for (g, bank, dlo, dstr, dst) in evacs:
                sc.wait_ge(t_s.h, gtr[g])
                dstt = arena if dst == 'arena' else aun
                full = dstt.ap().rearrange("p (a c) -> p a c", c=dstr)
                a0, c0 = dlo // dstr, dlo % dstr
                dv = full[:, a0:a0 + 4, c0:c0 + 128]
                sv = PB(bank).rearrange("p (g c) -> p g c", c=128)
                s_s.inc(sc.copy(dv, sv))
                assert s_s.n == evc[g]
        return prep_state['j'] - 1

    def shup_jobs(gate=None):
        return [dict(src=shup_h[ch * 128:(ch + 1) * 128, :], width=1024,
                     dst='arena', gate=gate,
                     groups=[(gg * 512, W_SHUP + gg * 4 * SI + ch * 128, SI)
                             for gg in range(2)])
                for ch in range(16)]

    def shdn_jobs(gate=None):
        jobs = []
        for ld in range(16):
            hc, half = ld // 2, ld % 2
            jobs.append(dict(
                src=shdown_h[hc * 128:(hc + 1) * 128, half * 1024:(half + 1) * 1024],
                width=1024, dst=SHD_DST, gate=gate,
                scales=[(0, 1024, 40 + hc)],
                groups=[(gg * 512, F_SHD + (half * 8 + gg * 4) * 1024 + hc * 128, 1024)
                        for gg in range(2)]))
        return jobs

    def up_jobs(e, dstbase, gate=None):
        return [dict(src=upw_h[e, ch * 128:(ch + 1) * 128, :], width=1024,
                     dst='arena', gate=gate,
                     groups=[(gg * 512, dstbase + gg * 4 * 512 + ch * 128, 512)
                             for gg in range(2)])
                for ch in range(4)]

    def down_jobs(e, dstbase, dst, gate=None):
        jobs = []
        for ld in range(4):
            h0 = ld * 2
            jobs.append(dict(
                src=downw_h[e, h0 * 128:(h0 + 2) * 128, :].rearrange(
                    "(c p) i -> p c i", p=128),
                width=1024, dst=dst, gate=gate,
                scales=[(0, 512, 40 + h0), (512, 1024, 40 + h0 + 1)],
                groups=[(c * 512, dstbase + (h0 + c) * 128, 1024)
                        for c in range(2)]))
        return jobs

    if NPAIR * T >= 16384:
        DNA_DST, F_DNA = 'aun', 0
        SHD_DST, F_SHD = 'aun', 0
    else:
        DNA_DST, F_DNA = 'arena', 45056
        SHD_DST, F_SHD = 'arena', 45056 + 16384

    jobs12 = []
    for e in range(4):
        jobs12 += up_jobs(e, W_E_UP[e])
        jobs12 += down_jobs(e, F_DNA + e * 4096, DNA_DST)
    for e in (4, 5):
        jobs12 += up_jobs(e, W_E_UP[e])
        jobs12 += down_jobs(e, W_E_DN[e], 'arena')
    prep(jobs12, first_sync_wait=(v_s.h, x1_done))
    prep12_done = s_s.n

    @blk.scalar
    def _(sc):
        sc.wait_ge(v_s.h, cw_done)
        s_s.inc(sc.activation(cwsq[:], cwsq[:], AF.Sqrt))
    sqcw_done = s_s.n


    prev_dn = [None]

    def ffn_pass(items, wait_first, final):
        outev = []
        NI = len(items)
        s_pass = s_s.n
        all_up, all_tr = {}, {}
        v_bs = {}
        for tc in range(TC):
            s_b = s_pass + tc * NI
            v_b = v_s.n
            v_bs[tc] = v_b
            up_cnt, dn_cnt, tr_cnt = {}, {}, {}
            pdn = prev_dn[0]

            @blk.tensor
            def _(t, tc=tc, s_b=s_b, v_b=v_b):
                if tc == 0:
                    for w in wait_first:
                        t.wait_ge(*w)

                def emit_up(ii):
                    item = items[ii]
                    ubank = 2 + (ii % 2)
                    if ii >= 2:
                        t.wait_ge(s_s.h, s_b + (ii - 2) + 1)
                    elif tc >= 1:
                        t.wait_ge(s_s.h, s_b)
                    last = None
                    for dc in range(DC):
                        if item[0] == 'sh':
                            rhs = arena[:, W_SHUP + dc * SI + item[1] * 512:
                                        W_SHUP + dc * SI + (item[1] + 1) * 512]
                        else:
                            rhs = arena[:, item[2] + dc * 512:
                                        item[2] + (dc + 1) * 512]
                        last = t.matmul(PB(ubank),
                                        hT[:, dc * T + tc * 128: dc * T + (tc + 1) * 128],
                                        rhs, start=(dc == 0), stop=(dc == DC - 1))
                    up_cnt[ii] = t_s.inc(last)

                emit_up(0)
                for ii, item in enumerate(items):
                    tbank = 4 + (ii % 2)
                    if ii + 1 < NI:
                        emit_up(ii + 1)
                    t.wait_ge(v_s.h, v_b + 2 * ii + 1)
                    if ii >= 2:
                        t.wait_ge(v_s.h, v_b + 2 * (ii - 2) + 2)
                    elif tc >= 1:
                        t.wait_ge(v_s.h, v_b)
                    last = None
                    for ic in range(4):
                        last = t.transpose(
                            PB(tbank).bitcast(BF16)[:, ic * 128:(ic + 1) * 128],
                            ffr[:, (2 + ii % 2) * 512 + ic * 128:
                                (2 + ii % 2) * 512 + (ic + 1) * 128], id_b[:])
                    tr_cnt[ii] = t_s.inc(last)
                    t.wait_ge(v_s.h, v_b + 2 * ii + 2)
                    if ii == 0 and tc >= 1:
                        t.wait_ge(v_s.h, outev[tc - 1])
                    last = None
                    for hc in range(2):
                        for ic in range(4):
                            if item[0] == 'sh':
                                shsrc = aun if SHD_DST == 'aun' else arena
                                rhs = shsrc[:, F_SHD + (item[1] * 4 + ic) * 1024 + hc * 512:
                                            F_SHD + (item[1] * 4 + ic) * 1024 + (hc + 1) * 512]
                            else:
                                wsrc = arena if item[4] == 'arena' else aun
                                rhs = wsrc[:, item[3] + ic * 1024 + hc * 512:
                                           item[3] + ic * 1024 + (hc + 1) * 512]
                            last = t.matmul(PB(hc),
                                            ffr[:, (4 + ii % 2) * 512 + ic * 128:
                                                (4 + ii % 2) * 512 + (ic + 1) * 128],
                                            rhs,
                                            start=(ii == 0 and ic == 0),
                                            stop=(ii == NI - 1 and ic == 3))
                    dn_cnt[ii] = t_s.inc(last)

            all_up[tc] = dict(up_cnt)
            all_tr[tc] = dict(tr_cnt)

            @blk.vector
            def _(v, tc=tc, s_b=s_b, pdn=pdn):
                if pdn is not None:
                    v.wait_ge(t_s.h, pdn)
                for ii, item in enumerate(items):
                    v.wait_ge(s_s.h, s_b + ii + 1)
                    rw = ffr[:, (ii % 2) * 512:(ii % 2) * 512 + 512]
                    v_s.inc(v.tensor_mul(
                        ffr[:, (2 + ii % 2) * 512:(2 + ii % 2) * 512 + 512], rw, rw))
                    v.wait_ge(t_s.h, tr_cnt[ii])
                    v_s.inc(v.tensor_copy(
                        ffr[:, (4 + ii % 2) * 512:(4 + ii % 2) * 512 + 512],
                        PB(4 + ii % 2).bitcast(BF16)[:, 0:512]))
            assert v_s.n == v_b + 2 * NI

            @blk.vector
            def _(v, tc=tc):
                v.wait_ge(t_s.h, dn_cnt[NI - 1])
                if final == 'x':
                    for hc in range(2):
                        sl = x_res[:, tc * 1024 + hc * 512: tc * 1024 + (hc + 1) * 512]
                        v_s.inc(v.tensor_add(sl, sl, PB(hc)))
                else:
                    if tc >= 2:
                        v.wait_ge(d_out.h, 16 * (tc - 1))
                    os = arf[:, F_OUTS // 2 + (tc % 2) * 1024:
                             F_OUTS // 2 + (tc % 2) * 1024 + 1024]
                    for hc in range(2):
                        v_s.inc(v.tensor_tensor(
                            os[:, hc * 512:(hc + 1) * 512],
                            x_res[:, tc * 1024 + hc * 512: tc * 1024 + (hc + 1) * 512],
                            PB(hc), ALU.add))
                outev.append(v_s.n)
            prev_dn[0] = dn_cnt[NI - 1]

            if final == 'out':
                @blk.sync
                def _(sy, tc=tc):
                    sy.wait_ge(v_s.h, outev[tc])
                    os = arf[:, F_OUTS // 2 + (tc % 2) * 1024:
                             F_OUTS // 2 + (tc % 2) * 1024 + 1024]
                    d_out.inc(sy.dma_start(out=out_h[tc * 128:(tc + 1) * 128, :],
                                           in_=os), 16)

        @blk.scalar
        def _(sc):
            for tc in range(TC):
                for ii, item in enumerate(items):
                    sc.wait_ge(t_s.h, all_up[tc][ii])
                    if ii < 2 and tc >= 1:
                        sc.wait_ge(v_s.h, v_bs[tc])
                    if item[0] == 'e':
                        scale = cwsq[:, tc * 8 + item[1]: tc * 8 + item[1] + 1]
                    else:
                        scale = 1.0
                    s_s.inc(sc.activation(ffr[:, (ii % 2) * 512:(ii % 2) * 512 + 512],
                                          PB(2 + ii % 2), AF.Relu, scale=scale))
        assert s_s.n == s_pass + TC * NI
        return outev[-1]

    itemsR1 = [('e', e, W_E_UP[e], F_DNA + e * 4096, DNA_DST) for e in range(4)]
    sR1 = ffn_pass(itemsR1, [(s_s.h, prep12_done), (s_s.h, sqcw_done)], 'x')
    r1_dn_end = prev_dn[0]

    if "xr1" in dumps:
        hh_ = mkdump("xr1", [128, TC * 1024], BF16)
        @blk.sync
        def _(sy):
            sy.wait_ge(v_s.h, sR1)
            d_sy.inc(sy.dma_start(out=hh_, in_=x_res[:]), 16)

    jobs3 = []
    for e in (6, 7):
        jobs3 += up_jobs(e, W_E_UP[e], gate=(t_s.h, r1_dn_end))
        jobs3 += down_jobs(e, W_E_DN[e], 'arena', gate=(t_s.h, r1_dn_end))
    prep(jobs3)
    prep3_done = s_s.n

    itemsR2a = [('e', e, W_E_UP[e], W_E_DN[e], 'arena') for e in (4, 5)]
    sR2a = ffn_pass(itemsR2a, [(s_s.h, prep12_done), (v_s.h, sR1)], 'x')
    r2a_dn_end = prev_dn[0]

    jobs4 = (shup_jobs(gate=(t_s.h, r2a_dn_end))
             + shdn_jobs(gate=(t_s.h, r2a_dn_end)))
    prep(jobs4)
    prep4_done = s_s.n

    itemsR2b = [('e', e, W_E_UP[e], W_E_DN[e], 'arena') for e in (6, 7)]
    sR2b = ffn_pass(itemsR2b, [(s_s.h, prep3_done), (v_s.h, sR2a)], 'x')

    if "x2" in dumps:
        hh_ = mkdump("x2", [128, TC * 1024], BF16)
        @blk.sync
        def _(sy):
            sy.wait_ge(v_s.h, sR2b)
            d_sy.inc(sy.dma_start(out=hh_, in_=x_res[:]), 16)

    itemsS = [('sh', sic) for sic in range(4)]
    sS = ffn_pass(itemsS, [(s_s.h, prep4_done), (v_s.h, sR2b)], 'out')

    @blk.sync
    def _(sy):
        sy.wait_ge(d_out.h, 16 * TC)

    st.close()
    return nc, dmp


from concourse.bass_utils import run_bass_kernel_spmd


def kernel(L0_attn_x, L0_attn_w_qkv, L0_attn_w_o, L0_ffn_gate_weight,
           L0_ffn_gate_bias, L0_ffn_up_weights, L0_ffn_down_weights,
           L0_ffn_shared_up_weight, L0_ffn_shared_down_weight, L0_c, L0_wmod):
    B, T, Dm = L0_attn_x.shape
    assert (B, T, Dm) == (8, 2048, 1024)
    nc, _ = build(T=T)
    shared = dict(
        w_qkv=np.ascontiguousarray(L0_attn_w_qkv),
        w_o=np.ascontiguousarray(L0_attn_w_o),
        gate_w=np.ascontiguousarray(L0_ffn_gate_weight),
        gate_b=np.ascontiguousarray(L0_ffn_gate_bias),
        up_w=np.ascontiguousarray(L0_ffn_up_weights),
        down_w=np.ascontiguousarray(L0_ffn_down_weights),
        shup_w=np.ascontiguousarray(L0_ffn_shared_up_weight),
        shdown_w=np.ascontiguousarray(L0_ffn_shared_down_weight),
        wmod=np.ascontiguousarray(L0_wmod),
    )
    in_maps = []
    for i in range(8):
        m = dict(shared)
        m["x"] = np.ascontiguousarray(L0_attn_x[i])
        m["c"] = np.ascontiguousarray(L0_c[i])
        in_maps.append(m)
    res = run_bass_kernel_spmd(nc, in_maps, core_ids=list(range(8)))
    return np.stack([res.results[i]["out"] for i in range(8)], axis=0)


# version 10
# speedup vs baseline: 1.4595x; 1.0329x over previous
import numpy as np
from contextlib import ExitStack

import concourse.bass as bass
import concourse.mybir as mybir

F32 = mybir.dt.float32
BF16 = mybir.dt.bfloat16
AF = mybir.ActivationFunctionType
ALU = mybir.AluOpType
AX = mybir.AxisListType.X

D = 1024
DC = 8
NPAIR = 8
E = 8
I_FF = 512
SI = 2048
LN_EPS = 1e-5


class Sem:
    def __init__(self, stack, nc, name):
        self.h = stack.enter_context(nc.semaphore(name))
        self.n = 0

    def inc(self, inst, k=1):
        inst.then_inc(self.h, k)
        self.n += k
        return self.n


def build(T=2048, dumps=(), detect_races=False):
    TC = T // 128
    NQC = T // 512
    KC = T // 128
    assert T % 512 == 0

    nc = bass.Bass("TRN2", target_bir_lowering=False,
                   detect_race_conditions=detect_races)

    x_h = nc.dram_tensor("x", [T, D], F32, kind="ExternalInput").ap()
    wqkv_h = nc.dram_tensor("w_qkv", [3 * D, D], F32, kind="ExternalInput").ap()
    wo_h = nc.dram_tensor("w_o", [D, D], F32, kind="ExternalInput").ap()
    gatew_h = nc.dram_tensor("gate_w", [E, D], F32, kind="ExternalInput").ap()
    gateb_h = nc.dram_tensor("gate_b", [E], F32, kind="ExternalInput").ap()
    upw_h = nc.dram_tensor("up_w", [E, I_FF, D], F32, kind="ExternalInput").ap()
    downw_h = nc.dram_tensor("down_w", [E, D, I_FF], F32, kind="ExternalInput").ap()
    shup_h = nc.dram_tensor("shup_w", [SI, D], F32, kind="ExternalInput").ap()
    shdown_h = nc.dram_tensor("shdown_w", [D, SI], F32, kind="ExternalInput").ap()
    c_h = nc.dram_tensor("c", [D], F32, kind="ExternalInput").ap()
    wmod_h = nc.dram_tensor("wmod", [D, 6 * D], F32, kind="ExternalInput").ap()
    out_h = nc.dram_tensor("out", [T, D], F32, kind="ExternalOutput").ap()

    dmp = {}
    def mkdump(name, shape, dtype=F32):
        if name in dumps:
            dmp[name] = nc.dram_tensor("dmp_" + name, list(shape), dtype,
                                       kind="ExternalOutput").ap()
        return dmp.get(name)

    st = ExitStack()
    sbt = lambda name, shape, dt: st.enter_context(nc.sbuf_tensor(name, shape, dt))

    x_res = sbt("x_res", [128, TC * 1024], BF16)
    hT = sbt("hT", [128, DC * T], BF16)
    aun = sbt("aun", [128, NPAIR * T], BF16)
    ARN = 51200 if NPAIR * T >= 16384 else 81920
    arena = sbt("arena", [128, ARN], BF16)
    arf = arena.ap().bitcast(F32)

    modT = sbt("modT", [128, 48], F32)
    coef = sbt("coef", [128, 48], F32)
    lnst = sbt("lnst", [128, 4 * TC], F32)
    stat6 = sbt("stat6", [128, 16], F32)
    sil_c = sbt("sil_c", [128, 8], BF16)
    sil_f = sbt("sil_f", [128, 16], F32)
    id_f = sbt("id_f", [128, 128], F32)
    id_b = sbt("id_b", [128, 128], BF16)
    ones64 = sbt("ones64", [128, 1], BF16)
    gwt = sbt("gwt", [128, 64], BF16)
    gwf = sbt("gwf", [128, 64], F32)
    gbias = sbt("gbias", [128, 8], F32)
    gb1 = sbt("gb1", [1, 8], F32)
    rtr = sbt("rtr", [128, 384], F32)
    cwsq = sbt("cwsq", [128, 8 * TC], F32)
    negm = sbt("negm", [128, 4], F32)
    mtmp = sbt("mtmp", [128, 8], F32)
    mcol = sbt("mcol", [1, 256], F32)
    mrow = sbt("mrow", [1, 8], F32)
    mrowb = sbt("mrowb", [1, 1], BF16)
    gb1b = sbt("gb1b", [1, 8], BF16)
    onesrow = sbt("onesrow", [1, 128], BF16)
    ffr = sbt("ffr", [128, 6 * 512], BF16)

    O_WQKVT = 0
    O_WOT = 16384
    O_QK65 = 24576
    O_V65 = O_QK65 + 4 * T
    O_PT = O_V65 + 2 * TC * 130
    O_SUM = O_PT + 2048
    O_RCP = O_SUM
    O_SSTG = O_SUM + T
    O_H1S = O_SSTG + 1024
    O_IND = O_H1S + 1024
    O_MQR = O_IND + 1024
    O_HSTG = O_MQR + 256
    O_STGB = O_HSTG + 4096
    assert O_STGB + 2048 <= ARN, O_STGB

    P_WMF = 24576
    P_WMB = P_WMF + 12288
    P_WQS = P_WMB + 6144
    assert P_WQS + 4096 <= ARN

    W_E_UP = {0: 8192, 1: 12288, 2: 16384, 3: 20480,
              4: 24576, 5: 28672, 6: 8192, 7: 12288}
    W_E_DN = {4: 32768, 5: 36864, 6: 16384, 7: 20480}
    W_SHUP = 24576
    F_OUTS = 40960

    stgb = arf[:, O_STGB // 2: O_STGB // 2 + 1024]
    stga = stgb
    SLOT = [arf[:, i * 1024:(i + 1) * 1024] for i in range(4)]
    wmf2 = [arf[:, P_WMF // 2: P_WMF // 2 + 3072],
            arf[:, P_WMF // 2 + 3072: P_WMF // 2 + 6144]]
    wmb2 = [arena[:, P_WMB:P_WMB + 3072],
            arena[:, P_WMB + 3072:P_WMB + 6144]]
    wqs = [arf[:, P_WQS // 2 + i * 1024: P_WQS // 2 + (i + 1) * 1024] for i in (0, 1)]
    hpre = [stgb[:, 0:1024], arf[:, P_WMF // 2: P_WMF // 2 + 1024]]

    psum = st.enter_context(nc.psum_tensor("psum", [128, 4096], F32))
    PB = lambda b: psum[:, b * 512:(b + 1) * 512]
    SC = [psum[:, 0:1024], psum[:, 1024:2048]]
    AVP = [psum[:, 2048:2560], psum[:, 2560:3072]]

    t_s = Sem(st, nc, "t_s")
    v_s = Sem(st, nc, "v_s")
    s_s = Sem(st, nc, "s_s")
    g_s = Sem(st, nc, "g_s")
    d_sy = Sem(st, nc, "d_sy")
    d_sc = Sem(st, nc, "d_sc")
    d_gp = Sem(st, nc, "d_gp")
    d_out = Sem(st, nc, "d_out")

    blk = st.enter_context(nc.Block())

    @blk.scalar
    def _(sc):
        d_sc.inc(sc.dma_start(out=rtr[0:8, 0:128],
                              in_=c_h.rearrange("(j p) -> j p", p=128)), 16)
        d_sc.inc(sc.dma_start(out=gb1[:], in_=gateb_h[None, :]), 16)
    ld_c = d_sc.n

    @blk.gpsimd
    def _(g):
        d_gp.inc(g.dma_start(out=x_res[:],
                             in_=x_h.rearrange("(t p) d -> p t d", p=128)), 16)
    ld_x = d_gp.n

    @blk.gpsimd
    def _(g):
        g_s.inc(g.memset(id_f[:], 0.0))
        g_s.inc(g.memset(id_b[:], 0.0))
        g_s.inc(g.memset(ones64[:], 1.0))
        g.wait_ge(g_s.h, g_s.n)
        g_s.inc(g.affine_select(out=id_f[:], in_=id_f[:], compare_op=ALU.not_equal,
                                fill=1.0, base=0, pattern=[[-1, 128]],
                                channel_multiplier=1))
        g_s.inc(g.affine_select(out=id_b[:], in_=id_b[:], compare_op=ALU.not_equal,
                                fill=1.0, base=0, pattern=[[-1, 128]],
                                channel_multiplier=1))
        g_s.inc(g.memset(onesrow[:], 1.0))
    g_const = g_s.n

    @blk.vector
    def _(v):
        v.wait_ge(d_sc.h, ld_c)
        v_s.inc(v.tensor_copy(gb1b[:], gb1[:]))
    gb_cast = v_s.n

    @blk.tensor
    def _(t):
        t.wait_ge(v_s.h, gb_cast)
        t.wait_ge(g_s.h, g_const)
        t_s.inc(t.matmul(PB(7)[:, 500:508], onesrow[:], gb1b[:],
                         start=True, stop=True))
    gb_mm = t_s.n

    @blk.vector
    def _(v):
        v.wait_ge(t_s.h, gb_mm)
        v_s.inc(v.tensor_copy(gbias[:], PB(7)[:, 500:508]))
    gb_done = v_s.n

    @blk.tensor
    def _(t):
        t.wait_ge(d_sc.h, ld_c)
        t.wait_ge(g_s.h, g_const)
        t_s.inc(t.transpose(PB(7)[:, 120:128], rtr[0:8, 0:128], id_f[0:8, 0:8]))
    ctr_mm = t_s.n

    @blk.scalar
    def _(sc):
        sc.wait_ge(t_s.h, ctr_mm)
        s_s.inc(sc.activation(sil_f[:, 8:16], PB(7)[:, 120:128], AF.Silu))
    sil_done = s_s.n

    @blk.vector
    def _(v):
        v.wait_ge(s_s.h, sil_done)
        v_s.inc(v.tensor_copy(sil_c[:], sil_f[:, 8:16]))
    silc_done = v_s.n

    wm_dma, wm_cast, wm_mm = {}, {}, {}

    def emit_wm(hc):
        half, dc = hc // DC, hc % DC
        slot = hc % 2

        @blk.sync
        def _(sy, hc=hc, dc=dc, half=half, slot=slot):
            if hc >= 2:
                sy.wait_ge(v_s.h, wm_cast[hc - 2])
            d_sy.inc(sy.dma_start(
                out=wmf2[slot][:],
                in_=wmod_h[dc * 128:(dc + 1) * 128,
                           half * 3072:(half + 1) * 3072]), 16)
            wm_dma[hc] = (d_sy.h, d_sy.n)

        @blk.vector
        def _(v, hc=hc, slot=slot):
            v.wait_ge(*wm_dma[hc])
            if hc >= 2:
                v.wait_ge(t_s.h, wm_mm[hc - 2])
            v_s.inc(v.tensor_copy(wmb2[slot][:], wmf2[slot][:]))
            wm_cast[hc] = v_s.n

        @blk.tensor
        def _(t, hc=hc, dc=dc, half=half, slot=slot):
            t.wait_ge(v_s.h, wm_cast[hc])
            if hc == 0:
                t.wait_ge(v_s.h, silc_done)
            last = None
            for nx in range(24):
                ncx = half * 24 + nx
                last = t.matmul(psum[:, ncx * 8 + dc:ncx * 8 + dc + 1],
                                wmb2[slot][:, nx * 128:(nx + 1) * 128],
                                sil_c[:, dc:dc + 1],
                                start=True, stop=True)
            t_s.inc(last)
            wm_mm[hc] = t_s.n

    for hc in range(DC):
        emit_wm(hc)

    def emit_coef_half(mm_end, m0, clo):
        @blk.vector
        def _(v):
            v.wait_ge(t_s.h, mm_end)
            v_s.inc(v.tensor_reduce(
                modT[:, m0:m0 + 24],
                psum.ap()[:, m0 * 8:(m0 + 24) * 8].rearrange(
                    "p (n d) -> p n d", d=8),
                axis=AX, op=ALU.add))
        modTh_done = v_s.n

        @blk.vector
        def _(v):
            v.wait_ge(v_s.h, modTh_done)
            v_s.inc(v.tensor_copy(coef[:, clo:clo + 8], modT[:, m0:m0 + 8]))
            v_s.inc(v.tensor_scalar(coef[:, clo + 8:clo + 16],
                                    modT[:, m0 + 8:m0 + 16], 1.0, None, ALU.add))

        @blk.scalar
        def _(sc):
            sc.wait_ge(v_s.h, modTh_done)
            s_s.inc(sc.activation(coef[:, clo + 16:clo + 24],
                                  modT[:, m0 + 16:m0 + 24], AF.Abs))
        gabs_done = s_s.n

        @blk.vector
        def _(v):
            v.wait_ge(s_s.h, gabs_done)
            src = clo + 16
            v_s.inc(v.tensor_scalar(stat6[:, 0:8], modT[:, m0 + 16:m0 + 24], 0.0,
                                    None, ALU.is_ge))
            v.wait_ge(v_s.h, v_s.n)
            v_s.inc(v.tensor_scalar(stat6[:, 8:16], stat6[:, 0:8], 2.0, -1.0,
                                    ALU.mult, ALU.add))
            v_s.inc(v.tensor_scalar(coef[:, src:src + 8], coef[:, src:src + 8],
                                    0.25, None, ALU.max))
            v.wait_ge(v_s.h, v_s.n)
            v_s.inc(v.tensor_mul(coef[:, src:src + 8], coef[:, src:src + 8],
                                 stat6[:, 8:16]))
        return v_s.n

    coefA_done = emit_coef_half(wm_mm[DC - 1], 0, 0)

    def ln_stats_tc(lnbase, tc, xwait=None):
        @blk.vector
        def _(v, tc=tc):
            if xwait is not None:
                v.wait_ge(*xwait)
            xs = x_res[:, tc * 1024:(tc + 1) * 1024]
            v_s.inc(v.bn_stats(stat6[:, 0:6], xs[:, 0:512]))
            v_s.inc(v.bn_stats(stat6[:, 6:12], xs[:, 512:1024]))
            v.wait_ge(v_s.h, v_s.n)
            v_s.inc(v.bn_aggr(stat6[:, 12:14], stat6[:, 0:12]))
            v.wait_ge(v_s.h, v_s.n)
            v_s.inc(v.tensor_copy(lnst[:, lnbase + TC + tc:lnbase + TC + tc + 1],
                                  stat6[:, 12:13]))
            v_s.inc(v.tensor_scalar(stat6[:, 14:15], stat6[:, 13:14], LN_EPS,
                                    None, ALU.add))
            v.wait_ge(v_s.h, v_s.n)
            v_s.inc(v.reciprocal(lnst[:, lnbase + tc:lnbase + tc + 1],
                                 stat6[:, 14:15]))

    def ln_stats_finish(lnbase):
        rec_done = v_s.n

        @blk.scalar
        def _(sc):
            sc.wait_ge(v_s.h, rec_done)
            s_s.inc(sc.activation(lnst[:, lnbase:lnbase + TC],
                                  lnst[:, lnbase:lnbase + TC], AF.Sqrt))
        sq_done = s_s.n

        @blk.vector
        def _(v):
            v.wait_ge(s_s.h, sq_done)
            v_s.inc(v.tensor_mul(lnst[:, lnbase + TC:lnbase + 2 * TC],
                                 lnst[:, lnbase + TC:lnbase + 2 * TC],
                                 lnst[:, lnbase:lnbase + TC]))
            v.wait_ge(v_s.h, v_s.n)
            v_s.inc(v.tensor_scalar(lnst[:, lnbase + TC:lnbase + 2 * TC],
                                    lnst[:, lnbase + TC:lnbase + 2 * TC],
                                    -1.0, None, ALU.mult))

    def ln_stats(lnbase, xwait=None, interleave=None):
        for tc in range(TC):
            if interleave is not None:
                interleave(tc)
            ln_stats_tc(lnbase, tc, xwait=(xwait if tc == 0 else None))
        ln_stats_finish(lnbase)

    def ln_apply(lnbase, cA, cB, coefw, first_wait=(), interleave=None):
        tr_done = []
        s_pre = s_s.n
        for tc in range(TC):
            slot = tc % 2
            if interleave is not None:
                interleave(tc)
            @blk.vector
            def _(v, tc=tc, slot=slot):
                if tc == 0:
                    v.wait_ge(v_s.h, coefw)
                    for w in first_wait:
                        v.wait_ge(*w)
                if tc >= 2:
                    v.wait_ge(t_s.h, tr_done[tc - 2])
                rs = lnst[:, lnbase + tc:lnbase + tc + 1]
                nm = lnst[:, lnbase + TC + tc:lnbase + TC + tc + 1]
                v_s.inc(v.tensor_scalar(hpre[slot],
                                        x_res[:, tc * 1024:(tc + 1) * 1024],
                                        rs, nm, ALU.mult, ALU.add))
            ap_done = v_s.n

            @blk.tensor
            def _(t, tc=tc, slot=slot, ap_done=ap_done):
                t.wait_ge(v_s.h, ap_done)
                if tc >= 2:
                    t.wait_ge(s_s.h, s_pre + (tc - 2) + 1)
                last = None
                for dc in range(DC):
                    bank = 2 + 2 * (tc % 2) + (dc // 4)
                    last = t.transpose(
                        PB(bank)[:, (dc % 4) * 128:(dc % 4) * 128 + 128],
                        hpre[slot][:, dc * 128:(dc + 1) * 128], id_f[:])
                t_s.inc(last)
                tr_done.append(t_s.n)

        @blk.scalar
        def _(sc):
            for tc in range(TC):
                sc.wait_ge(t_s.h, tr_done[tc])
                last = None
                for dc in range(DC):
                    bank = 2 + 2 * (tc % 2) + (dc // 4)
                    last = sc.activation(
                        hT[:, dc * T + tc * 128: dc * T + (tc + 1) * 128],
                        PB(bank)[:, (dc % 4) * 128:(dc % 4) * 128 + 128],
                        AF.Identity,
                        scale=coef[:, cA + dc:cA + dc + 1],
                        bias=coef[:, cB + dc:cB + dc + 1])
                s_s.inc(last)
                assert s_s.n == s_pre + tc + 1
        return s_s.n

    def wmB_interleave(tc):
        hcb = DC + tc // 2
        if tc % 2 == 0 and hcb < 2 * DC:
            emit_wm(hcb)

    ln_stats(0, xwait=(d_gp.h, ld_x), interleave=wmB_interleave)
    for hc in range(min(2 * DC, DC + (TC + 1) // 2), 2 * DC):
        emit_wm(hc)
    coefB_done = emit_coef_half(wm_mm[2 * DC - 1], 24, 24)

    wq_dma, wq_tr, wq_ev = [], [], []

    def emit_wq(ch):
        slot = ch % 2
        @blk.sync
        def _(sy, ch=ch, slot=slot):
            if ch >= 2:
                sy.wait_ge(t_s.h, wq_tr[ch - 2])
            d_sy.inc(sy.dma_start(out=wqs[slot],
                                  in_=wqkv_h[ch * 128:(ch + 1) * 128, :]), 16)
            wq_dma.append(d_sy.n)

        @blk.tensor
        def _(t, ch=ch, slot=slot):
            t.wait_ge(d_sy.h, wq_dma[ch])
            if ch >= 1:
                t.wait_ge(v_s.h, wq_ev[ch - 1])
            last = None
            for dc in range(DC):
                bank = 6 + (dc // 4)
                last = t.transpose(PB(bank)[:, (dc % 4) * 128:(dc % 4) * 128 + 128],
                                   wqs[slot][:, dc * 128:(dc + 1) * 128], id_f[:])
            t_s.inc(last)
            wq_tr.append(t_s.n)

        @blk.vector
        def _(v, ch=ch):
            v.wait_ge(t_s.h, wq_tr[ch])
            last = None
            for dc in range(DC):
                bank = 6 + (dc // 4)
                last = v.tensor_copy(
                    arena[:, O_WQKVT + dc * 3072 + ch * 128:
                          O_WQKVT + dc * 3072 + (ch + 1) * 128],
                    PB(bank)[:, (dc % 4) * 128:(dc % 4) * 128 + 128])
            v_s.inc(last)
            wq_ev.append(v_s.n)

    wq_next = [0]

    def wq_interleave(tc):
        n = 2 if tc < TC // 2 else 1
        for _i in range(n):
            if wq_next[0] < 24:
                emit_wq(wq_next[0])
                wq_next[0] += 1

    ln1_done = ln_apply(0, 8, 0, coefA_done, interleave=wq_interleave)
    while wq_next[0] < 24:
        emit_wq(wq_next[0])
        wq_next[0] += 1
    wqkvT_done = wq_ev[-1]

    if "hT" in dumps:
        h = mkdump("hT", [128, DC * T], BF16)
        @blk.sync
        def _(sy):
            sy.wait_ge(s_s.h, ln1_done)
            d_sy.inc(sy.dma_start(out=h, in_=hT[:]), 16)

    @blk.gpsimd
    def _(g):
        g.wait_ge(t_s.h, wq_tr[-1])
        g.wait_ge(t_s.h, wm_mm[2 * DC - 1])
        ap = arena.ap()[:, O_V65:O_V65 + 2 * TC * 130]
        g_s.inc(g.memset(ap.rearrange("p (b t h j) -> p b t h j", b=2, t=TC, h=2)
                         [:, :, :, :, 64:65], 1.0))
        g_s.inc(g.memset(arena[64:65, 24576 + 2 * T:24576 + 4 * T], 1.0))
        g_s.inc(g.memset(arena[0:16, O_IND:O_IND + 1024], 0.0))
        g.wait_ge(g_s.h, g_s.n)
        g_s.inc(g.affine_select(
            out=arena.ap()[0:16, O_IND:O_IND + 1024].rearrange(
                "p (a h j) -> p a h j", h=2, j=64),
            in_=arena.ap()[0:16, O_IND:O_IND + 1024].rearrange(
                "p (a h j) -> p a h j", h=2, j=64),
            compare_op=ALU.not_equal, fill=1.0, base=0,
            pattern=[[-2, 8], [-1, 2], [0, 64]], channel_multiplier=1))
    v65_init = g_s.n

    QB = [O_QK65 + 0 * T, O_QK65 + 1 * T]
    KB = [O_QK65 + 2 * T, O_QK65 + 3 * T]
    QC2 = max(1, NQC // 2)
    NB = 2 * QC2
    def qcs_of(b):
        qc2 = b % QC2
        return [q for q in (2 * qc2, 2 * qc2 + 1) if q < NQC]

    sqbf = arena[:, O_STGB:O_STGB + T]
    mqb = arena[:, O_MQR:O_MQR + 16]
    mqrow = arena[0:16, O_MQR + 128:O_MQR + 256]
    hstg = arena[:, O_HSTG:O_HSTG + 4096]
    qnf = rtr[:, 320:352]
    PT = [arena[:, O_PT:O_PT + 1024], arena[:, O_PT + 1024:O_PT + 2048]]
    sums_bf = arena[0:16, O_SUM:O_SUM + T]
    rcp_bf = arena[0:16, O_RCP:O_RCP + T]
    sstg = arena[:, O_SSTG:O_SSTG + 1024]
    h1s = arena[0:64, O_H1S:O_H1S + 1024]
    h1_dma_all = [0, 0]
    sums_dma_all = [0, 0]
    SUMSCALE = 2097152.0

    def plan_qkv(p, evac_pred, hend_ref, odd_dma):
        tcnt = {}
        seq = ([('v', tc) for tc in range(TC)]
               + [('q', n) for n in range(NQC)] + [('k', n) for n in range(NQC)])
        jobs_t, jobs_v = [], []
        v_at_plan = v_s.n
        for si, (kind, idx) in enumerate(seq):
            bank = 6 + (si % 2)
            def tjob(t, kind=kind, idx=idx, si=si, bank=bank):
                if si >= 2:
                    t.wait_ge(v_s.h, evac_pred[si - 2])
                else:
                    t.wait_ge(v_s.h, v_at_plan)
                last = None
                if kind in ('q', 'k'):
                    ebase = (0 if kind == 'q' else 1024) + p * 128
                    for dc in range(DC):
                        last = t.matmul(
                            PB(bank),
                            arena[:, dc * 3072 + ebase: dc * 3072 + ebase + 128],
                            hT[:, dc * T + idx * 512: dc * T + (idx + 1) * 512],
                            start=(dc == 0), stop=(dc == DC - 1))
                else:
                    ebase = 2048 + p * 128
                    for dc in range(DC):
                        last = t.matmul(
                            PB(bank)[:, 0:128],
                            hT[:, dc * T + idx * 128: dc * T + (idx + 1) * 128],
                            arena[:, dc * 3072 + ebase: dc * 3072 + ebase + 128],
                            start=(dc == 0), stop=(dc == DC - 1))
                t_s.inc(last)
                tcnt[si] = t_s.n
            def vjob(v, kind=kind, idx=idx, si=si, bank=bank):
                v.wait_ge(t_s.h, tcnt[si])
                if kind == 'v':
                    dst = arena.ap()[:, O_V65 + (p % 2) * TC * 130 + idx * 130:
                                     O_V65 + (p % 2) * TC * 130 + (idx + 1) * 130]
                    v_s.inc(v.tensor_copy(
                        dst.rearrange("p (h j) -> p h j", h=2, j=65)[:, :, 0:64],
                        PB(bank)[:, 0:128].rearrange("p (h j) -> p h j", h=2, j=64)))
                else:
                    base = QB[0] if kind == 'q' else KB[0]
                    hoff = 0 if kind == 'q' else 2048
                    v.wait_ge(t_s.h, hend_ref['h0'])
                    v_s.inc(v.tensor_copy(
                        arena[0:64, base + idx * 512: base + (idx + 1) * 512],
                        PB(bank)[0:64, :]))
                    v.wait_ge(d_sy.h, odd_dma)
                    v_s.inc(v.tensor_copy(
                        hstg[64:128, hoff + idx * 512: hoff + (idx + 1) * 512],
                        PB(bank)[64:128, :]))
            jobs_t.append(tjob)
            jobs_v.append(vjob)
        nops = [1 if kind == 'v' else 2 for (kind, idx) in seq]
        return jobs_t, jobs_v, nops

    def emit_odd_dmas(pair_end_t):
        @blk.sync
        def _(sy):
            sy.wait_ge(t_s.h, pair_end_t)
            sy.wait_ge(v_s.h, v_s.n)
            d_sy.inc(sy.dma_start(out=arena[0:64, QB[1]:QB[1] + T],
                                  in_=hstg[64:128, 0:T]), 16)
            d_sy.inc(sy.dma_start(out=arena[0:64, KB[1]:KB[1] + T],
                                  in_=hstg[64:128, 2048:2048 + T]), 16)
        return d_sy.n

    def emit_norms(p, h, data_wait):
        res = {}
        @blk.vector
        def _(v):
            v.wait_ge(*data_wait)
            v_s.inc(v.tensor_mul(sqbf[0:64, :], arena[0:64, QB[h]:QB[h] + T],
                                 arena[0:64, QB[h]:QB[h] + T]))
        res['qsq'] = v_s.n
        @blk.tensor
        def _(t):
            t.wait_ge(v_s.h, res['qsq'])
            last = None
            for tcx in range(TC):
                last = t.matmul(PB(7)[:, tcx:tcx + 1],
                                sqbf[0:64, tcx * 128:(tcx + 1) * 128],
                                ones64[0:64, :], start=True, stop=True)
            t_s.inc(last)
        res['nq'] = t_s.n
        @blk.vector
        def _(v):
            v.wait_ge(t_s.h, res['nq'])
            v_s.inc(v.tensor_mul(sqbf[0:64, :], arena[0:64, KB[h]:KB[h] + T],
                                 arena[0:64, KB[h]:KB[h] + T]))
        res['ksq'] = v_s.n
        @blk.tensor
        def _(t):
            t.wait_ge(v_s.h, res['ksq'])
            last = None
            for tcx in range(TC):
                last = t.matmul(PB(7)[:, TC + tcx:TC + tcx + 1],
                                sqbf[0:64, tcx * 128:(tcx + 1) * 128],
                                ones64[0:64, :], start=True, stop=True)
            t_s.inc(last)
        res['nk'] = t_s.n
        @blk.vector
        def _(v):
            v.wait_ge(t_s.h, res['nk'])
            v_s.inc(v.reduce_max(mtmp[:, 0:1], PB(7)[:, TC:2 * TC], axis=AX))
        res['kred'] = v_s.n
        @blk.sync
        def _(sy):
            sy.wait_ge(v_s.h, res['kred'])
            d_sy.inc(sy.dma_start(out=mcol[0:1, 0:128], in_=mtmp[:, 0:1],
                                  allow_slow_non_contiguous=True), 16)
        res['coll'] = d_sy.n
        @blk.vector
        def _(v):
            v.wait_ge(d_sy.h, res['coll'])
            v_s.inc(v.reduce_max(mrow[0:1, 0:1], mcol[0:1, 0:128], axis=AX))
            v.wait_ge(v_s.h, v_s.n)
            v_s.inc(v.tensor_copy(mrowb[:], mrow[0:1, 0:1]))
        res['mkb'] = v_s.n
        @blk.tensor
        def _(t):
            t.wait_ge(v_s.h, res['mkb'])
            t_s.inc(t.matmul(PB(7)[:, 40:41], onesrow[:], mrowb[:],
                             start=True, stop=True))
        res['bc'] = t_s.n
        @blk.vector
        def _(v):
            v.wait_ge(t_s.h, res['bc'])
            v_s.inc(v.tensor_copy(mtmp[:, 4:5], PB(7)[:, 40:41]))
            v_s.inc(v.tensor_copy(qnf[:, 0:TC], PB(7)[:, 0:TC]))
        res['prep'] = v_s.n
        @blk.scalar
        def _(sc):
            sc.wait_ge(v_s.h, res['prep'])
            s_s.inc(sc.activation(qnf[:, 16:16 + TC], qnf[:, 0:TC], AF.Ln,
                                  scale=mtmp[:, 4:5]))
            s_s.inc(sc.activation(qnf[:, 16:16 + TC], qnf[:, 16:16 + TC],
                                  AF.Exp, scale=0.5))
        res['mq'] = s_s.n
        @blk.vector
        def _(v):
            v.wait_ge(s_s.h, res['mq'])
            v_s.inc(v.tensor_scalar(mqb[:, 0:TC], qnf[:, 16:16 + TC], -1.0,
                                    None, ALU.mult))
        res['neg'] = v_s.n
        @blk.tensor
        def _(t):
            t.wait_ge(v_s.h, res['neg'])
            t_s.inc(t.transpose(PB(7).bitcast(BF16)[0:TC, 256:384],
                                mqb[:, 0:TC], id_b[:]))
        res['tr'] = t_s.n
        @blk.vector
        def _(v):
            v.wait_ge(t_s.h, res['tr'])
            v_s.inc(v.tensor_copy(mqrow[0:TC, :], PB(7).bitcast(BF16)[0:TC, 256:384]))
        res['row'] = v_s.n
        @blk.sync
        def _(sy):
            sy.wait_ge(v_s.h, res['row'])
            d_sy.inc(sy.dma_start(out=arena[64:65, QB[h]:QB[h] + T],
                                  in_=mqrow[0:TC, :]), 16)
        res['dma'] = d_sy.n
        return res['dma']

    pre_base = v_s.n
    nseq = TC + 2 * NQC
    nops_all = [1 if i < TC else 2 for i in range(nseq)]
    odd_prev = [d_sy.n]
    jt0, jv0, nops0 = plan_qkv(0, [pre_base + sum(nops_all[:i + 1])
                                   for i in range(nseq)],
                               {'h0': 0}, d_sy.n)
    @blk.tensor
    def _(t):
        t.wait_ge(v_s.h, wqkvT_done)
        t.wait_ge(s_s.h, ln1_done)
        t.wait_ge(g_s.h, v65_init)
        for j in jt0:
            j(t)
    @blk.vector
    def _(v):
        v.wait_ge(g_s.h, v65_init)
        for j in jv0:
            j(v)
    odd0 = emit_odd_dmas(t_s.n)
    nd0 = emit_norms(0, 0, (v_s.h, v_s.n))
    nd1 = emit_norms(0, 1, (d_sy.h, odd0))
    pair_ready = {0: (t_s.n, v_s.n, s_s.n, nd1)}

    if "qkv0" in dumps:
        h = mkdump("qkv0", [128, 4 * T + TC * 130], BF16)
        @blk.sync
        def _(sy):
            sy.wait_ge(v_s.h, pair_ready[0][1])
            sy.wait_ge(d_sy.h, d_sy.n)
            d_sy.inc(sy.dma_start(out=h[:, 0:4 * T],
                                  in_=arena[:, O_QK65:O_QK65 + 4 * T]), 16)
            d_sy.inc(sy.dma_start(out=h[:, 4 * T:],
                                  in_=arena[:, O_V65:O_V65 + TC * 130]), 16)

    for p in range(NPAIR):
        t0, v0, s0, mdma = pair_ready[p]
        NG2 = NB * KC

        njobs = nseq if p + 1 < NPAIR else 0
        per_b = [[] for _ in range(NB)]
        if njobs:
            for i in range(TC):
                per_b[min(i * QC2 // TC, QC2 - 1)].append(i)
            for i in range(TC, nseq):
                per_b[QC2 + min((i - TC) * QC2 // (2 * NQC), QC2 - 1)].append(i)

        v_base = v_s.n
        nops = nops_all
        evac_done = []
        job_pred = {}
        acc = v_base
        for b in range(NB):
            acc += 2
            evac_done.append(acc)
            for i in per_b[b]:
                job_pred[i] = acc + sum(nops[i2] for i2 in per_b[b]
                                        if i2 <= i)
            acc += sum(nops[i2] for i2 in per_b[b])
        s_base = s_s.n
        hend = {'h0': 0}

        if p + 1 < NPAIR:
            jobs_t, jobs_v, _ = plan_qkv(
                p + 1, [job_pred[i] for i in range(njobs)],
                hend, odd_prev[0])
        else:
            jobs_t, jobs_v = [], []

        qk_cnt, av_cnt = {}, {}

        def emit_av(t, b, kc, h):
            g2 = b * KC + kc
            par = g2 % 2
            qcs = qcs_of(b)
            t.wait_ge(s_s.h, s_base + g2 + 1)
            if kc == 0 and b >= 1:
                t.wait_ge(v_s.h, evac_done[b - 1])
            vb = O_V65 + (p % 2) * TC * 130
            last = None
            for j, qc in enumerate(qcs):
                last = t.matmul(
                    AVP[j][0:65, :],
                    arena[:, vb + kc * 130 + h * 65: vb + kc * 130 + h * 65 + 65],
                    PT[par][:, j * 512:(j + 1) * 512],
                    start=(kc == 0), stop=(kc == KC - 1))
            av_cnt[g2] = t_s.inc(last)

        @blk.tensor
        def _(t, p=p):
            t.wait_ge(v_s.h, v0)
            t.wait_ge(d_sy.h, mdma)
            pend = []
            for b in range(NB):
                h = b // QC2
                qcs = qcs_of(b)
                for kc in range(KC):
                    g2 = b * KC + kc
                    par = g2 % 2
                    if g2 >= 2:
                        t.wait_ge(s_s.h, s_base + (g2 - 2) + 1)
                    elif p > 0:
                        t.wait_ge(s_s.h, s_base)
                    last = None
                    for j, qc in enumerate(qcs):
                        last = t.matmul(
                            SC[par][:, j * 512:(j + 1) * 512],
                            arena[0:65, KB[h] + kc * 128: KB[h] + (kc + 1) * 128],
                            arena[0:65, QB[h] + qc * 512: QB[h] + (qc + 1) * 512],
                            start=True, stop=True)
                    qk_cnt[g2] = t_s.inc(last)
                    if kc >= 1:
                        emit_av(t, b, kc - 1, h)
                    if pend and kc % 2 == 1:
                        jobs_t[pend.pop(0)](t)
                emit_av(t, b, KC - 1, h)
                for i in pend:
                    jobs_t[i](t)
                pend = list(per_b[b])
                if b == QC2 - 1:
                    hend['h0'] = t_s.n
                if b == NB - 1:
                    hend['h1'] = t_s.n
            for i in pend:
                jobs_t[i](t)

        @blk.scalar
        def _(sc, p=p):
            sc.wait_ge(t_s.h, t0)
            for b in range(NB):
                w = 512 * len(qcs_of(b))
                for kc in range(KC):
                    g2 = b * KC + kc
                    par = g2 % 2
                    sc.wait_ge(t_s.h, qk_cnt[g2])
                    if g2 >= 2:
                        sc.wait_ge(t_s.h, av_cnt[g2 - 2])
                    s_s.inc(sc.activation(PT[par][:, 0:w], SC[par][:, 0:w],
                                          AF.Exp, scale=0.125))
        assert s_s.n == s_base + NG2

        for b in range(NB):
            h = b // QC2
            @blk.vector
            def _(v, p=p, b=b, h=h):
                v.wait_ge(t_s.h, av_cnt[b * KC + KC - 1])
                qcs = qcs_of(b)
                w = 512 * len(qcs)
                q0 = qcs[0]
                if h == 0:
                    v_s.inc(v.tensor_copy(
                        aun[0:64, p * T + q0 * 512: p * T + q0 * 512 + w],
                        psum[0:64, 2048:2048 + w]))
                else:
                    v.wait_ge(d_sy.h, h1_dma_all[-1])
                    v_s.inc(v.tensor_copy(h1s[:, 0:w], psum[0:64, 2048:2048 + w]))
                v.wait_ge(d_sy.h, sums_dma_all[-1])
                v_s.inc(v.tensor_scalar(sstg[64:65, 0:w],
                                        psum[64:65, 2048:2048 + w],
                                        SUMSCALE, None, ALU.mult))
                for i in per_b[b]:
                    jobs_v[i](v)
            assert v_s.n == evac_done[b] + sum(nops[i2] for i2 in per_b[b])

            @blk.sync
            def _(sy, p=p, b=b, h=h):
                sy.wait_ge(v_s.h, v_s.n)
                qcs = qcs_of(b)
                w = 512 * len(qcs)
                q0 = qcs[0]
                if h == 1:
                    d_sy.inc(sy.dma_start(
                        out=aun[64:128, p * T + q0 * 512: p * T + q0 * 512 + w],
                        in_=h1s[:, 0:w]), 16)
                    h1_dma_all.append(d_sy.n)
                d_sy.inc(sy.dma_start(
                    out=sums_bf[2 * p + h: 2 * p + h + 1, q0 * 512: q0 * 512 + w],
                    in_=sstg[64:65, 0:w]), 16)
                sums_dma_all.append(d_sy.n)

        if p + 1 < NPAIR:
            odd_n = emit_odd_dmas(hend['h1'])
            odd_prev[0] = odd_n
            ndh0 = emit_norms(p + 1, 0, (v_s.h, v_s.n))
            ndh1 = emit_norms(p + 1, 1, (d_sy.h, odd_n))
            pair_ready[p + 1] = (t_s.n, v_s.n, s_s.n, ndh1)

    attn_t, attn_v = t_s.n, v_s.n

    if "aun" in dumps:
        h = mkdump("aun", [128, NPAIR * T], BF16)
        @blk.sync
        def _(sy):
            sy.wait_ge(v_s.h, attn_v)
            d_sy.inc(sy.dma_start(out=h, in_=aun[:]), 16)
    if "sums" in dumps:
        h = mkdump("sums", [16, T], BF16)
        @blk.sync
        def _(sy):
            sy.wait_ge(d_sy.h, d_sy.n)
            d_sy.inc(sy.dma_start(out=h, in_=sums_bf[:]), 16)

    @blk.vector
    def _(v):
        v.wait_ge(d_sy.h, d_sy.n)
        with nc.allow_low_precision(reason="softmax denominators in bf16; error budget analyzed"):
            v_s.inc(v.reciprocal(rcp_bf[:], sums_bf[:]))
    rcp_done = v_s.n

    rs_pred = v_s.n
    rs_t = {}
    @blk.tensor
    def _(t):
        t.wait_ge(v_s.h, rcp_done)
        for p in range(NPAIR):
            for qc in range(NQC):
                i = p * NQC + qc
                bank = 4 + (i % 2)
                if i >= 2:
                    t.wait_ge(v_s.h, rs_pred + (i - 2) + 1)
                rs_t[i] = t_s.inc(t.matmul(
                    PB(bank), arena[0:16, O_IND + p * 128: O_IND + (p + 1) * 128],
                    rcp_bf[:, qc * 512:(qc + 1) * 512], start=True, stop=True))

    @blk.vector
    def _(v):
        for p in range(NPAIR):
            for qc in range(NQC):
                i = p * NQC + qc
                bank = 4 + (i % 2)
                v.wait_ge(t_s.h, rs_t[i])
                sl = aun[:, p * T + qc * 512: p * T + (qc + 1) * 512]
                v_s.inc(v.scalar_tensor_tensor(sl, PB(bank), SUMSCALE, sl,
                                               ALU.mult, ALU.mult))
    rescale_done = v_s.n
    assert rescale_done == rs_pred + NPAIR * NQC

    wo_dma, wo_sc, wo_tr, wo_ev = [], [], [], []
    for ch in range(8):
        @blk.sync
        def _(sy, ch=ch):
            if ch == 0:
                sy.wait_ge(t_s.h, attn_t)
            else:
                sy.wait_ge(t_s.h, wo_tr[ch - 1])
            d_sy.inc(sy.dma_start(out=stga, in_=wo_h[ch * 128:(ch + 1) * 128, :]), 16)
            wo_dma.append(d_sy.n)
        @blk.vector
        def _(v, ch=ch):
            v.wait_ge(d_sy.h, wo_dma[ch])
            v_s.inc(v.tensor_scalar(stga, stga, coef[:, 16 + ch:16 + ch + 1],
                                    None, ALU.mult))
            wo_sc.append(v_s.n)
        @blk.tensor
        def _(t, ch=ch):
            t.wait_ge(v_s.h, wo_sc[ch])
            if ch >= 1:
                t.wait_ge(v_s.h, wo_ev[ch - 1])
            last = None
            for dc in range(DC):
                bank = 6 + (dc // 4)
                last = t.transpose(PB(bank)[:, (dc % 4) * 128:(dc % 4) * 128 + 128],
                                   stga[:, dc * 128:(dc + 1) * 128], id_f[:])
            t_s.inc(last)
            wo_tr.append(t_s.n)
        @blk.vector
        def _(v, ch=ch):
            v.wait_ge(t_s.h, wo_tr[ch])
            last = None
            for dc in range(DC):
                bank = 6 + (dc // 4)
                last = v.tensor_copy(
                    arena[:, O_WOT + dc * 1024 + ch * 128:
                          O_WOT + dc * 1024 + (ch + 1) * 128],
                    PB(bank)[:, (dc % 4) * 128:(dc % 4) * 128 + 128])
            v_s.inc(last)
            wo_ev.append(v_s.n)
    wot_done = v_s.n

    op_pred = v_s.n
    op_t = {}
    @blk.tensor
    def _(t):
        t.wait_ge(v_s.h, rescale_done)
        t.wait_ge(v_s.h, wot_done)
        for tc in range(TC):
            for ec in range(2):
                i = tc * 2 + ec
                bank = i % 2
                if i >= 2:
                    j = i - 2
                    t.wait_ge(v_s.h, op_pred + (j // 2) * 8 + (j % 2) + 1)
                last = None
                for pc in range(NPAIR):
                    last = t.matmul(PB(bank),
                                    aun[:, pc * T + tc * 128: pc * T + (tc + 1) * 128],
                                    arena[:, O_WOT + pc * 1024 + ec * 512:
                                          O_WOT + pc * 1024 + (ec + 1) * 512],
                                    start=(pc == 0), stop=(pc == NPAIR - 1))
                op_t[i] = t_s.inc(last)

    for tc in range(TC):
        @blk.vector
        def _(v, tc=tc):
            for ec in range(2):
                i = tc * 2 + ec
                bank = i % 2
                v.wait_ge(t_s.h, op_t[i])
                sl = x_res[:, tc * 1024 + ec * 512: tc * 1024 + (ec + 1) * 512]
                v_s.inc(v.tensor_add(sl, sl, PB(bank)))
        ln_stats_tc(2 * TC, tc)
    x1_done = v_s.n

    if "x1" in dumps:
        h = mkdump("x1", [128, TC * 1024], BF16)
        @blk.sync
        def _(sy):
            sy.wait_ge(v_s.h, x1_done)
            d_sy.inc(sy.dma_start(out=h, in_=x_res[:]), 16)

    ln_stats_finish(2 * TC)
    ln2_done = ln_apply(2 * TC, 32, 24, coefB_done,
                        first_wait=[(v_s.h, x1_done)])

    if "h2T" in dumps:
        h = mkdump("h2T", [128, DC * T], BF16)
        @blk.sync
        def _(sy):
            sy.wait_ge(s_s.h, ln2_done)
            d_sy.inc(sy.dma_start(out=h, in_=hT[:]), 16)

    @blk.sync
    def _(sy):
        for dcx in range(DC):
            d_sy.inc(sy.dma_start(
                out=gwf[:, dcx * 8:(dcx + 1) * 8],
                in_=gatew_h[:, dcx * 128:(dcx + 1) * 128].rearrange("e p -> p e"),
                allow_slow_non_contiguous=True), 16)
    gw_dma = d_sy.n

    @blk.vector
    def _(v):
        v.wait_ge(d_sy.h, gw_dma)
        v_s.inc(v.tensor_copy(gwt[:], gwf[:]))
    gwt_done = v_s.n

    @blk.tensor
    def _(t):
        t.wait_ge(v_s.h, gwt_done)
        t.wait_ge(s_s.h, ln2_done)
        last = None
        for tc in range(TC):
            for dc in range(DC):
                last = t.matmul(PB(4)[:, tc * 8:(tc + 1) * 8],
                                hT[:, dc * T + tc * 128: dc * T + (tc + 1) * 128],
                                gwt[:, dc * 8:(dc + 1) * 8],
                                start=(dc == 0), stop=(dc == DC - 1))
        t_s.inc(last)
    lg_done = t_s.n

    @blk.scalar
    def _(sc):
        sc.wait_ge(t_s.h, lg_done)
        s_s.inc(sc.activation(rtr[:, 0:8 * TC], PB(4)[:, 0:8 * TC], AF.Sigmoid))
    sig_done = s_s.n

    @blk.vector
    def _(v):
        v.wait_ge(s_s.h, sig_done)
        v.wait_ge(g_s.h, g_const)
        SCO, SFC, TMP = 0, 128, 256
        for tc in range(TC):
            sco = rtr[:, SCO + tc * 8: SCO + tc * 8 + 8]
            sfc = rtr[:, SFC + tc * 8: SFC + tc * 8 + 8]
            tmp = rtr[:, TMP:TMP + 16]
            v_s.inc(v.tensor_add(sfc, sco, gbias[:]))
            v.wait_ge(v_s.h, v_s.n)
            t2 = rtr.ap()[:, TMP:TMP + 16].rearrange("p (g k) -> p g k", g=2, k=8)
            s2 = rtr.ap()[:, SFC + tc * 8: SFC + tc * 8 + 8].rearrange(
                "p (g k) -> p g k", g=2, k=4)
            pr = [(0, 1), (0, 2), (0, 3), (1, 2), (1, 3), (2, 3)]
            for j, (a, b) in enumerate(pr):
                v_s.inc(v.tensor_add(t2[:, :, j:j + 1], s2[:, :, a:a + 1],
                                     s2[:, :, b:b + 1]))
            v.wait_ge(v_s.h, v_s.n)
            for j in (1, 2, 3, 4, 5):
                v_s.inc(v.tensor_max(t2[:, :, 0:1], t2[:, :, 0:1], t2[:, :, j:j + 1]))
                v.wait_ge(v_s.h, v_s.n)
            v_s.inc(v.tensor_tensor(tmp[:, 12:13], tmp[:, 0:1], tmp[:, 8:9],
                                    ALU.is_ge))
            v.wait_ge(v_s.h, v_s.n)
            v_s.inc(v.tensor_scalar(tmp[:, 13:14], tmp[:, 12:13], -1.0, 1.0,
                                    ALU.mult, ALU.add))
            msk = rtr[:, TMP + 16:TMP + 24]
            v.wait_ge(v_s.h, v_s.n)
            v_s.inc(v.tensor_scalar(msk[:, 0:4], sfc[:, 0:4], tmp[:, 12:13],
                                    None, ALU.mult))
            v_s.inc(v.tensor_scalar(msk[:, 4:8], sfc[:, 4:8], tmp[:, 13:14],
                                    None, ALU.mult))
            v.wait_ge(v_s.h, v_s.n)
            v_s.inc(v.reduce_max(tmp[:, 14:15], msk, axis=AX))
            v.wait_ge(v_s.h, v_s.n)
            ge1 = rtr[:, TMP + 24:TMP + 32]
            v_s.inc(v.tensor_scalar(ge1, msk, tmp[:, 14:15], None, ALU.is_ge))
            v.wait_ge(v_s.h, v_s.n)
            v_s.inc(v.scalar_tensor_tensor(ge1, ge1, -3.0, msk, ALU.mult, ALU.add))
            v.wait_ge(v_s.h, v_s.n)
            v_s.inc(v.reduce_max(tmp[:, 15:16], ge1, axis=AX))
            v.wait_ge(v_s.h, v_s.n)
            sel = rtr[:, TMP + 32:TMP + 40]
            v_s.inc(v.tensor_scalar(sel, msk, tmp[:, 15:16], None, ALU.is_ge))
            v.wait_ge(v_s.h, v_s.n)
            we = rtr[:, TMP + 40:TMP + 48]
            v_s.inc(v.tensor_mul(we, sco, sel))
            v.wait_ge(v_s.h, v_s.n)
            v_s.inc(v.reduce_sum(tmp[:, 14:15], we, axis=AX))
            v.wait_ge(v_s.h, v_s.n)
            v_s.inc(v.tensor_scalar(tmp[:, 14:15], tmp[:, 14:15], 1e-20, None,
                                    ALU.add))
            v.wait_ge(v_s.h, v_s.n)
            v_s.inc(v.reciprocal(tmp[:, 15:16], tmp[:, 14:15]))
            v.wait_ge(v_s.h, v_s.n)
            v_s.inc(v.tensor_scalar(cwsq[:, tc * 8:(tc + 1) * 8], we,
                                    tmp[:, 15:16], 2.5, ALU.mult, ALU.mult))
    cw_done = v_s.n

    if "cw" in dumps:
        h = mkdump("cw", [128, 8 * TC])
        @blk.sync
        def _(sy):
            sy.wait_ge(v_s.h, cw_done)
            d_sy.inc(sy.dma_start(out=h, in_=cwsq[:]), 16)

    prep_state = dict(j=0, g=0, dmac={}, scc={}, gtr={}, jtr={}, evc={})

    def prep(jobs, first_sync_wait=None, first_tensor_wait=None):
        dmac, scc, gtr, jtr, evc = (prep_state['dmac'], prep_state['scc'],
                                    prep_state['gtr'], prep_state['jtr'],
                                    prep_state['evc'])
        s_pre = s_s.n
        g_pre = prep_state['g']
        evacs = []
        for job in jobs:
            j = prep_state['j']
            slot = SLOT[j % 4]
            @blk.sync
            def _(sy, j=j, job=job, slot=slot):
                if j == 0 and first_sync_wait is not None:
                    sy.wait_ge(*first_sync_wait)
                if job.get('gate') is not None:
                    sy.wait_ge(*job['gate'])
                if j >= 4:
                    sy.wait_ge(t_s.h, jtr[j - 4])
                d_sy.inc(sy.dma_start(out=slot[:, 0:job['width']], in_=job['src']), 16)
                dmac[j] = d_sy.n
            if job.get('scales'):
                @blk.vector
                def _(v, j=j, job=job, slot=slot):
                    v.wait_ge(d_sy.h, dmac[j])
                    last = None
                    for (lo, hi, cc) in job['scales']:
                        last = v.tensor_scalar(slot[:, lo:hi], slot[:, lo:hi],
                                               coef[:, cc:cc + 1], None, ALU.mult)
                    v_s.inc(last)
                    scc[j] = v_s.n
            else:
                scc[j] = None
            for (slo, dlo, dstr) in job['groups']:
                g = prep_state['g']
                bank = 6 + (g % 2)
                @blk.tensor
                def _(t, j=j, g=g, bank=bank, slo=slo, slot=slot):
                    if g == 0 and first_tensor_wait is not None:
                        t.wait_ge(*first_tensor_wait)
                    if scc[j] is not None:
                        t.wait_ge(v_s.h, scc[j])
                    else:
                        t.wait_ge(d_sy.h, dmac[j])
                    if g >= 2:
                        t.wait_ge(s_s.h, evc[g - 2])
                    last = None
                    for k in range(4):
                        last = t.transpose(PB(bank)[:, k * 128:(k + 1) * 128],
                                           slot[:, slo + k * 128: slo + (k + 1) * 128],
                                           id_f[:])
                    t_s.inc(last)
                    gtr[g] = t_s.n
                evc[g] = s_pre + (g - g_pre) + 1
                evacs.append((g, bank, dlo, dstr, job['dst']))
                prep_state['g'] = g + 1
            jtr[j] = gtr[prep_state['g'] - 1]
            prep_state['j'] = j + 1

        @blk.scalar
        def _(sc):
            for (g, bank, dlo, dstr, dst) in evacs:
                sc.wait_ge(t_s.h, gtr[g])
                dstt = arena if dst == 'arena' else aun
                full = dstt.ap().rearrange("p (a c) -> p a c", c=dstr)
                a0, c0 = dlo // dstr, dlo % dstr
                dv = full[:, a0:a0 + 4, c0:c0 + 128]
                sv = PB(bank).rearrange("p (g c) -> p g c", c=128)
                s_s.inc(sc.copy(dv, sv))
                assert s_s.n == evc[g]
        return prep_state['j'] - 1

    def shup_jobs(gate=None):
        return [dict(src=shup_h[ch * 128:(ch + 1) * 128, :], width=1024,
                     dst='arena', gate=gate,
                     groups=[(gg * 512, W_SHUP + gg * 4 * SI + ch * 128, SI)
                             for gg in range(2)])
                for ch in range(16)]

    def shdn_jobs(gate=None):
        jobs = []
        for ld in range(16):
            hc, half = ld // 2, ld % 2
            jobs.append(dict(
                src=shdown_h[hc * 128:(hc + 1) * 128, half * 1024:(half + 1) * 1024],
                width=1024, dst=SHD_DST, gate=gate,
                scales=[(0, 1024, 40 + hc)],
                groups=[(gg * 512, F_SHD + (half * 8 + gg * 4) * 1024 + hc * 128, 1024)
                        for gg in range(2)]))
        return jobs

    def up_jobs(e, dstbase, gate=None):
        return [dict(src=upw_h[e, ch * 128:(ch + 1) * 128, :], width=1024,
                     dst='arena', gate=gate,
                     groups=[(gg * 512, dstbase + gg * 4 * 512 + ch * 128, 512)
                             for gg in range(2)])
                for ch in range(4)]

    def down_jobs(e, dstbase, dst, gate=None):
        jobs = []
        for ld in range(4):
            h0 = ld * 2
            jobs.append(dict(
                src=downw_h[e, h0 * 128:(h0 + 2) * 128, :].rearrange(
                    "(c p) i -> p c i", p=128),
                width=1024, dst=dst, gate=gate,
                scales=[(0, 512, 40 + h0), (512, 1024, 40 + h0 + 1)],
                groups=[(c * 512, dstbase + (h0 + c) * 128, 1024)
                        for c in range(2)]))
        return jobs

    if NPAIR * T >= 16384:
        DNA_DST, F_DNA = 'aun', 0
        SHD_DST, F_SHD = 'aun', 0
    else:
        DNA_DST, F_DNA = 'arena', 45056
        SHD_DST, F_SHD = 'arena', 45056 + 16384

    jobs12 = []
    for e in range(4):
        jobs12 += up_jobs(e, W_E_UP[e])
        jobs12 += down_jobs(e, F_DNA + e * 4096, DNA_DST)
    for e in (4, 5):
        jobs12 += up_jobs(e, W_E_UP[e])
        jobs12 += down_jobs(e, W_E_DN[e], 'arena')
    prep(jobs12, first_sync_wait=(v_s.h, x1_done))
    prep12_done = s_s.n

    @blk.scalar
    def _(sc):
        sc.wait_ge(v_s.h, cw_done)
        s_s.inc(sc.activation(cwsq[:], cwsq[:], AF.Sqrt))
    sqcw_done = s_s.n


    prev_dn = [None]

    def ffn_pass(items, wait_first, final):
        outev = []
        NI = len(items)
        s_pass = s_s.n
        all_up, all_tr = {}, {}
        v_bs = {}
        for tc in range(TC):
            s_b = s_pass + tc * NI
            v_b = v_s.n
            v_bs[tc] = v_b
            up_cnt, dn_cnt, tr_cnt = {}, {}, {}
            pdn = prev_dn[0]

            @blk.tensor
            def _(t, tc=tc, s_b=s_b, v_b=v_b):
                if tc == 0:
                    for w in wait_first:
                        t.wait_ge(*w)

                def emit_up(ii):
                    item = items[ii]
                    ubank = 2 + (ii % 2)
                    if ii >= 2:
                        t.wait_ge(s_s.h, s_b + (ii - 2) + 1)
                    elif tc >= 1:
                        t.wait_ge(s_s.h, s_b)
                    last = None
                    for dc in range(DC):
                        if item[0] == 'sh':
                            rhs = arena[:, W_SHUP + dc * SI + item[1] * 512:
                                        W_SHUP + dc * SI + (item[1] + 1) * 512]
                        else:
                            rhs = arena[:, item[2] + dc * 512:
                                        item[2] + (dc + 1) * 512]
                        last = t.matmul(PB(ubank),
                                        hT[:, dc * T + tc * 128: dc * T + (tc + 1) * 128],
                                        rhs, start=(dc == 0), stop=(dc == DC - 1))
                    up_cnt[ii] = t_s.inc(last)

                emit_up(0)
                for ii, item in enumerate(items):
                    tbank = 4 + (ii % 2)
                    if ii + 1 < NI:
                        emit_up(ii + 1)
                    t.wait_ge(v_s.h, v_b + 2 * ii + 1)
                    if ii >= 2:
                        t.wait_ge(v_s.h, v_b + 2 * (ii - 2) + 2)
                    elif tc >= 1:
                        t.wait_ge(v_s.h, v_b)
                    last = None
                    for ic in range(4):
                        last = t.transpose(
                            PB(tbank).bitcast(BF16)[:, ic * 128:(ic + 1) * 128],
                            ffr[:, (2 + ii % 2) * 512 + ic * 128:
                                (2 + ii % 2) * 512 + (ic + 1) * 128], id_b[:])
                    tr_cnt[ii] = t_s.inc(last)
                    t.wait_ge(v_s.h, v_b + 2 * ii + 2)
                    if ii == 0 and tc >= 1:
                        t.wait_ge(v_s.h, outev[tc - 1])
                    last = None
                    for hc in range(2):
                        for ic in range(4):
                            if item[0] == 'sh':
                                shsrc = aun if SHD_DST == 'aun' else arena
                                rhs = shsrc[:, F_SHD + (item[1] * 4 + ic) * 1024 + hc * 512:
                                            F_SHD + (item[1] * 4 + ic) * 1024 + (hc + 1) * 512]
                            else:
                                wsrc = arena if item[4] == 'arena' else aun
                                rhs = wsrc[:, item[3] + ic * 1024 + hc * 512:
                                           item[3] + ic * 1024 + (hc + 1) * 512]
                            last = t.matmul(PB(hc),
                                            ffr[:, (4 + ii % 2) * 512 + ic * 128:
                                                (4 + ii % 2) * 512 + (ic + 1) * 128],
                                            rhs,
                                            start=(ii == 0 and ic == 0),
                                            stop=(ii == NI - 1 and ic == 3))
                    dn_cnt[ii] = t_s.inc(last)

            all_up[tc] = dict(up_cnt)
            all_tr[tc] = dict(tr_cnt)

            @blk.vector
            def _(v, tc=tc, s_b=s_b, pdn=pdn):
                if pdn is not None:
                    v.wait_ge(t_s.h, pdn)
                for ii, item in enumerate(items):
                    v.wait_ge(s_s.h, s_b + ii + 1)
                    rw = ffr[:, (ii % 2) * 512:(ii % 2) * 512 + 512]
                    v_s.inc(v.tensor_mul(
                        ffr[:, (2 + ii % 2) * 512:(2 + ii % 2) * 512 + 512], rw, rw))
                    v.wait_ge(t_s.h, tr_cnt[ii])
                    v_s.inc(v.tensor_copy(
                        ffr[:, (4 + ii % 2) * 512:(4 + ii % 2) * 512 + 512],
                        PB(4 + ii % 2).bitcast(BF16)[:, 0:512]))
            assert v_s.n == v_b + 2 * NI

            @blk.vector
            def _(v, tc=tc):
                v.wait_ge(t_s.h, dn_cnt[NI - 1])
                if final == 'x':
                    for hc in range(2):
                        sl = x_res[:, tc * 1024 + hc * 512: tc * 1024 + (hc + 1) * 512]
                        v_s.inc(v.tensor_add(sl, sl, PB(hc)))
                else:
                    if tc >= 2:
                        v.wait_ge(d_out.h, 16 * (tc - 1))
                    os = arf[:, F_OUTS // 2 + (tc % 2) * 1024:
                             F_OUTS // 2 + (tc % 2) * 1024 + 1024]
                    for hc in range(2):
                        v_s.inc(v.tensor_tensor(
                            os[:, hc * 512:(hc + 1) * 512],
                            x_res[:, tc * 1024 + hc * 512: tc * 1024 + (hc + 1) * 512],
                            PB(hc), ALU.add))
                outev.append(v_s.n)
            prev_dn[0] = dn_cnt[NI - 1]

            if final == 'out':
                @blk.sync
                def _(sy, tc=tc):
                    sy.wait_ge(v_s.h, outev[tc])
                    os = arf[:, F_OUTS // 2 + (tc % 2) * 1024:
                             F_OUTS // 2 + (tc % 2) * 1024 + 1024]
                    d_out.inc(sy.dma_start(out=out_h[tc * 128:(tc + 1) * 128, :],
                                           in_=os), 16)

        @blk.scalar
        def _(sc):
            for tc in range(TC):
                for ii, item in enumerate(items):
                    sc.wait_ge(t_s.h, all_up[tc][ii])
                    if ii < 2 and tc >= 1:
                        sc.wait_ge(v_s.h, v_bs[tc])
                    if item[0] == 'e':
                        scale = cwsq[:, tc * 8 + item[1]: tc * 8 + item[1] + 1]
                    else:
                        scale = 1.0
                    s_s.inc(sc.activation(ffr[:, (ii % 2) * 512:(ii % 2) * 512 + 512],
                                          PB(2 + ii % 2), AF.Relu, scale=scale))
        assert s_s.n == s_pass + TC * NI
        return outev[-1]

    itemsR1 = [('e', e, W_E_UP[e], F_DNA + e * 4096, DNA_DST) for e in range(4)]
    sR1 = ffn_pass(itemsR1, [(s_s.h, prep12_done), (s_s.h, sqcw_done)], 'x')
    r1_dn_end = prev_dn[0]

    if "xr1" in dumps:
        hh_ = mkdump("xr1", [128, TC * 1024], BF16)
        @blk.sync
        def _(sy):
            sy.wait_ge(v_s.h, sR1)
            d_sy.inc(sy.dma_start(out=hh_, in_=x_res[:]), 16)

    jobs3 = []
    for e in (6, 7):
        jobs3 += up_jobs(e, W_E_UP[e], gate=(t_s.h, r1_dn_end))
        jobs3 += down_jobs(e, W_E_DN[e], 'arena', gate=(t_s.h, r1_dn_end))
    prep(jobs3)
    prep3_done = s_s.n

    itemsR2a = [('e', e, W_E_UP[e], W_E_DN[e], 'arena') for e in (4, 5)]
    sR2a = ffn_pass(itemsR2a, [(s_s.h, prep12_done), (v_s.h, sR1)], 'x')
    r2a_dn_end = prev_dn[0]

    jobs4 = (shup_jobs(gate=(t_s.h, r2a_dn_end))
             + shdn_jobs(gate=(t_s.h, r2a_dn_end)))
    prep(jobs4)
    prep4_done = s_s.n

    itemsR2b = [('e', e, W_E_UP[e], W_E_DN[e], 'arena') for e in (6, 7)]
    sR2b = ffn_pass(itemsR2b, [(s_s.h, prep3_done), (v_s.h, sR2a)], 'x')

    if "x2" in dumps:
        hh_ = mkdump("x2", [128, TC * 1024], BF16)
        @blk.sync
        def _(sy):
            sy.wait_ge(v_s.h, sR2b)
            d_sy.inc(sy.dma_start(out=hh_, in_=x_res[:]), 16)

    itemsS = [('sh', sic) for sic in range(4)]
    sS = ffn_pass(itemsS, [(s_s.h, prep4_done), (v_s.h, sR2b)], 'out')

    @blk.sync
    def _(sy):
        sy.wait_ge(d_out.h, 16 * TC)

    st.close()
    return nc, dmp


from concourse.bass_utils import run_bass_kernel_spmd


def kernel(L0_attn_x, L0_attn_w_qkv, L0_attn_w_o, L0_ffn_gate_weight,
           L0_ffn_gate_bias, L0_ffn_up_weights, L0_ffn_down_weights,
           L0_ffn_shared_up_weight, L0_ffn_shared_down_weight, L0_c, L0_wmod):
    B, T, Dm = L0_attn_x.shape
    assert (B, T, Dm) == (8, 2048, 1024)
    nc, _ = build(T=T)
    shared = dict(
        w_qkv=np.ascontiguousarray(L0_attn_w_qkv),
        w_o=np.ascontiguousarray(L0_attn_w_o),
        gate_w=np.ascontiguousarray(L0_ffn_gate_weight),
        gate_b=np.ascontiguousarray(L0_ffn_gate_bias),
        up_w=np.ascontiguousarray(L0_ffn_up_weights),
        down_w=np.ascontiguousarray(L0_ffn_down_weights),
        shup_w=np.ascontiguousarray(L0_ffn_shared_up_weight),
        shdown_w=np.ascontiguousarray(L0_ffn_shared_down_weight),
        wmod=np.ascontiguousarray(L0_wmod),
    )
    in_maps = []
    for i in range(8):
        m = dict(shared)
        m["x"] = np.ascontiguousarray(L0_attn_x[i])
        m["c"] = np.ascontiguousarray(L0_c[i])
        in_maps.append(m)
    res = run_bass_kernel_spmd(nc, in_maps, core_ids=list(range(8)))
    return np.stack([res.results[i]["out"] for i in range(8)], axis=0)
